# version 07
# baseline (speedup 1.0000x reference)
import contextlib
import numpy as np
import ml_dtypes
import concourse.bass as bass
import concourse.mybir as mybir
from concourse.bass_utils import run_bass_kernel_spmd
from concourse.alu_op_type import AluOpType

F32 = mybir.dt.float32
BF16 = mybir.dt.bfloat16
AF = mybir.ActivationFunctionType
AX = mybir.AxisListType.X
BF = ml_dtypes.bfloat16

NCORE = 8
T = 8192
D = 2048
NBLK = 16
BLK = T // NBLK
TLOC = 2 * BLK
NTILE = TLOC // 128
NCH = D // 128
NH = 16
RS = float(1.0 / np.sqrt(128.0))


def _core_blocks(i):
    return (i, NBLK - 1 - i)


def _global_rows(i):
    bA, bB = _core_blocks(i)
    return np.r_[bA * BLK:(bA + 1) * BLK, bB * BLK:(bB + 1) * BLK]


def _rope_tables(pos, scale):
    inv = (1.0 / (1000000.0 ** (np.arange(0, 128, 2, dtype=np.float32) / np.float32(128)))).astype(np.float32)
    f = pos.astype(np.float32)[:, None] * inv[None, :]
    c = np.cos(f).astype(np.float32) * np.float32(scale)
    s = np.sin(f).astype(np.float32) * np.float32(scale)
    cosT = np.ascontiguousarray(np.concatenate([c, c], axis=1).T)
    sinS = np.ascontiguousarray(np.concatenate([-s, s], axis=1).T)
    return cosT, sinS


class SemPool:
    def __init__(self, ectx):
        self.ectx = ectx
        self.cur = []

    def enter_context(self, cm):
        h = self.ectx.enter_context(cm)
        self.cur.append(h)
        return h


class Ctx:
    def __init__(self, nc, ectx, prefix="", semstack=None):
        self.nc = nc
        self.ectx = ectx
        self.prefix = prefix
        self.semstack = semstack if semstack is not None else ectx
        self._ns = 0

    def sb(self, name, shape, dtype):
        return self.ectx.enter_context(self.nc.sbuf_tensor(self.prefix + name, shape, dtype))

    def ps(self, name, shape, dtype):
        return self.ectx.enter_context(self.nc.psum_tensor(self.prefix + name, shape, dtype))

    def sem(self, name=None):
        if name is None:
            self._ns += 1
            name = f"s{self._ns}"
        return self.semstack.enter_context(self.nc.semaphore(self.prefix + name))



def emit_l0_pre(nc, block, phase, pgate, ios, prefix, semstack):
    x_h = ios["x"]
    wqT_h = ios["wqT"]
    wkT_h = ios["wkT"]
    wvT_h = ios["wvT"]
    wqkvT_h = ios["wqkvT"]
    wgT_h = ios["wgT"]
    cosq_h = ios["cosq"]
    sinq_h = ios["sinq"]
    cosk_h = ios["cosk"]
    sink_h = ios["sink"]
    ident_h = ios["ident"]
    qT_o = ios["qT_o"]
    kT_o = ios["kT_o"]
    v_o = ios["v_o"]
    q2T_o = ios["q2T_o"]
    kshT_o = ios["kshT_o"]
    vsh_o = ios["vsh_o"]
    qn2_o = ios["qn2_o"]
    q2n2_o = ios["q2n2_o"]
    k2m_o = ios["k2m_o"]
    k2shm_o = ios["k2shm_o"]
    knorm_d = ios["knorm_d"]
    with contextlib.ExitStack() as ectx:
        C = Ctx(nc, ectx, prefix, semstack)
        xt2 = C.sb("xt2", [128, 2 * D], F32)
        sq_s = C.sb("sq_s", [128, D], F32)
        sqbf = C.sb("sqbf", [128, TLOC], BF16)
        st8 = C.sb("st8", [128, 64], F32)
        sx8 = st8[:, 0:8]; sqs8 = st8[:, 8:16]; m8 = st8[:, 16:24]; e28 = st8[:, 24:32]
        msq8 = st8[:, 32:40]; var8 = st8[:, 40:48]; lnv8 = st8[:, 48:56]; rstd8 = st8[:, 56:64]
        eps1 = C.sb("eps1", [128, 1], F32)
        xnbf = C.sb("xnbf", [128, D], BF16)
        xnT = C.sb("xnT", [128, NCH * TLOC], BF16)
        ident = C.sb("ident_sb", [128, 128], BF16)
        ones_bf = C.sb("ones_bf", [128, 1], BF16)
        cosq = C.sb("cosq_sb", [128, TLOC], F32)
        sinq = C.sb("sinq_sb", [128, TLOC], F32)
        cosk = C.sb("cosk_sb", [128, TLOC], F32)
        sink = C.sb("sink_sb", [128, TLOC], F32)
        ropetmp = C.sb("ropetmp", [128, TLOC], F32)
        stage = C.sb("stage", [128, 2 * TLOC], BF16)
        vsb = C.sb("vsb", [128, NTILE * NH * 129], BF16)
        vsbv = vsb[:].rearrange("p (t h c) -> p t h c", h=NH, c=129)
        nrow = C.sb("nrow", [1, 2 * TLOC], F32)
        k2rows = C.sb("k2rows", [NH, TLOC], F32)
        k2m = C.sb("k2m", [NH, 1], F32)
        kshrow = C.sb("kshrow", [1, TLOC], F32)
        k2shm = C.sb("k2shm", [1, 1], F32)
        wslab = C.sb("wslab", [128, 2 * NCH * 512], BF16)
        wsv = wslab[:].rearrange("p (b c f) -> p b c f", b=2, f=512)
        wsv128 = wslab[:].rearrange("p (b c f) -> p b c f", b=8, f=128)
        wgslab = C.sb("wgslab", [128, NCH * 16], BF16)
        wgv = wgslab[:].rearrange("p (c f) -> p c f", f=16)
        logit8 = C.sb("logit8", [128, 128], F32)
        gx8 = C.sb("gx8", [128, 24], F32)
        mx8 = gx8[:, 0:8]; ngm8 = gx8[:, 8:16]; top8 = gx8[:, 16:24]
        exps = C.sb("exps", [128, 128], F32)
        ind8 = C.sb("ind8", [128, 128], F32)
        svec8 = C.sb("svec8", [128, 128], F32)
        kfsb = C.sb("kfsb", [128, 512], F32)
        seltmp = C.sb("seltmp", [128, 512], F32)
        ksh8 = C.sb("ksh8", [128, 8 * 128], F32)
        vsh8 = C.sb("vsh8", [128, 8 * 128], F32)
        kshbf = C.sb("kshbf", [128, 128], BF16)
        kshT8 = C.sb("kshT8", [128, TLOC], BF16)
        vshst = C.sb("vshst", [128, 132], BF16)

        pj = C.ps("pj", [128, 2048], F32)
        ptb = C.ps("ptb", [128, 2048], BF16)
        nrm = C.ps("nrm", [128, 1024], F32)
        plg = nrm

        tld = C.sem("tld")
        xldA = [C.sem("xldA0"), C.sem("xldA1")]
        xldB = [C.sem("xldB0"), C.sem("xldB1")]
        pav = C.sem("pav")
        pas = C.sem("pas")
        stm = C.sem("stm")
        n_st = [0]

        def St(r):
            r.then_inc(stm, 16)
            n_st[0] += 16


        @block.sync
        def _(sync):
            for t in range(NTILE):
                if t >= 2:
                    sync.wait_ge(pav, t - 1)
                    sync.wait_ge(pas, t - 1)
                sync.dma_start(out=xt2[:, (t % 2) * D:(t % 2) * D + D],
                               in_=x_h.rearrange("(t p) f -> p t f", p=128)[:, t, :]).then_inc(xldA[t % 2], 16)
            sync.dma_start(out=cosq[:], in_=cosq_h).then_inc(tld, 16)
            sync.wait_ge(tld, 16)
            sync.dma_start(out=sinq[:], in_=sinq_h).then_inc(tld, 16)
            sync.wait_ge(tld, 32)
            sync.dma_start(out=cosk[:], in_=cosk_h).then_inc(tld, 16)
            sync.wait_ge(tld, 48)
            sync.dma_start(out=sink[:], in_=sink_h).then_inc(tld, 16)
            sync.wait_ge(tld, 64)
            sync.dma_start(out=ident[:], in_=ident_h).then_inc(tld, 16)

        ms = C.sem("ms")

        @block.gpsimd
        def _(gpsimd):
            gpsimd.memset(eps1[:], 1e-5)
            gpsimd.memset(ones_bf[:], 1.0)
            gpsimd.memset(vsb[:].rearrange("p (a c) -> p a c", c=129)[:, :, 128:129], 1.0)
            gpsimd.memset(vshst[:, 128:129], 1.0)
            gpsimd.memset(vshst[:, 129:132], 0.0)
            gpsimd.drain().then_inc(ms, 1)

        @block.vector
        def _(vector):
            for t in range(NTILE):
                vector.wait_ge(xldA[t % 2], (t // 2 + 1) * 16)
                vector.reduce_sum(sx8[:, t:t + 1], xt2[:, (t % 2) * D:(t % 2) * D + D], axis=AX)
                vector.drain().then_inc(pav, 1)

        @block.scalar
        def _(scalar):
            for t in range(NTILE):
                scalar.wait_ge(xldA[t % 2], (t // 2 + 1) * 16)
                scalar.activation(sq_s[:], xt2[:, (t % 2) * D:(t % 2) * D + D], AF.Square,
                                  accum_out=sqs8[:, t:t + 1])
                scalar.drain().then_inc(pas, 1)

        xn_s = C.sem("xn_s")

        @block.sync
        def _(sync):
            sync.wait_ge(pav, NTILE)
            sync.wait_ge(pas, NTILE)
            for t in range(NTILE):
                if t >= 2:
                    sync.wait_ge(xn_s, t - 1)
                sync.dma_start(out=xt2[:, (t % 2) * D:(t % 2) * D + D],
                               in_=x_h.rearrange("(t p) f -> p t f", p=128)[:, t, :]).then_inc(xldB[t % 2], 16)

        pb = C.sem("pb")

        @block.vector
        def _(vector):
            vector.wait_ge(pav, NTILE)
            vector.wait_ge(pas, NTILE)
            c = 1.0 / D
            vector.tensor_scalar_mul(m8, sx8, c)
            vector.tensor_scalar_mul(e28, sqs8, c)
            vector.drain()
            vector.tensor_tensor(msq8, m8, m8, op=AluOpType.mult)
            vector.drain()
            vector.tensor_tensor(var8, e28, msq8, op=AluOpType.subtract)
            vector.drain().then_inc(pb, 1)

        @block.scalar
        def _(scalar):
            scalar.wait_ge(pb, 1)
            scalar.wait_ge(ms, 1)
            scalar.activation(lnv8, var8, AF.Ln, bias=eps1[:])
            scalar.drain()
            scalar.activation(rstd8, lnv8, AF.Exp, scale=-0.5)
            scalar.drain().then_inc(pb, 1)
        PB_DONE = 2

        tr_s = C.sem("tr_s")
        xe_s = C.sem("xe_s")

        @block.vector
        def _(vector):
            vector.wait_ge(pb, PB_DONE)
            for t in range(NTILE):
                vector.wait_ge(xldB[t % 2], (t // 2 + 1) * 16)
                vector.tensor_scalar(xnbf[:], xt2[:, (t % 2) * D:(t % 2) * D + D],
                                     m8[:, t:t + 1], rstd8[:, t:t + 1],
                                     op0=AluOpType.subtract, op1=AluOpType.mult).then_inc(xn_s, 1)
                for g in range(2):
                    G = t * 2 + g
                    vector.wait_ge(tr_s, G + 1)
                    vector.tensor_copy(
                        xnT[:].rearrange("p (c tt) -> p c tt", tt=TLOC)[:, g * 8:(g + 1) * 8, t * 128:(t + 1) * 128],
                        ptb[:, (G % 2) * 1024:(G % 2) * 1024 + 1024].rearrange("p (c tt) -> p c tt", tt=128)).then_inc(xe_s, 1)

        @block.tensor
        def _(tensor):
            tensor.wait_ge(tld, 80)
            for t in range(NTILE):
                tensor.wait_ge(xn_s, t + 1)
                for g in range(2):
                    G = t * 2 + g
                    if G >= 2:
                        tensor.wait_ge(xe_s, G - 1)
                    for j in range(8):
                        e = g * 8 + j
                        r = tensor.transpose(ptb[:, (G % 2) * 1024 + j * 128:(G % 2) * 1024 + (j + 1) * 128],
                                             xnbf[:, e * 128:(e + 1) * 128], ident[:])
                    r.then_inc(tr_s, 1)
        XE_ALL = 16

        vw_s = [C.sem("vw_s0"), C.sem("vw_s1")]
        vm_s = C.sem("vm_s")
        ve_s = C.sem("ve_s")

        @block.gpsimd
        def _(gpsimd):
            gpsimd.wait_ge(ms, 1)
            for fs in range(4):
                if fs >= 2:
                    gpsimd.wait_ge(vm_s, (fs - 1) * 8)
                gpsimd.dma_start(out=wsv[:, fs % 2],
                                 in_=wvT_h[:, fs * 512:(fs + 1) * 512].rearrange("(c p) f -> p c f", p=128)
                                 ).then_inc(vw_s[fs % 2], 16)

        @block.tensor
        def _(tensor):
            tensor.wait_ge(xe_s, XE_ALL)
            for fs in range(4):
                tensor.wait_ge(vw_s[fs % 2], (fs // 2 + 1) * 16)
                for t in range(NTILE):
                    it = fs * 8 + t
                    if it >= 4:
                        tensor.wait_ge(ve_s, it - 3)
                    bank = pj[:, (it % 4) * 512:(it % 4) * 512 + 512]
                    for e in range(NCH):
                        r = tensor.matmul(bank,
                                          xnT[:, e * TLOC + t * 128: e * TLOC + (t + 1) * 128],
                                          wsv[:, fs % 2, e],
                                          start=(e == 0), stop=(e == NCH - 1))
                    r.then_inc(vm_s, 1)

        @block.scalar
        def _(scalar):
            for fs in range(4):
                for t in range(NTILE):
                    it = fs * 8 + t
                    scalar.wait_ge(vm_s, it + 1)
                    bank = pj[:, (it % 4) * 512:(it % 4) * 512 + 512]
                    for hh in range(4):
                        r = scalar.copy(vsbv[:, t, 4 * fs + hh, 0:128], bank[:, hh * 128:(hh + 1) * 128])
                    r.then_inc(ve_s, 1)
        VE_ALL = 32

        @block.gpsimd
        def _(gpsimd):
            gpsimd.wait_ge(ve_s, VE_ALL)
            St(gpsimd.dma_start(out=v_o.rearrange("(t p) c -> p t c", p=128),
                                in_=vsb[:].rearrange("p (t c) -> p t c", c=NH * 129)))

        def fstream(tag, w_ap, out_ap, rope, cos_sb, sin_sb, scale, nout_ap, first_wait, prev_so, prev_no=None, dve_tld=False):
            lw = [C.sem(f"{tag}lw0"), C.sem(f"{tag}lw1")]
            pm = C.sem(f"{tag}pm")
            se = C.sem(f"{tag}se")
            qs = C.sem(f"{tag}qs")
            nm = C.sem(f"{tag}nm")
            nr = C.sem(f"{tag}nr")
            so = [C.sem(f"{tag}so0"), C.sem(f"{tag}so1")]
            no = [C.sem(f"{tag}no0"), C.sem(f"{tag}no1")]

            @block.gpsimd
            def _(gpsimd):
                gpsimd.wait_ge(ms, 1)
                for h in range(NH + 1):
                    if h < NH:
                        if h >= 2:
                            gpsimd.wait_ge(pm, h - 1)
                        gpsimd.dma_start(out=wsv128[:, h % 2],
                                         in_=w_ap[:, h * 128:(h + 1) * 128].rearrange("(c p) f -> p c f", p=128)
                                         ).then_inc(lw[h % 2], 16)
                    if h >= 1:
                        j = h - 1
                        gpsimd.wait_ge(qs, j + 1)
                        gpsimd.dma_start(out=out_ap[j * 128:(j + 1) * 128, :],
                                         in_=stage[:, (j % 2) * TLOC:(j % 2) * TLOC + TLOC]).then_inc(so[j % 2], 16)
                        gpsimd.wait_ge(nr, j + 1)
                        gpsimd.dma_start(out=nout_ap[j:j + 1, :],
                                         in_=nrow[0:1, (j % 2) * TLOC:(j % 2) * TLOC + TLOC]).then_inc(no[j % 2], 16)

            @block.tensor
            def _(tensor):
                tensor.wait_ge(*first_wait)
                for h in range(NH):
                    tensor.wait_ge(lw[h % 2], (h // 2 + 1) * 16)
                    if h >= 2:
                        tensor.wait_ge(se, h - 1)
                    for half in range(2):
                        bank = pj[:, (h % 2) * 1024 + half * 512:(h % 2) * 1024 + half * 512 + 512]
                        for e in range(NCH):
                            r = tensor.matmul(bank,
                                              wsv128[:, h % 2, e],
                                              xnT[:, e * TLOC + half * 512: e * TLOC + (half + 1) * 512],
                                              start=(e == 0), stop=(e == NCH - 1))
                    r.then_inc(pm, 1)
                    if h >= 1:
                        tensor.wait_ge(qs, h)
                        if h >= 2:
                            tensor.wait_ge(nr, h - 1)
                        for half in range(2):
                            r = tensor.matmul(nrm[0:1, half * 512:half * 512 + 512],
                                              ones_bf[:], sqbf[:, half * 512:(half + 1) * 512],
                                              start=True, stop=True)
                        r.then_inc(nm, 1)
                tensor.wait_ge(qs, NH)
                tensor.wait_ge(nr, NH - 1)
                for half in range(2):
                    r = tensor.matmul(nrm[0:1, half * 512:half * 512 + 512],
                                      ones_bf[:], sqbf[:, half * 512:(half + 1) * 512],
                                      start=True, stop=True)
                r.then_inc(nm, 1)

            @block.vector
            def _(vector):
                if dve_tld:
                    vector.wait_ge(tld, 80)
                for h in range(NH):
                    vector.wait_ge(pm, h + 1)
                    if h < 2 and prev_so is not None:
                        vector.wait_ge(prev_so[0], prev_so[2])
                        vector.wait_ge(prev_so[1], prev_so[2])
                    if h >= 2:
                        vector.wait_ge(so[h % 2], (h // 2) * 16)
                    stg = stage[:, (h % 2) * TLOC:(h % 2) * TLOC + TLOC]
                    pjh = pj[:, (h % 2) * 1024:(h % 2) * 1024 + 1024]
                    if rope:
                        vector.tensor_tensor(ropetmp[0:64, :], pjh[64:128, :], sin_sb[0:64, :], op=AluOpType.mult)
                        vector.tensor_tensor(ropetmp[64:128, :], pjh[0:64, :], sin_sb[64:128, :], op=AluOpType.mult)
                        vector.tensor_tensor(stg, pjh, cos_sb[:], op=AluOpType.mult)
                        vector.drain()
                        vector.tensor_tensor(stg, stg, ropetmp[:], op=AluOpType.add).then_inc(se, 1)
                    else:
                        vector.tensor_scalar_mul(stg, pjh, scale).then_inc(se, 1)
                    vector.wait_ge(nm, h + 1)
                    if h < 2 and prev_no is not None:
                        vector.wait_ge(prev_no[0], prev_no[2])
                        vector.wait_ge(prev_no[1], prev_no[2])
                    if h >= 2:
                        vector.wait_ge(no[h % 2], (h // 2) * 16)
                    vector.tensor_copy(nrow[0:1, (h % 2) * TLOC:(h % 2) * TLOC + TLOC], nrm[0:1, :]).then_inc(nr, 1)

            @block.scalar
            def _(scalar):
                for h in range(NH):
                    scalar.wait_ge(se, h + 1)
                    stg = stage[:, (h % 2) * TLOC:(h % 2) * TLOC + TLOC]
                    scalar.activation(sqbf[:], stg, AF.Square).then_inc(qs, 1)

            return nr, so, no

        qnr, qso, qno = fstream("q", wqT_h, qT_o, True, cosq, sinq, RS, qn2_o,
                                (ve_s, VE_ALL), None, None, dve_tld=True)
        knr, kso, kno = fstream("k", wkT_h, kT_o, True, cosk, sink, 1.0, knorm_d,
                                (qnr, NH), (qso[0], qso[1], 8 * 16), (qno[0], qno[1], 8 * 16), dve_tld=True)
        q2nr, q2so, q2no = fstream("q2", wqkvT_h[:, 0:D], q2T_o, False, None, None, RS, q2n2_o,
                                   (knr, NH), (kso[0], kso[1], 8 * 16), (kno[0], kno[1], 8 * 16))

        nst_s = C.sem("nst_s")

        @block.sync
        def _(sync):
            sync.wait_ge(kno[0], 8 * 16)
            sync.wait_ge(kno[1], 8 * 16)
            sync.dma_start(out=k2rows[:], in_=knorm_d).then_inc(nst_s, 16)

        @block.vector
        def _(vector):
            vector.wait_ge(nst_s, 16)
            vector.reduce_max(k2m[:], k2rows[:], axis=AX)
            vector.drain().then_inc(nst_s, 1)

        @block.gpsimd
        def _(gpsimd):
            gpsimd.wait_ge(nst_s, 17)
            St(gpsimd.dma_start(out=k2m_o, in_=k2m[:]))

        gw_s = C.sem("gw_s")
        gl_s = C.sem("gl_s")

        @block.gpsimd
        def _(gpsimd):
            gpsimd.wait_ge(ms, 1)
            gpsimd.dma_start(out=wgv, in_=wgT_h.rearrange("(c p) f -> p c f", p=128)).then_inc(gw_s, 16)

        @block.tensor
        def _(tensor):
            tensor.wait_ge(gw_s, 16)
            tensor.wait_ge(q2nr, NH)
            for t in range(NTILE):
                for e in range(NCH):
                    r = tensor.matmul(plg[:, t * 16:(t + 1) * 16],
                                      xnT[:, e * TLOC + t * 128: e * TLOC + (t + 1) * 128],
                                      wgv[:, e],
                                      start=(e == 0), stop=(e == NCH - 1))
            r.then_inc(gl_s, 1)

        @block.vector
        def _(vector):
            vector.wait_ge(gl_s, 1)
            vector.tensor_copy(logit8[:], plg[:, 0:128])
            vector.drain()
            for t in range(NTILE):
                vector.reduce_max(mx8[:, t:t + 1], logit8[:, t * 16:(t + 1) * 16], axis=AX)
            vector.drain()
            vector.tensor_scalar_mul(ngm8, mx8, -1.0)
            vector.drain().then_inc(gl_s, 1)

        @block.scalar
        def _(scalar):
            scalar.wait_ge(gl_s, 2)
            for t in range(NTILE):
                scalar.activation(exps[:, t * 16:(t + 1) * 16], logit8[:, t * 16:(t + 1) * 16], AF.Exp,
                                  bias=ngm8[:, t:t + 1], accum_out=top8[:, t:t + 1])
            scalar.drain().then_inc(gl_s, 1)

        @block.vector
        def _(vector):
            vector.wait_ge(gl_s, 3)
            vector.reciprocal(top8, top8)
            vector.drain()
            for t in range(NTILE):
                vector.tensor_scalar(ind8[:, t * 16:(t + 1) * 16], logit8[:, t * 16:(t + 1) * 16],
                                     mx8[:, t:t + 1], None, op0=AluOpType.is_equal)
                vector.tensor_scalar(svec8[:, t * 16:(t + 1) * 16], logit8[:, t * 16:(t + 1) * 16],
                                     mx8[:, t:t + 1], top8[:, t:t + 1],
                                     op0=AluOpType.is_equal, op1=AluOpType.mult)
            vector.drain().then_inc(gl_s, 1)
        GL_DONE = 4

        sw_s = [C.sem("sw_s0"), C.sem("sw_s1")]
        sm_s = C.sem("sm_s")
        sev_s = C.sem("sev_s")
        sel_s = C.sem("sel_s")

        @block.gpsimd
        def _(gpsimd):
            for fs in range(8):
                if fs >= 2:
                    gpsimd.wait_ge(sm_s, (fs - 1) * 8)
                gpsimd.dma_start(out=wsv[:, fs % 2],
                                 in_=wqkvT_h[:, D + fs * 512: D + (fs + 1) * 512].rearrange("(c p) f -> p c f", p=128)
                                 ).then_inc(sw_s[fs % 2], 16)

        @block.tensor
        def _(tensor):
            for fs in range(8):
                tensor.wait_ge(sw_s[fs % 2], (fs // 2 + 1) * 16)
                for t in range(NTILE):
                    it = fs * 8 + t
                    if it >= 4:
                        tensor.wait_ge(sev_s, it - 3)
                    bank = pj[:, (it % 4) * 512:(it % 4) * 512 + 512]
                    for e in range(NCH):
                        r = tensor.matmul(bank,
                                          xnT[:, e * TLOC + t * 128: e * TLOC + (t + 1) * 128],
                                          wsv[:, fs % 2, e],
                                          start=(e == 0), stop=(e == NCH - 1))
                    r.then_inc(sm_s, 1)

        @block.vector
        def _(vector):
            vector.wait_ge(gl_s, GL_DONE)
            for fs in range(8):
                wvec = ind8 if fs < 4 else svec8
                acc = ksh8 if fs < 4 else vsh8
                for t in range(NTILE):
                    it = fs * 8 + t
                    vector.wait_ge(sm_s, it + 1)
                    bank = pj[:, (it % 4) * 512:(it % 4) * 512 + 512]
                    vector.tensor_copy(kfsb[:], bank)
                    vector.drain().then_inc(sev_s, 1)
                    hsel = wvec[:, t * 16 + 4 * (fs % 4): t * 16 + 4 * (fs % 4) + 4]
                    vector.tensor_tensor(seltmp[:].rearrange("p (h d) -> p h d", d=128),
                                         kfsb[:].rearrange("p (h d) -> p h d", d=128),
                                         hsel.rearrange("p (h o) -> p h o", o=1).broadcast_to((128, 4, 128)),
                                         op=AluOpType.mult)
                    vector.drain()
                    vector.tensor_tensor(seltmp[:, 0:256], seltmp[:, 0:256], seltmp[:, 256:512], op=AluOpType.add)
                    vector.drain()
                    if fs % 4 == 0:
                        vector.tensor_tensor(acc[:, t * 128:(t + 1) * 128], seltmp[:, 0:128], seltmp[:, 128:256], op=AluOpType.add)
                    else:
                        vector.tensor_tensor(seltmp[:, 0:128], seltmp[:, 0:128], seltmp[:, 128:256], op=AluOpType.add)
                        vector.drain()
                        vector.tensor_tensor(acc[:, t * 128:(t + 1) * 128], acc[:, t * 128:(t + 1) * 128], seltmp[:, 0:128], op=AluOpType.add)
                    vector.drain().then_inc(sel_s, 1)
        SEL_ALL = 64

        fin_s = C.sem("fin_s")
        ft_s = C.sem("ft_s")
        vsc_s = C.sem("vsc_s")
        vst_s = C.sem("vst_s")

        @block.vector
        def _(vector):
            vector.wait_ge(sel_s, SEL_ALL)
            for t in range(NTILE):
                vector.tensor_copy(kshbf[:], ksh8[:, t * 128:(t + 1) * 128])
                vector.drain().then_inc(fin_s, 1)
                vector.wait_ge(ft_s, t + 1)
                vector.tensor_copy(kshT8[:, t * 128:(t + 1) * 128], ptb[:, 0:128])
                vector.drain().then_inc(fin_s, 1)

        @block.tensor
        def _(tensor):
            for t in range(NTILE):
                tensor.wait_ge(fin_s, 2 * t + 1)
                tensor.transpose(ptb[:, 0:128], kshbf[:], ident[:]).then_inc(ft_s, 1)

        @block.scalar
        def _(scalar):
            scalar.wait_ge(sel_s, SEL_ALL)
            for t in range(NTILE):
                if t > 0:
                    scalar.wait_ge(vst_s, t * 16)
                scalar.copy(vshst[:, 0:128], vsh8[:, t * 128:(t + 1) * 128])
                scalar.drain().then_inc(vsc_s, 1)

        @block.gpsimd
        def _(gpsimd):
            for t in range(NTILE):
                gpsimd.wait_ge(vsc_s, t + 1)
                gpsimd.dma_start(out=vsh_o[t * 128:(t + 1) * 128, :], in_=vshst[:]).then_inc(vst_s, 16)

        ksq_s = C.sem("ksq_s")

        @block.scalar
        def _(scalar):
            scalar.wait_ge(fin_s, 2 * NTILE)
            scalar.activation(sqbf[:], kshT8[:], AF.Square)
            scalar.drain().then_inc(ksq_s, 1)

        @block.tensor
        def _(tensor):
            tensor.wait_ge(ksq_s, 1)
            for half in range(2):
                r = tensor.matmul(nrm[0:1, half * 512:half * 512 + 512],
                                  ones_bf[:], sqbf[:, half * 512:(half + 1) * 512],
                                  start=True, stop=True)
            r.then_inc(ksq_s, 1)

        @block.vector
        def _(vector):
            vector.wait_ge(ksq_s, 2)
            vector.tensor_copy(kshrow[:], nrm[0:1, :])
            vector.drain()
            vector.reduce_max(k2shm[:], kshrow[:], axis=AX)
            vector.drain().then_inc(ksq_s, 1)

        @block.gpsimd
        def _(gpsimd):
            gpsimd.wait_ge(ksq_s, 3)
            St(gpsimd.dma_start(out=kshT_o, in_=kshT8[:]))
            St(gpsimd.dma_start(out=k2shm_o, in_=k2shm[:]))
            gpsimd.wait_ge(vst_s, NTILE * 16)
            for sgrp in (qso, kso, q2so, qno, q2no, kno):
                gpsimd.wait_ge(sgrp[0], 8 * 16)
                gpsimd.wait_ge(sgrp[1], 8 * 16)
            gpsimd.wait_ge(stm, n_st[0])
            gpsimd.sem_inc(phase, 1)



KSLOT = 17


def emit_l0_attn(nc, block, phase, pgate, ios, prefix, semstack):
    x_h = ios["x"]
    qT_h = ios["qT"]
    q2T_h = ios["q2T"]
    kTl_h = ios["kTl"]
    vl_h = ios["vl"]
    kshTl_h = ios["kshTl"]
    vshl_h = ios["vshl"]
    kTg_h = ios["kTg"]
    vg_h = ios["vg"]
    kshg_h = ios["kshg"]
    vshg_h = ios["vshg"]
    qn2_h = ios["qn2"]
    q2n2_h = ios["q2n2"]
    kng_h = ios["kng"]
    pm_h = ios["pm"]
    woT_h = ios["woT"]
    maskT_h = ios["maskT"]
    ident_h = ios["ident"]
    x1_o = ios["x1_o"]
    nb_d = ios["nb_d"]
    NBIAS = 2 * NH * 2 * KSLOT
    with contextlib.ExitStack() as ectx:
        C = Ctx(nc, ectx, prefix, semstack)
        att = C.sb("att", [128, NTILE * D], BF16)
        attv = att[:].rearrange("p (t f) -> p t f", f=D)
        y2g = C.sb("y2g", [128, NTILE * D], BF16)
        y2gv = y2g[:].rearrange("p (t f) -> p t f", f=D)
        ptbuf = C.sb("ptbuf", [128, 2 * 256], BF16)
        kTb = C.sb("kTb", [128, 2 * 2 * 512], BF16)
        kTbv = kTb[:].rearrange("p (b c f) -> p b c f", b=2, f=512)
        vbb = C.sb("vbb", [128, 2 * 4 * 258], BF16)
        vbv = vbb[:].rearrange("p (b t c) -> p b t c", b=2, c=258)
        bias_sb = C.sb("bias_sb", [128, NBIAS], F32)
        nb16 = C.sb("nb16", [NH, 4], F32)
        nbrow = C.sb("nbrow", [1, 64], F32)
        pmrow = C.sb("pmrow", [1, NBIAS], F32)
        brow = C.sb("brow", [1, NBIAS], F32)
        brow_bf = C.sb("brow_bf", [1, NBIAS], BF16)
        k2a = C.sb("k2a", [NH, NCORE], F32)
        k2sha = C.sb("k2sha", [1, NCORE], F32)
        K2 = C.sb("K2", [NH, 1], F32)
        K2shb = C.sb("K2shb", [NH, 1], F32)
        K2sh1 = C.sb("K2sh1", [1, 1], BF16)
        ones_row = C.sb("ones_row", [1, 128], BF16)
        maskT = C.sb("maskT_sb", [128, 128], BF16)
        ident = C.sb("ident_sb", [128, 128], BF16)
        rec1 = C.sb("rec1", [128, 2], F32)
        ytmp = C.sb("ytmp", [128, 2 * 128], F32)
        kshTl = C.sb("kshTl_sb", [128, TLOC], BF16)
        vshl = C.sb("vshl_sb", [128, NTILE * 132], BF16)
        vshlv = vshl[:].rearrange("p (t c) -> p t c", c=132)

        arena = C.sb("arena", [128, 100352], mybir.dt.uint8)
        abase = nc.lookup_mloc(arena).addr

        def at(name, shape, dtype, off):
            return nc.alloc_sbuf_tensor_at(name, shape, dtype, offset=abase + off)

        kTl = at("kTl_sb", [128, NCH * TLOC], BF16, 0)
        vl = at("vl_sb", [128, NTILE * NH * 129], BF16, 32768)
        vlv = vl[:].rearrange("p (t c) -> p t c", c=NH * 129)
        q2Tl = at("q2Tl", [128, NCH * TLOC], BF16, 0)
        kshTs = at("kshTs", [128, T], BF16, 32768)
        vshs = at("vshs", [128, 64 * 132], BF16, 49152)
        vshv = vshs[:].rearrange("p (t c) -> p t c", c=132)
        wof = at("wof", [128, NCH * D], BF16, 0)
        wofv = wof[:].rearrange("p (c f) -> p c f", f=D)
        qTl = at("qTl", [128, NCH * TLOC], BF16, 67584)
        attTt = at("attTt", [128, NCH * 128], BF16, 67584)
        xt2 = at("xt2", [128, 2 * D], F32, 67584 + 4096)
        x1st = at("x1st", [128, D], F32, 67584 + 4096 + 16384)
        etmp = at("etmp", [128, 512], F32, 67584 + 4096 + 16384 + 8192)
        ybase = nc.lookup_mloc(y2g).addr
        qn2s = nc.alloc_sbuf_tensor_at("qn2s", [NH, TLOC], F32, offset=ybase)
        q2n2s = nc.alloc_sbuf_tensor_at("q2n2s", [NH, TLOC], F32, offset=ybase + 4096)
        bnd = nc.alloc_sbuf_tensor_at("bnd", [NH, TLOC], F32, offset=ybase + 8192)

        psA = C.ps("psA", [128, 1024], F32)
        psO = C.ps("psO", [128, 2048], F32)
        ptb = C.ps("ptb", [128, 2048], BF16)

        ld0 = C.sem("ld0")
        ld1 = C.sem("ld1")

        @block.sync
        def _(sync):
            for i, (dst, srcap) in enumerate([
                (qTl[:].rearrange("p (c t) -> p c t", t=TLOC), qT_h.rearrange("(c p) t -> p c t", p=128)),
                (kTl[:].rearrange("p (c t) -> p c t", t=TLOC), kTl_h.rearrange("(c p) t -> p c t", p=128)),
                (vlv, vl_h.rearrange("(t p) c -> p t c", p=128)),
                (kshTl[:], kshTl_h),
                (vshlv, vshl_h.rearrange("(t p) c -> p t c", p=128)),
                (qn2s[:], qn2_h),
                (q2n2s[:], q2n2_h),
                (k2a[:].rearrange("h (r o) -> h r o", o=1), kng_h.rearrange("(r h) c -> h r c", h=17)[0:16]),
                (k2sha[:].rearrange("h (r o) -> h r o", o=1), kng_h.rearrange("(r h) c -> h r c", h=17)[16:17]),
                (maskT[:], maskT_h),
                (ident[:], ident_h),
                (pmrow[:], pm_h),
            ]):
                sync.wait_ge(ld0, i * 16)
                sync.dma_start(out=dst, in_=srcap, allow_slow_non_contiguous=True).then_inc(ld0, 16)
        LD0 = 12 * 16

        ms = C.sem("ms")

        @block.gpsimd
        def _(gpsimd):
            gpsimd.memset(ones_row[:], 1.0)
            gpsimd.drain().then_inc(ms, 1)

        bs = C.sem("bs")

        @block.vector
        def _(vector):
            vector.wait_ge(ld0, LD0)
            vector.reduce_max(K2[:], k2a[:], axis=AX)
            vector.reduce_max(K2shb[0:1, :], k2sha[:], axis=AX)
            vector.drain()
            vector.tensor_scalar(bnd[:], qn2s[:], K2[:], None, op0=AluOpType.mult)
            vector.drain().then_inc(bs, 1)

        @block.scalar
        def _(scalar):
            scalar.wait_ge(bs, 1)
            scalar.activation(bnd[:], bnd[:], AF.Sqrt)
            scalar.drain().then_inc(bs, 1)

        @block.vector
        def _(vector):
            vector.wait_ge(bs, 2)
            for g in range(2):
                vector.reduce_max(nb16[:, g:g + 1], bnd[:, g * 512:(g + 1) * 512], axis=AX)
            vector.tensor_copy(K2sh1[:], K2shb[0:1, 0:1])
            vector.drain().then_inc(bs, 1)

        @block.tensor
        def _(tensor):
            tensor.wait_ge(bs, 3)
            tensor.wait_ge(ms, 1)
            tensor.matmul(psA[:, 0:1], ones_row[:], K2sh1[:], start=True, stop=True).then_inc(bs, 1)

        @block.vector
        def _(vector):
            vector.wait_ge(bs, 4)
            vector.tensor_copy(K2shb[:], psA[0:NH, 0:1])
            vector.drain()
            vector.tensor_scalar(bnd[:], q2n2s[:], K2shb[:], None, op0=AluOpType.mult)
            vector.drain().then_inc(bs, 1)

        @block.scalar
        def _(scalar):
            scalar.wait_ge(bs, 5)
            scalar.activation(bnd[:], bnd[:], AF.Sqrt)
            scalar.drain().then_inc(bs, 1)

        @block.vector
        def _(vector):
            vector.wait_ge(bs, 6)
            for g in range(2):
                vector.reduce_max(nb16[:, 2 + g:3 + g], bnd[:, g * 512:(g + 1) * 512], axis=AX)
            vector.drain()
            vector.tensor_scalar_mul(nb16[:], nb16[:], -1.0)
            vector.drain().then_inc(bs, 1)

        kv_ld = [C.sem("kv0"), C.sem("kv1")]
        stc = C.sem("stc")
        exc = C.sem("exc")
        mkc = C.sem("mkc")
        pvc = C.sem("pvc")
        evc = C.sem("evc")
        ysl = C.sem("ysl")
        bsd1 = C.sem("bsd1")
        bsd2 = C.sem("bsd2")
        cvs = C.sem("cvs")

        @block.gpsimd
        def _(gpsimd):
            gpsimd.wait_ge(bs, 7)
            gpsimd.dma_start(out=nb_d, in_=nb16[:]).then_inc(bsd1, 16)
            ncv = 0
            for (srcT, dstT) in ios.get("conv", []):
                R, Cc = srcT.shape[0], srcT.shape[1]
                step = max(256, (2 * 1024 * 1024) // (Cc * 4))
                r0 = 0
                while r0 < R:
                    r1 = min(R, r0 + step)
                    gpsimd.wait_ge(cvs, ncv * 16)
                    gpsimd.wait_ge(pvc, min(ncv * 100, 3100))
                    gpsimd.dma_start(out=dstT[r0:r1], in_=srcT[r0:r1]).then_inc(cvs, 16)
                    ncv += 1
                    r0 = r1

        @block.sync
        def _(sync):
            sync.wait_ge(bsd1, 16)
            sync.dma_start(out=nbrow[0:1, :].rearrange("o (h g) -> o h g", g=4),
                           in_=nb_d.rearrange("(o h) g -> o h g", o=1)).then_inc(bsd2, 16)

        @block.vector
        def _(vector):
            vector.wait_ge(bsd2, 16)
            nbv = nbrow[0:1, :].rearrange("o (h wg) -> o h wg", wg=4)
            bv = brow[0:1, :].rearrange("o (w h g s) -> o w h g s", w=2, h=NH, g=2)
            for w in range(2):
                for g in range(2):
                    vector.tensor_tensor(
                        bv[:, w, :, g, :],
                        nbv[:, :, 2 * w + g:2 * w + g + 1].broadcast_to((1, NH, KSLOT)),
                        pmrow[0:1, :].rearrange("o (w h g s) -> o w h g s", w=2, h=NH, g=2)[:, w, :, g, :],
                        op=AluOpType.mult)
            vector.drain()
            vector.tensor_scalar(pmrow[:], pmrow[:], 1e9, 1e9, op0=AluOpType.mult, op1=AluOpType.subtract)
            vector.drain()
            vector.tensor_tensor(brow[:], brow[:], pmrow[:], op=AluOpType.add)
            vector.drain()
            vector.tensor_copy(brow_bf[:], brow[:])
            vector.drain().then_inc(bs, 1)

        @block.tensor
        def _(tensor):
            tensor.wait_ge(bs, 8)
            for seg in range((NBIAS + 511) // 512):
                w = min(512, NBIAS - seg * 512)
                if seg > 0:
                    tensor.wait_ge(bs, 8 + 2 * seg)
                tensor.matmul(psA[:, 0:w], ones_row[:], brow_bf[0:1, seg * 512:seg * 512 + w],
                              start=True, stop=True).then_inc(bs, 1)

        @block.vector
        def _(vector):
            for seg in range((NBIAS + 511) // 512):
                w = min(512, NBIAS - seg * 512)
                vector.wait_ge(bs, 9 + seg * 2)
                vector.tensor_copy(bias_sb[:, seg * 512:seg * 512 + w], psA[:, 0:w])
                vector.drain().then_inc(bs, 1)
        BS_DONE = 8 + 2 * ((NBIAS + 511) // 512)


        CNT = dict(st=0, mk=0, pvg=0, m=0, ev=0, ysl=0)
        CNT_EX = [0]
        CNT_DV = [0]

        def attention(which):
            groups = []
            for hg in range(8):
                for g in range(2):
                    for qq in range(2):
                        steps = []
                        for kt in range(2 * qq + 2):
                            for hh in range(2):
                                steps.append(("diag", kt, hh))
                        for kb in range(8 if g == 0 else 16):
                            for kt in range(4):
                                for hh in range(2):
                                    steps.append((kb, kt, hh))
                        groups.append((hg, g, qq, steps))

            g_meta = []
            st = CNT["st"]; pvg = CNT["pvg"]; m = CNT["m"]
            ev = CNT["ev"]; ysl_n = CNT["ysl"]
            for (hg, g, qq, steps) in groups:
                meta = dict(pvg_base=pvg, ev_base=ev, m_base=m, ysl_base=ysl_n,
                            nsteps=len(steps))
                st += len(steps)
                pvg += len(steps)
                if which == 0:
                    m += 8 if g == 0 else 16
                ev += 4
                ysl_n += 4
                g_meta.append(meta)

            if which == 0:
                ldgate = []
                pv = CNT["pvg"]
                for (hg, g, qq, steps) in groups:
                    ndiag = (2 * qq + 2) * 2
                    pv += ndiag
                    nkb = 8 if g == 0 else 16
                    for kb in range(nkb):
                        pv += 8
                        ldgate.append(pv)

                @block.sync
                def _(sync):
                    mm = CNT["m"]
                    li = 0
                    for (hg, g, qq, steps) in groups:
                        h0 = 2 * hg
                        nkb = 8 if g == 0 else 16
                        for kb in range(nkb):
                            rk = kb if kb < 8 else 15 - kb
                            hf = 0 if kb < 8 else 1
                            if li >= 2:
                                sync.wait_ge(pvc, ldgate[li - 2])
                            sync.dma_start(
                                out=kTbv[:, mm % 2],
                                in_=kTg_h[rk * 2048 + h0 * 128: rk * 2048 + (h0 + 2) * 128,
                                          hf * 512:(hf + 1) * 512]
                                .rearrange("(c p) f -> p c f", p=128)).then_inc(kv_ld[mm % 2], 16)
                            sync.dma_start(
                                out=vbv[:, mm % 2],
                                in_=vg_h[rk * 1024 + hf * 512: rk * 1024 + hf * 512 + 512,
                                         h0 * 129:(h0 + 2) * 129]
                                .rearrange("(t p) c -> p t c", p=128)).then_inc(kv_ld[mm % 2], 16)
                            mm += 1
                            li += 1
            else:
                @block.sync
                def _(sync):
                    sync.wait_ge(evc, CNT["ev"])
                    n1 = 0
                    sync.dma_start(out=q2Tl[:].rearrange("p (c t) -> p c t", t=TLOC),
                                   in_=q2T_h.rearrange("(c p) t -> p c t", p=128)).then_inc(ld1, 16)
                    n1 += 16
                    for gb in range(NBLK):
                        rk = gb if gb < 8 else 15 - gb
                        hf = 0 if gb < 8 else 1
                        sync.wait_ge(ld1, n1)
                        sync.dma_start(out=kshTs[:, gb * 512:(gb + 1) * 512],
                                       in_=kshg_h[rk * 128:(rk + 1) * 128, hf * 512:(hf + 1) * 512]
                                       ).then_inc(ld1, 16)
                        n1 += 16
                        sync.wait_ge(ld1, n1)
                        sync.dma_start(out=vshv[:, gb * 4:(gb + 1) * 4, :],
                                       in_=vshg_h[rk * 1024 + hf * 512: rk * 1024 + hf * 512 + 512, :]
                                       .rearrange("(t p) c -> p t c", p=128)).then_inc(ld1, 16)
                        n1 += 16

            qsrc = qTl if which == 0 else q2Tl

            @block.tensor
            def _(tensor):
                tensor.wait_ge(bs, BS_DONE)
                if which == 1:
                    tensor.wait_ge(ld1, 16 + 32 * 16)
                st0 = CNT["st"]
                flat = []
                mkcnt = CNT["mk"]
                mloc = CNT["m"]
                for gi, (hg, g, qq, steps) in enumerate(groups):
                    h0 = 2 * hg
                    for si, (kb, kt, hh) in enumerate(steps):
                        diag = (kb == "diag")
                        if diag and kt >= 2 * qq:
                            mkcnt += 1
                        flat.append(dict(gi=gi, g=g, qq=qq, h=h0 + hh, hh=hh, kb=kb, kt=kt,
                                         diag=diag, first=(si == 0), mk=mkcnt,
                                         m=(None if diag or which == 1 else mloc + kb)))
                    if which == 0:
                        mloc += 8 if g == 0 else 16

                def emit_st(ii, info):
                    st = st0 + ii
                    if info["m"] is not None and info["kt"] == 0 and info["hh"] == 0:
                        tensor.wait_ge(kv_ld[info["m"] % 2], (info["m"] // 2 + 1) * 32)
                    if st >= 2:
                        tensor.wait_ge(exc, st - 1)
                    g, qq, h, kt, kb = info["g"], info["qq"], info["h"], info["kt"], info["kb"]
                    if info["diag"]:
                        if which == 0:
                            lhsT = kTl[:, h * TLOC + (g * 4 + kt) * 128: h * TLOC + (g * 4 + kt + 1) * 128]
                        else:
                            lhsT = kshTl[:, (g * 4 + kt) * 128:(g * 4 + kt + 1) * 128]
                    else:
                        if which == 0:
                            lhsT = kTbv[:, info["m"] % 2, info["hh"], kt * 128:(kt + 1) * 128]
                        else:
                            lhsT = kshTs[:, (kb * 4 + kt) * 128:(kb * 4 + kt + 1) * 128]
                    qoff = h * TLOC + g * 512 + qq * 256
                    tensor.matmul(psA[:, (st % 2) * 512:(st % 2) * 512 + 256],
                                  lhsT, qsrc[:, qoff:qoff + 256],
                                  start=True, stop=True).then_inc(stc, 1)

                def emit_pv(ii, info):
                    st = st0 + ii
                    if info["first"]:
                        tensor.wait_ge(evc, g_meta[info["gi"]]["ev_base"])
                    if info["diag"] and info["kt"] >= 2 * info["qq"]:
                        tensor.wait_ge(mkc, info["mk"])
                    else:
                        tensor.wait_ge(exc, st + 1)
                    g, qq, h, kt, kb = info["g"], info["qq"], info["h"], info["kt"], info["kb"]
                    if info["diag"]:
                        jlist = [j for j in range(2) if qq * 2 + j >= kt]
                    else:
                        jlist = [0, 1]
                    for j in jlist:
                        slot = info["hh"] * 2 + j
                        first = (info["diag"] and kt == 0)
                        last = (kb == (7 if g == 0 else 15) and kt == 3)
                        if info["diag"]:
                            if which == 0:
                                rhs = vlv[:, g * 4 + kt, h * 129:h * 129 + 129]
                            else:
                                rhs = vshlv[:, g * 4 + kt, 0:129]
                        else:
                            if which == 0:
                                rhs = vbv[:, info["m"] % 2, kt, h * 129 - info["hh"] * 2 * 129 if False else info["hh"] * 129:info["hh"] * 129 + 129]
                            else:
                                rhs = vshv[:, kb * 4 + kt, 0:129]
                        r = tensor.matmul(psO[:, slot * 512:slot * 512 + 129],
                                          ptbuf[:, (st % 2) * 256 + j * 128:(st % 2) * 256 + (j + 1) * 128],
                                          rhs, start=first, stop=last)
                    r.then_inc(pvc, 1)

                for ii, info in enumerate(flat):
                    emit_st(ii, info)
                    if ii > 0:
                        emit_pv(ii - 1, flat[ii - 1])
                emit_pv(len(flat) - 1, flat[-1])
                CNT["st"] = st0 + len(flat)
                CNT["mk"] = mkcnt
                CNT["m"] = mloc
                CNT["pvg"] = CNT["st"]

            @block.scalar
            def _(scalar):
                scalar.wait_ge(bs, BS_DONE)
                st = CNT_EX[0]
                for gi, (hg, g, qq, steps) in enumerate(groups):
                    h0 = 2 * hg
                    for (kb, kt, hh) in steps:
                        h = h0 + hh
                        slotk = 0 if kb == "diag" else 1 + kb
                        bcol = (0 if which == 0 else NH * 2 * KSLOT) + h * 2 * KSLOT + g * KSLOT + slotk
                        scalar.wait_ge(stc, st + 1)
                        if st >= 2:
                            scalar.wait_ge(pvc, st - 1)
                        scalar.activation(ptbuf[:, (st % 2) * 256:(st % 2) * 256 + 256],
                                          psA[:, (st % 2) * 512:(st % 2) * 512 + 256], AF.Exp,
                                          bias=bias_sb[:, bcol:bcol + 1]).then_inc(exc, 1)
                        st += 1
                    if which == 1:
                        for s in range(4):
                            hh, j = s // 2, s % 2
                            h = h0 + hh
                            tloc = g * 4 + qq * 2 + j
                            scalar.wait_ge(evc, g_meta[gi]["ev_base"] + s + 1)
                            scalar.activation(y2gv[:, tloc, h * 128:(h + 1) * 128],
                                              ytmp[:, (s % 2) * 128:(s % 2) * 128 + 128], AF.Silu).then_inc(ysl, 1)
                CNT_EX[0] = st

            @block.vector
            def _(vector):
                vector.wait_ge(bs, BS_DONE)
                st = CNT_DV[0]
                for gi, (hg, g, qq, steps) in enumerate(groups):
                    h0 = 2 * hg
                    for (kb, kt, hh) in steps:
                        if kb == "diag" and kt >= 2 * qq:
                            j = kt - 2 * qq
                            vector.wait_ge(exc, st + 1)
                            vector.tensor_tensor(
                                ptbuf[:, (st % 2) * 256 + j * 128:(st % 2) * 256 + (j + 1) * 128],
                                ptbuf[:, (st % 2) * 256 + j * 128:(st % 2) * 256 + (j + 1) * 128],
                                maskT[:], op=AluOpType.mult).then_inc(mkc, 1)
                        st += 1
                    vector.wait_ge(pvc, g_meta[gi]["pvg_base"] + g_meta[gi]["nsteps"])
                    for s in range(4):
                        hh, j = s // 2, s % 2
                        h = h0 + hh
                        tloc = g * 4 + qq * 2 + j
                        rs = rec1[:, s % 2:s % 2 + 1]
                        vector.reciprocal(rs, psO[:, s * 512 + 128:s * 512 + 129])
                        vector.drain()
                        if which == 0:
                            vector.tensor_scalar(attv[:, tloc, h * 128:(h + 1) * 128],
                                                 psO[:, s * 512:s * 512 + 128],
                                                 rs, None, op0=AluOpType.mult).then_inc(evc, 1)
                        else:
                            if s >= 2:
                                vector.wait_ge(ysl, g_meta[gi]["ysl_base"] + s - 1)
                            vector.tensor_scalar(ytmp[:, (s % 2) * 128:(s % 2) * 128 + 128],
                                                 psO[:, s * 512:s * 512 + 128],
                                                 rs, None, op0=AluOpType.mult).then_inc(evc, 1)
                CNT_DV[0] = st
            CNT["ev"] += len(groups) * 4
            if which == 1:
                CNT["ysl"] += len(groups) * 4

        attention(0)
        ATT_EVC = CNT["ev"]
        attention(1)
        ALL_EVC = CNT["ev"]
        ALL_YSL = CNT["ysl"]

        wo_ld = C.sem("wo_ld")
        xld = [C.sem("xl0"), C.sem("xl1")]
        trs = C.sem("trs")
        tev = C.sem("tev")
        epm = C.sem("epm")
        ecb = C.sem("ecb")
        x1s = C.sem("x1s")

        @block.gpsimd
        def _(gpsimd):
            gpsimd.wait_ge(ysl, ALL_YSL)
            gpsimd.dma_start(out=wofv, in_=woT_h.rearrange("(c p) f -> p c f", p=128)).then_inc(wo_ld, 16)

        @block.sync
        def _(sync):
            for t in range(NTILE):
                if t >= 2:
                    sync.wait_ge(ecb, (t - 1) * 4)
                sync.dma_start(out=xt2[:, (t % 2) * D:(t % 2) * D + D],
                               in_=x_h.rearrange("(t p) f -> p t f", p=128)[:, t, :]).then_inc(xld[t % 2], 16)

        @block.tensor
        def _(tensor):
            tensor.wait_ge(evc, ALL_EVC)
            tensor.wait_ge(wo_ld, 16)
            for t in range(NTILE):
                for gtr in range(2):
                    G = t * 2 + gtr
                    if G >= 2:
                        tensor.wait_ge(tev, G - 1)
                    for jj in range(8):
                        e = gtr * 8 + jj
                        r = tensor.transpose(ptb[:, (G % 2) * 1024 + jj * 128:(G % 2) * 1024 + (jj + 1) * 128],
                                             attv[:, t, e * 128:(e + 1) * 128], ident[:])
                    r.then_inc(trs, 1)
                tensor.wait_ge(tev, 2 * t + 2)
                for fs in range(4):
                    it = t * 4 + fs
                    if it >= 2:
                        tensor.wait_ge(ecb, it - 1)
                    for e in range(NCH):
                        r = tensor.matmul(psA[:, (it % 2) * 512:(it % 2) * 512 + 512],
                                          attTt[:, e * 128:(e + 1) * 128],
                                          wofv[:, e, fs * 512:(fs + 1) * 512],
                                          start=(e == 0), stop=(e == NCH - 1))
                    r.then_inc(epm, 1)

        @block.vector
        def _(vector):
            for t in range(NTILE):
                for gtr in range(2):
                    G = t * 2 + gtr
                    vector.wait_ge(trs, G + 1)
                    if t >= 1 and gtr == 0:
                        vector.wait_ge(epm, t * 4)
                    vector.tensor_copy(
                        attTt[:].rearrange("p (c tt) -> p c tt", tt=128)[:, gtr * 8:(gtr + 1) * 8, :],
                        ptb[:, (G % 2) * 1024:(G % 2) * 1024 + 1024].rearrange("p (c tt) -> p c tt", tt=128))
                    vector.drain().then_inc(tev, 1)
                vector.wait_ge(xld[t % 2], (t // 2 + 1) * 16)
                vector.wait_ge(ysl, ALL_YSL)
                for fs in range(4):
                    it = t * 4 + fs
                    vector.wait_ge(epm, it + 1)
                    if t > 0 and fs == 0:
                        vector.wait_ge(x1s, t * 16)
                    vector.tensor_tensor(etmp[:], psA[:, (it % 2) * 512:(it % 2) * 512 + 512],
                                         y2gv[:, t, fs * 512:(fs + 1) * 512], op=AluOpType.mult)
                    vector.drain()
                    vector.tensor_tensor(x1st[:, fs * 512:(fs + 1) * 512], etmp[:],
                                         xt2[:, (t % 2) * D + fs * 512:(t % 2) * D + (fs + 1) * 512],
                                         op=AluOpType.add)
                    vector.drain().then_inc(ecb, 1)

        @block.gpsimd
        def _(gpsimd):
            for t in range(NTILE):
                gpsimd.wait_ge(ecb, (t + 1) * 4)
                gpsimd.dma_start(out=x1_o.rearrange("(t p) f -> p t f", p=128)[:, t, :],
                                 in_=x1st[:]).then_inc(x1s, 16)
            gpsimd.wait_ge(x1s, NTILE * 16)
            gpsimd.sem_inc(phase, 1)



EPSR = 1.1920929e-07


def emit_l1_a(nc, block, phase, pgate, ios, prefix, semstack):
    x1_h = ios["x1"]
    wqkvT_h = ios["wqkvT"]
    ident_h = ios["ident"]
    stats_o = ios["stats_o"]
    sig_o = ios["sig_o"]
    wcum_o = ios["wcum_o"]
    kvcum_o = ios["kvcum_o"]
    tot_o = ios["tot_o"]
    with contextlib.ExitStack() as ectx:
        C = Ctx(nc, ectx, prefix, semstack)
        xt2 = C.sb("xt2", [128, 2 * D], F32)
        xnf = C.sb("xnf", [128, D], F32)
        sq_s = C.sb("sq_s", [128, D], F32)
        sqbf = C.sb("sqbf", [128, TLOC], BF16)
        st8 = C.sb("st8", [128, 64], F32)
        sx8 = st8[:, 0:8]; sqs8 = st8[:, 8:16]; m8 = st8[:, 16:24]; e28 = st8[:, 24:32]
        msq8 = st8[:, 32:40]; var8 = st8[:, 40:48]; lnv8 = st8[:, 48:56]; rstd8 = st8[:, 56:64]
        stst = C.sb("stst", [128, 16], F32)
        rr8 = C.sb("rr8", [128, 24], F32)
        ssr8 = rr8[:, 0:8]; lnr8 = rr8[:, 8:16]; rsr8 = rr8[:, 16:24]
        eps1 = C.sb("eps1", [128, 1], F32)
        epsr = C.sb("epsr", [128, 1], F32)
        hbf = C.sb("hbf", [128, D], BF16)
        hT = C.sb("hT", [128, NCH * TLOC], BF16)
        ident = C.sb("ident_sb", [128, 128], BF16)
        ones_bf = C.sb("ones_bf", [128, 1], BF16)
        ones_f = C.sb("ones_f", [1, 128], F32)
        qsb = C.sb("qsb", [128, NCH * TLOC], BF16)
        ksb = C.sb("ksb", [128, NCH * TLOC], BF16)
        vsb = C.sb("vsb", [128, NCH * TLOC], BF16)
        wslab = C.sb("wslab", [128, 2 * NCH * 128], BF16)
        wsv = wslab[:].rearrange("p (b c f) -> p b c f", b=2, f=128)
        nqrow = C.sb("nqrow", [1, TLOC], F32)
        nkrow = C.sb("nkrow", [1, TLOC], F32)
        rqrow = C.sb("rqrow", [1, TLOC], F32)
        rkrow = C.sb("rkrow", [1, TLOC], F32)
        rqb = C.sb("rqb", [128, TLOC], F32)
        rkb = C.sb("rkb", [128, TLOC], F32)
        tot = C.sb("tot", [128, 64], F32)
        hbase = nc.lookup_mloc(hT).addr
        ktmp = nc.alloc_sbuf_tensor_at("ktmp", [128, TLOC], F32, offset=hbase)
        wtmp = nc.alloc_sbuf_tensor_at("wtmp", [128, TLOC], F32, offset=hbase + 4096)
        wvtmp = nc.alloc_sbuf_tensor_at("wvtmp", [128, TLOC], F32, offset=hbase + 8192)
        qtmp = nc.alloc_sbuf_tensor_at("qtmp", [128, TLOC], F32, offset=hbase + 12288)
        xbase = nc.lookup_mloc(xt2).addr
        cst = nc.alloc_sbuf_tensor_at("cst", [128, 4 * TLOC], F32, offset=xbase)
        sgst = nc.alloc_sbuf_tensor_at("sgst", [128, 2 * TLOC], BF16, offset=nc.lookup_mloc(sq_s).addr)

        pj = C.ps("pj", [128, 2048], F32)
        ptb_ctx = contextlib.ExitStack()
        ptb = ptb_ctx.enter_context(nc.psum_tensor("p3_ptb", [128, 2048], BF16))

        xldA = [C.sem("xldA0"), C.sem("xldA1")]
        xldB = [C.sem("xldB0"), C.sem("xldB1")]
        pav = C.sem("pav")
        pas = C.sem("pas")
        tld = C.sem("tld")
        ms = C.sem("ms")
        stm = C.sem("stm")
        n_st = [0]

        def St(r):
            r.then_inc(stm, 16)
            n_st[0] += 16


        @block.sync
        def _(sync):
            for t in range(NTILE):
                if t >= 2:
                    sync.wait_ge(pav, t - 1)
                    sync.wait_ge(pas, t - 1)
                sync.dma_start(out=xt2[:, (t % 2) * D:(t % 2) * D + D],
                               in_=x1_h.rearrange("(t p) f -> p t f", p=128)[:, t, :]).then_inc(xldA[t % 2], 16)
            sync.dma_start(out=ident[:], in_=ident_h).then_inc(tld, 16)

        @block.gpsimd
        def _(gpsimd):
            gpsimd.memset(eps1[:], 1e-5)
            gpsimd.memset(epsr[:], EPSR)
            gpsimd.memset(ones_bf[:], 1.0)
            gpsimd.memset(ones_f[:], 1.0)
            gpsimd.drain().then_inc(ms, 1)

        @block.vector
        def _(vector):
            for t in range(NTILE):
                vector.wait_ge(xldA[t % 2], (t // 2 + 1) * 16)
                vector.reduce_sum(sx8[:, t:t + 1], xt2[:, (t % 2) * D:(t % 2) * D + D], axis=AX)
                vector.drain().then_inc(pav, 1)

        @block.scalar
        def _(scalar):
            for t in range(NTILE):
                scalar.wait_ge(xldA[t % 2], (t // 2 + 1) * 16)
                scalar.activation(sq_s[:], xt2[:, (t % 2) * D:(t % 2) * D + D], AF.Square,
                                  accum_out=sqs8[:, t:t + 1])
                scalar.drain().then_inc(pas, 1)

        xn_s = C.sem("xn_s")
        pb = C.sem("pb")

        @block.sync
        def _(sync):
            sync.wait_ge(pav, NTILE)
            sync.wait_ge(pas, NTILE)
            for t in range(NTILE):
                if t >= 2:
                    sync.wait_ge(xn_s, 2 * (t - 1))
                sync.dma_start(out=xt2[:, (t % 2) * D:(t % 2) * D + D],
                               in_=x1_h.rearrange("(t p) f -> p t f", p=128)[:, t, :]).then_inc(xldB[t % 2], 16)

        @block.vector
        def _(vector):
            vector.wait_ge(pav, NTILE)
            vector.wait_ge(pas, NTILE)
            c = 1.0 / D
            vector.tensor_scalar_mul(m8, sx8, c)
            vector.tensor_scalar_mul(e28, sqs8, c)
            vector.drain()
            vector.tensor_tensor(msq8, m8, m8, op=AluOpType.mult)
            vector.drain()
            vector.tensor_tensor(var8, e28, msq8, op=AluOpType.subtract)
            vector.drain().then_inc(pb, 1)

        @block.scalar
        def _(scalar):
            scalar.wait_ge(pb, 1)
            scalar.wait_ge(ms, 1)
            scalar.activation(lnv8, var8, AF.Ln, bias=eps1[:])
            scalar.drain()
            scalar.activation(rstd8, lnv8, AF.Exp, scale=-0.5)
            scalar.drain().then_inc(pb, 1)

        @block.vector
        def _(vector):
            vector.wait_ge(pb, 2)
            vector.tensor_copy(stst[:, 0:8], m8)
            vector.tensor_copy(stst[:, 8:16], rstd8)
            vector.drain().then_inc(pb, 1)

        @block.gpsimd
        def _(gpsimd):
            gpsimd.wait_ge(pb, 3)
            St(gpsimd.dma_start(out=stats_o, in_=stst[:]))

        tr_s = C.sem("tr_s")
        xe_s = C.sem("xe_s")
        rq_s = C.sem("rq_s")

        @block.vector
        def _(vector):
            vector.wait_ge(pb, 2)
            for t in range(NTILE):
                vector.wait_ge(xldB[t % 2], (t // 2 + 1) * 16)
                vector.tensor_scalar(xnf[:], xt2[:, (t % 2) * D:(t % 2) * D + D],
                                     m8[:, t:t + 1], rstd8[:, t:t + 1],
                                     op0=AluOpType.subtract, op1=AluOpType.mult)
                vector.drain().then_inc(xn_s, 1)
                vector.wait_ge(rq_s, t + 1)
                vector.tensor_scalar(hbf[:], xnf[:], rsr8[:, t:t + 1], None, op0=AluOpType.mult)
                vector.drain().then_inc(xn_s, 1)
                for g in range(2):
                    G = t * 2 + g
                    vector.wait_ge(tr_s, G + 1)
                    vector.tensor_copy(
                        hT[:].rearrange("p (c tt) -> p c tt", tt=TLOC)[:, g * 8:(g + 1) * 8, t * 128:(t + 1) * 128],
                        ptb[:, (G % 2) * 1024:(G % 2) * 1024 + 1024].rearrange("p (c tt) -> p c tt", tt=128)).then_inc(xe_s, 1)

        @block.scalar
        def _(scalar):
            for t in range(NTILE):
                scalar.wait_ge(xn_s, 2 * t + 1)
                scalar.activation(sq_s[:], xnf[:], AF.Square, accum_out=ssr8[:, t:t + 1])
                scalar.drain()
                scalar.activation(lnr8[:, t:t + 1], ssr8[:, t:t + 1], AF.Ln, bias=epsr[:], scale=1.0 / D)
                scalar.drain()
                scalar.activation(rsr8[:, t:t + 1], lnr8[:, t:t + 1], AF.Exp, scale=-0.5)
                scalar.drain().then_inc(rq_s, 1)

        @block.tensor
        def _(tensor):
            tensor.wait_ge(tld, 16)
            for t in range(NTILE):
                tensor.wait_ge(xn_s, 2 * t + 2)
                for g in range(2):
                    G = t * 2 + g
                    if G >= 2:
                        tensor.wait_ge(xe_s, G - 1)
                    for j in range(8):
                        e = g * 8 + j
                        r = tensor.transpose(ptb[:, (G % 2) * 1024 + j * 128:(G % 2) * 1024 + (j + 1) * 128],
                                             hbf[:, e * 128:(e + 1) * 128], ident[:])
                    r.then_inc(tr_s, 1)
        XE_ALL = 16
        ptb_ctx.close()
        nrm = C.ps("nrm", [128, 1024], F32)
        nr2 = C.ps("nr2", [128, 1024], F32)

        lw = [C.sem("lw0"), C.sem("lw1")]
        pm = C.sem("pm")
        se = C.sem("se")
        qs2 = C.sem("qs2")
        nmq = C.sem("nmq")

        @block.gpsimd
        def _(gpsimd):
            gpsimd.wait_ge(ms, 1)
            for i in range(3 * NCH):
                if i >= 2:
                    gpsimd.wait_ge(pm, i - 1)
                gpsimd.dma_start(out=wsv[:, i % 2],
                                 in_=wqkvT_h[:, i * 128:(i + 1) * 128].rearrange("(c p) f -> p c f", p=128)
                                 ).then_inc(lw[i % 2], 16)

        @block.tensor
        def _(tensor):
            tensor.wait_ge(xe_s, XE_ALL)
            for i in range(3 * NCH):
                sect, hc = divmod(i, NCH)
                tensor.wait_ge(lw[i % 2], (i // 2 + 1) * 16)
                if i >= 2:
                    tensor.wait_ge(se, i - 1)
                for half in range(2):
                    bank = pj[:, (i % 2) * 1024 + half * 512:(i % 2) * 1024 + half * 512 + 512]
                    for e in range(NCH):
                        r = tensor.matmul(bank,
                                          wsv[:, i % 2, e],
                                          hT[:, e * TLOC + half * 512: e * TLOC + (half + 1) * 512],
                                          start=(e == 0), stop=(e == NCH - 1))
                r.then_inc(pm, 1)
                if sect < 2:
                    tensor.wait_ge(qs2, i + 1)
                    bank2 = nrm if sect == 0 else nr2
                    for half in range(2):
                        r = tensor.matmul(bank2[0:1, half * 512:half * 512 + 512],
                                          ones_bf[:], sqbf[:, half * 512:(half + 1) * 512],
                                          start=(hc == 0), stop=(hc == NCH - 1))
                    r.then_inc(nmq, 1)

        @block.vector
        def _(vector):
            for i in range(3 * NCH):
                sect, hc = divmod(i, NCH)
                dst = (qsb, ksb, vsb)[sect]
                vector.wait_ge(pm, i + 1)
                vector.tensor_copy(dst[:, hc * TLOC:(hc + 1) * TLOC],
                                   pj[:, (i % 2) * 1024:(i % 2) * 1024 + 1024]).then_inc(se, 1)

        @block.scalar
        def _(scalar):
            for i in range(3 * NCH):
                sect, hc = divmod(i, NCH)
                if sect < 2:
                    scalar.wait_ge(se, i + 1)
                    dst = (qsb, ksb)[sect]
                    scalar.activation(sqbf[:], dst[:, hc * TLOC:(hc + 1) * TLOC], AF.Square).then_inc(qs2, 1)
                else:
                    scalar.drain().then_inc(qs2, 1)

        rw = C.sem("rw")

        @block.vector
        def _(vector):
            vector.wait_ge(nmq, 2 * NCH)
            vector.tensor_copy(nqrow[:], nrm[0:1, :])
            vector.tensor_copy(nkrow[:], nr2[0:1, :])
            vector.drain().then_inc(rw, 1)

        @block.scalar
        def _(scalar):
            scalar.wait_ge(rw, 1)
            scalar.activation(nqrow[:], nqrow[:], AF.Ln, bias=epsr[0:1, :], scale=1.0 / D)
            scalar.activation(nkrow[:], nkrow[:], AF.Ln, bias=epsr[0:1, :], scale=1.0 / D)
            scalar.drain()
            scalar.activation(rqrow[:], nqrow[:], AF.Exp, scale=-0.5)
            scalar.activation(rkrow[:], nkrow[:], AF.Exp, scale=-0.5)
            scalar.drain().then_inc(rw, 1)

        @block.tensor
        def _(tensor):
            tensor.wait_ge(rw, 2)
            for half in range(2):
                tensor.matmul(nrm[:, half * 512:half * 512 + 512], ones_f[:],
                              rqrow[0:1, half * 512:(half + 1) * 512], start=True, stop=True)
                r = tensor.matmul(nr2[:, half * 512:half * 512 + 512], ones_f[:],
                                  rkrow[0:1, half * 512:(half + 1) * 512], start=True, stop=True)
            r.then_inc(rw, 1)

        @block.vector
        def _(vector):
            vector.wait_ge(rw, 3)
            vector.tensor_copy(rqb[:], nrm[:])
            vector.tensor_copy(rkb[:], nr2[:])
            vector.drain().then_inc(rw, 1)

        ck = C.sem("ck")
        cka = C.sem("cka")
        cst_s = [C.sem("cst_s0"), C.sem("cst_s1")]
        sg_s = [C.sem("sg_s0"), C.sem("sg_s1")]

        @block.vector
        def _(vector):
            vector.wait_ge(rw, 4)
            for e in range(NCH):
                if e >= 2:
                    vector.wait_ge(cst_s[e % 2], (e // 2) * 32)
                vector.tensor_tensor(ktmp[:], ksb[:, e * TLOC:(e + 1) * TLOC], rkb[:], op=AluOpType.mult)
                vector.drain().then_inc(ck, 1)
                vector.wait_ge(cka, 2 * e + 1)
                vector.tensor_tensor(wvtmp[:], wtmp[:], vsb[:, e * TLOC:(e + 1) * TLOC], op=AluOpType.mult)
                vector.drain()
                cslot = cst[:, (e % 2) * 2 * TLOC:(e % 2) * 2 * TLOC + 2 * TLOC]
                for b in range(2):
                    sl = slice(b * BLK, (b + 1) * BLK)
                    vector.tensor_tensor_scan(cslot[:, 0:TLOC][:, sl], wtmp[:, sl], wtmp[:, sl], 0.0,
                                              op0=AluOpType.add, op1=AluOpType.bypass)
                    vector.tensor_tensor_scan(cslot[:, TLOC:2 * TLOC][:, sl], wvtmp[:, sl], wvtmp[:, sl], 0.0,
                                              op0=AluOpType.add, op1=AluOpType.bypass)
                vector.drain()
                vector.tensor_copy(tot[:, e:e + 1], cslot[:, 511:512])
                vector.tensor_copy(tot[:, 16 + e:17 + e], cslot[:, 1023:1024])
                vector.tensor_copy(tot[:, 32 + e:33 + e], cslot[:, TLOC + 511:TLOC + 512])
                vector.tensor_copy(tot[:, 48 + e:49 + e], cslot[:, TLOC + 1023:TLOC + 1024])
                vector.drain().then_inc(ck, 1)
                vector.tensor_tensor(qtmp[:], qsb[:, e * TLOC:(e + 1) * TLOC], rqb[:], op=AluOpType.mult)
                vector.drain().then_inc(ck, 1)

        @block.scalar
        def _(scalar):
            for e in range(NCH):
                scalar.wait_ge(ck, 3 * e + 1)
                scalar.activation(wtmp[:], ktmp[:], AF.Exp).then_inc(cka, 1)
                scalar.wait_ge(ck, 3 * e + 3)
                if e >= 2:
                    scalar.wait_ge(sg_s[e % 2], (e // 2) * 16)
                scalar.activation(sgst[:, (e % 2) * TLOC:(e % 2) * TLOC + TLOC], qtmp[:], AF.Sigmoid).then_inc(cka, 1)

        @block.gpsimd
        def _(gpsimd):
            for e in range(NCH):
                gpsimd.wait_ge(ck, 3 * e + 2)
                cslot = cst[:, (e % 2) * 2 * TLOC:(e % 2) * 2 * TLOC + 2 * TLOC]
                gpsimd.dma_start(out=wcum_o[e * 128:(e + 1) * 128, :], in_=cslot[:, 0:TLOC]).then_inc(cst_s[e % 2], 16)
                gpsimd.dma_start(out=kvcum_o[e * 128:(e + 1) * 128, :], in_=cslot[:, TLOC:2 * TLOC]).then_inc(cst_s[e % 2], 16)
                gpsimd.wait_ge(cka, 2 * e + 2)
                gpsimd.dma_start(out=sig_o[e * 128:(e + 1) * 128, :],
                                 in_=sgst[:, (e % 2) * TLOC:(e % 2) * TLOC + TLOC]).then_inc(sg_s[e % 2], 16)
            gpsimd.wait_ge(ck, 3 * NCH)
            St(gpsimd.dma_start(out=tot_o, in_=tot[:]))
            for par in range(2):
                gpsimd.wait_ge(cst_s[par], NCH * 16)
                gpsimd.wait_ge(sg_s[par], NCH * 8)
            gpsimd.wait_ge(stm, n_st[0])
            gpsimd.sem_inc(phase, 1)



def emit_l1_b(nc, block, phase, pgate, ios, prefix, semstack):
    x1_h = ios["x1"]
    stats_h = ios["stats"]
    sig_h = ios["sig"]
    wcum_h = ios["wcum"]
    kvcum_h = ios["kvcum"]
    tots_h = ios["tots"]
    selm_h = ios["selm"]
    wswT_h = ios["wswT"]
    woutT_h = ios["woutT"]
    s_o = ios["s_o"]
    with contextlib.ExitStack() as ectx:
        C = Ctx(nc, ectx, prefix, semstack)
        tots = C.sb("tots_sb", [128, NCORE * 64], F32)
        selm = C.sb("selm_sb", [128, 32], F32)
        carry = C.sb("carry", [128, 64], F32)
        stt8 = C.sb("stt8", [128, 16], F32)
        cch = C.sb("cch", [128, 2 * 2 * TLOC], F32)
        sgch = C.sb("sgch", [128, 2 * TLOC], BF16)
        wg_t = C.sb("wg_t", [128, TLOC], F32)
        kvg = C.sb("kvg", [128, TLOC], F32)
        rec = C.sb("rec", [128, TLOC], F32)
        yT = C.sb("yT", [128, NCH * TLOC], BF16)
        zbuf = C.sb("zbuf", [128, 64 * 512], BF16)
        zbv = zbuf[:].rearrange("p (c f) -> p c f", f=512)
        swsl = C.sb("swsl", [128, 4 * NCH * 128], BF16)
        swv = swsl[:].rearrange("p (b c f) -> p b c f", b=4, f=128)
        wosl = C.sb("wosl", [128, 2 * 512], BF16)
        wov = wosl[:].rearrange("p (b f) -> p b f", b=2)
        silt = C.sb("silt", [128, 2 * 512], F32)
        x1t = C.sb("x1t", [128, 4 * D], F32)
        xnsl = C.sb("xnsl", [128, 2 * 512], F32)
        ssl = C.sb("ssl", [128, 2 * 512], F32)

        pj = C.ps("pj", [128, 1024], F32)
        po = C.ps("po", [128, 2048], F32)

        ld0 = C.sem("ld0")

        @block.sync
        def _(sync):
            sync.dma_start(out=tots[:].rearrange("p (r c) -> p r c", c=64),
                           in_=tots_h.rearrange("(r p) c -> p r c", p=128)).then_inc(ld0, 16)
            sync.wait_ge(ld0, 16)
            sync.dma_start(out=selm[:], in_=selm_h).then_inc(ld0, 16)
            sync.wait_ge(ld0, 32)
            sync.dma_start(out=stt8[:], in_=stats_h).then_inc(ld0, 16)

        cr = C.sem("cr")

        @block.vector
        def _(vector):
            vector.wait_ge(ld0, 48)
            vector.memset(carry[:], 0.0)
            vector.drain()
            for b in range(2):
                for qty in range(2):
                    for gblk in range(NBLK):
                        corec = gblk if gblk < 8 else 15 - gblk
                        blki = 0 if gblk < 8 else 1
                        src0 = tots[:, corec * 64 + qty * 32 + blki * 16: corec * 64 + qty * 32 + blki * 16 + 16]
                        vector.scalar_tensor_tensor(
                            carry[:, qty * 32 + b * 16: qty * 32 + b * 16 + 16],
                            src0, selm[:, b * 16 + gblk: b * 16 + gblk + 1],
                            carry[:, qty * 32 + b * 16: qty * 32 + b * 16 + 16],
                            op0=AluOpType.mult, op1=AluOpType.add)
                        vector.drain()
            vector.drain().then_inc(cr, 1)

        ych = [C.sem("ych0"), C.sem("ych1")]
        yc = C.sem("yc")

        @block.sync
        def _(sync):
            for e in range(NCH):
                if e >= 2:
                    sync.wait_ge(yc, e - 1)
                sync.dma_start(out=cch[:, (e % 2) * 2 * TLOC:(e % 2) * 2 * TLOC + TLOC],
                               in_=wcum_h[e * 128:(e + 1) * 128, :]).then_inc(ych[e % 2], 16)
                sync.dma_start(out=cch[:, (e % 2) * 2 * TLOC + TLOC:(e % 2) * 2 * TLOC + 2 * TLOC],
                               in_=kvcum_h[e * 128:(e + 1) * 128, :]).then_inc(ych[e % 2], 16)
                sync.dma_start(out=sgch[:, (e % 2) * TLOC:(e % 2) * TLOC + TLOC],
                               in_=sig_h[e * 128:(e + 1) * 128, :]).then_inc(ych[e % 2], 16)

        @block.vector
        def _(vector):
            vector.wait_ge(cr, 1)
            for e in range(NCH):
                vector.wait_ge(ych[e % 2], (e // 2 + 1) * 48)
                cslot = cch[:, (e % 2) * 2 * TLOC:(e % 2) * 2 * TLOC + 2 * TLOC]
                for b in range(2):
                    sl = slice(b * BLK, (b + 1) * BLK)
                    vector.tensor_scalar(wg_t[:, sl], cslot[:, 0:TLOC][:, sl],
                                         carry[:, 0 * 32 + b * 16 + e:0 * 32 + b * 16 + e + 1], 1e-6,
                                         op0=AluOpType.add, op1=AluOpType.add)
                    vector.tensor_scalar(kvg[:, sl], cslot[:, TLOC:2 * TLOC][:, sl],
                                         carry[:, 1 * 32 + b * 16 + e:1 * 32 + b * 16 + e + 1], None,
                                         op0=AluOpType.add)
                vector.drain()
                vector.reciprocal(rec[:], wg_t[:])
                vector.drain()
                vector.tensor_tensor(kvg[:], kvg[:], rec[:], op=AluOpType.mult)
                vector.drain()
                vector.tensor_tensor(yT[:, e * TLOC:(e + 1) * TLOC], kvg[:],
                                     sgch[:, (e % 2) * TLOC:(e % 2) * TLOC + TLOC], op=AluOpType.mult)
                vector.drain().then_inc(yc, 1)
        YC_ALL = NCH

        swl = [C.sem("swl0"), C.sem("swl1")]
        wol = [C.sem("wol0"), C.sem("wol1")]
        zm = C.sem("zm")
        zs = C.sem("zs")
        zv = C.sem("zv")
        om = C.sem("om")
        oe = C.sem("oe")
        xls = C.sem("xls")
        sso = [C.sem("sso0"), C.sem("sso1")]

        @block.gpsimd
        def _(gpsimd):
            for tp in range(2):
                for zc in range(64):
                    i = tp * 64 + zc
                    if i >= 2:
                        gpsimd.wait_ge(zm, i - 1)
                    gpsimd.dma_start(out=swv[:, (i % 2) * 2],
                                     in_=wswT_h[:, zc * 128:(zc + 1) * 128].rearrange("(c p) f -> p c f", p=128)
                                     ).then_inc(swl[i % 2], 16)
                    gpsimd.dma_start(out=swv[:, (i % 2) * 2 + 1],
                                     in_=wswT_h[:, 8192 + zc * 128:8192 + (zc + 1) * 128].rearrange("(c p) f -> p c f", p=128)
                                     ).then_inc(swl[i % 2], 16)
                for fs in range(4):
                    for zc in range(64):
                        i2 = tp * 256 + fs * 64 + zc
                        if i2 >= 2:
                            gpsimd.wait_ge(om, i2 - 1)
                        gpsimd.dma_start(out=wov[:, i2 % 2],
                                         in_=woutT_h[zc * 128:(zc + 1) * 128, fs * 512:(fs + 1) * 512]
                                         .rearrange("(c p) f -> p c f", p=128)[:, 0, :] if False else
                                         woutT_h[zc * 128:(zc + 1) * 128, fs * 512:(fs + 1) * 512]
                                         ).then_inc(wol[i2 % 2], 16)

        @block.tensor
        def _(tensor):
            tensor.wait_ge(yc, YC_ALL)
            for tp in range(2):
                for zc in range(64):
                    i = tp * 64 + zc
                    tensor.wait_ge(swl[i % 2], (i // 2 + 1) * 32)
                    if i >= 1:
                        tensor.wait_ge(zv, i)
                        tensor.wait_ge(zs, i)
                    if tp == 1 and zc == 0:
                        tensor.wait_ge(oe, 16)
                    for e in range(NCH):
                        tensor.matmul(pj[:, 0:512], swv[:, (i % 2) * 2, e],
                                      yT[:, e * TLOC + tp * 512: e * TLOC + tp * 512 + 512],
                                      start=(e == 0), stop=(e == NCH - 1))
                    for e in range(NCH):
                        r = tensor.matmul(pj[:, 512:1024], swv[:, (i % 2) * 2 + 1, e],
                                          yT[:, e * TLOC + tp * 512: e * TLOC + tp * 512 + 512],
                                          start=(e == 0), stop=(e == NCH - 1))
                    r.then_inc(zm, 1)
                for fs in range(4):
                    for zc in range(64):
                        i2 = tp * 256 + fs * 64 + zc
                        tensor.wait_ge(wol[i2 % 2], (i2 // 2 + 1) * 16)
                        if zc == 0:
                            tensor.wait_ge(zv, tp * 64 + 64)
                            if fs > 0 or tp > 0:
                                tensor.wait_ge(oe, (tp * 4 + fs) * 4)
                        for t4 in range(4):
                            r = tensor.matmul(po[:, t4 * 512:(t4 + 1) * 512],
                                              zbv[:, zc, t4 * 128:(t4 + 1) * 128],
                                              wov[:, i2 % 2],
                                              start=(zc == 0), stop=(zc == 63))
                        r.then_inc(om, 1)

        @block.scalar
        def _(scalar):
            for tp in range(2):
                for zc in range(64):
                    i = tp * 64 + zc
                    scalar.wait_ge(zm, i + 1)
                    if i >= 2:
                        scalar.wait_ge(zv, i - 1)
                    scalar.activation(silt[:, (i % 2) * 512:(i % 2) * 512 + 512], pj[:, 512:1024], AF.Silu).then_inc(zs, 1)

        @block.sync
        def _(sync):
            for tp in range(2):
                for t4 in range(4):
                    tg = tp * 4 + t4
                    if tg >= 4:
                        sync.wait_ge(oe, tg + 9)
                    sync.dma_start(out=x1t[:, t4 * D:(t4 + 1) * D],
                                   in_=x1_h.rearrange("(t p) f -> p t f", p=128)[:, tg, :]).then_inc(xls, 16)
                for fs in range(4):
                    for t4 in range(4):
                        tg = tp * 4 + t4
                        idx = tp * 16 + fs * 4 + t4
                        j = (fs * 4 + t4) % 2
                        sync.wait_ge(oe, idx + 1)
                        sync.dma_start(
                            out=s_o.rearrange("(t p) f -> p t f", p=128)[:, tg, fs * 512:(fs + 1) * 512],
                            in_=ssl[:, j * 512:(j + 1) * 512]).then_inc(sso[idx % 2], 16)
            sync.wait_ge(sso[0], 16 * 16)
            sync.wait_ge(sso[1], 16 * 16)
            sync.sem_inc(phase, 1)

        @block.vector
        def _(vector):
            for tp in range(2):
                for zc in range(64):
                    i = tp * 64 + zc
                    vector.wait_ge(zs, i + 1)
                    if tp == 1 and zc == 0:
                        vector.wait_ge(om, 256)
                    vector.tensor_tensor(zbv[:, zc, :], pj[:, 0:512],
                                         silt[:, (i % 2) * 512:(i % 2) * 512 + 512], op=AluOpType.mult).then_inc(zv, 1)
                for fs in range(4):
                    for t4 in range(4):
                        tg = tp * 4 + t4
                        vector.wait_ge(om, tp * 256 + fs * 64 + 64)
                        vector.wait_ge(xls, (tp + 1) * 64)
                        j = (fs * 4 + t4) % 2
                        idx2 = tp * 16 + fs * 4 + t4
                        if idx2 >= 2:
                            vector.wait_ge(sso[idx2 % 2], (idx2 // 2) * 16)
                        xsl_ = xnsl[:, j * 512:(j + 1) * 512]
                        vector.tensor_scalar(xsl_, x1t[:, t4 * D + fs * 512: t4 * D + fs * 512 + 512],
                                             stt8[:, tg:tg + 1], stt8[:, 8 + tg:9 + tg],
                                             op0=AluOpType.subtract, op1=AluOpType.mult)
                        vector.drain()
                        vector.tensor_tensor(ssl[:, j * 512:(j + 1) * 512],
                                             po[:, t4 * 512:(t4 + 1) * 512], xsl_, op=AluOpType.add).then_inc(oe, 1)




def emit_l1_c(nc, block, phase, pgate, ios, prefix, semstack):
    x1_h = ios["x1"]
    stats_h = ios["stats"]
    s_h = ios["s"]
    WT_h = ios["WT"]
    b_h = ios["b"]
    lAT_h = ios["lAT"]
    lBT_h = ios["lBT"]
    gwT_h = ios["gwT"]
    gb_h = ios["gb"]
    ident_h = ios["ident"]
    x2_o = ios["x2_o"]
    with contextlib.ExitStack() as ectx:
        C = Ctx(nc, ectx, prefix, semstack)
        stt8 = C.sb("stt8", [128, 16], F32)
        xt2 = C.sb("xt2", [128, 2 * D], F32)
        xnbf = C.sb("xnbf", [128, D], BF16)
        xnT = C.sb("xnT", [128, NCH * TLOC], BF16)
        ident = C.sb("ident_sb", [128, 128], BF16)
        ones_bf = C.sb("ones_bf", [128, 1], BF16)
        ones_rbf = C.sb("ones_rbf", [1, 128], BF16)
        lATs = C.sb("lATs", [128, NCH * 128], BF16)
        lATv = lATs[:].rearrange("p (c f) -> p c f", f=128)
        lBTs = C.sb("lBTs", [128, D], BF16)
        gws = C.sb("gws", [128, NCH * 8], BF16)
        gwv = gws[:].rearrange("p (c f) -> p c f", f=8)
        gbr = C.sb("gbr", [1, 8], BF16)
        br = C.sb("br", [1, D], BF16)
        WTsl = C.sb("WTsl", [128, 2 * NCH * 512], BF16)
        WTv = WTsl[:].rearrange("p (b c f) -> p b c f", b=2, f=512)
        lg8 = C.sb("lg8", [128, 8 * 8], F32)
        t2_8 = C.sb("t2_8", [128, 8 * 8], F32)
        gx = C.sb("gx", [128, 6 * 8], F32)
        m1 = gx[:, 0:8]; m2 = gx[:, 8:16]; dd = gx[:, 16:24]
        g1 = gx[:, 24:32]; g2 = gx[:, 32:40]; mneg = gx[:, 40:48]
        wgt8 = C.sb("wgt8", [128, 8 * 8], F32)
        w2t = C.sb("w2t", [128, 8], F32)
        xa = C.sb("xa", [128, 128], F32)
        xawbf = C.sb("xawbf", [128, 128], BF16)
        xawT = C.sb("xawT", [128, NTILE * 128], BF16)
        gsl = C.sb("gsl", [128, 2 * 512], BF16)
        x1sl = C.sb("x1sl", [128, 2 * 512], F32)
        s_sl = C.sb("s_sl", [128, 2 * 512], F32)
        x2sl = C.sb("x2sl", [128, 2 * 512], F32)

        pj = C.ps("pj", [128, 1024], F32)
        plg = C.ps("plg", [128, 512], F32)
        ptb = C.ps("ptb", [128, 2048], BF16)

        ld0 = C.sem("ld0")
        xld = [C.sem("xld0"), C.sem("xld1")]
        ms = C.sem("ms")

        gld = C.sem("gld")
        xn_s = C.sem("xn_s")
        tr_s = C.sem("tr_s")
        xe_s = C.sem("xe_s")
        lgm = C.sem("lgm")
        pp = C.sem("pp")
        ppa = C.sem("ppa")
        xtr = C.sem("xtr")

        @block.sync
        def _(sync):
            sync.dma_start(out=stt8[:], in_=stats_h).then_inc(ld0, 16)
            sync.wait_ge(ld0, 16)
            sync.dma_start(out=ident[:], in_=ident_h).then_inc(ld0, 16)
            for t in range(NTILE):
                if t >= 2:
                    sync.wait_ge(xn_s, 2 * t - 3)
                sync.dma_start(out=xt2[:, (t % 2) * D:(t % 2) * D + D],
                               in_=x1_h.rearrange("(t p) f -> p t f", p=128)[:, t, :]).then_inc(xld[t % 2], 16)
        LD0 = 32

        @block.gpsimd
        def _(gpsimd):
            gpsimd.memset(ones_bf[:], 1.0)
            gpsimd.memset(ones_rbf[:], 1.0)
            gpsimd.drain().then_inc(ms, 1)
            for i, (dst, srcap) in enumerate([
                (lATv, lAT_h.rearrange("(c p) f -> p c f", p=128)),
                (lBTs[:], lBT_h),
                (gwv, gwT_h.rearrange("(c p) f -> p c f", p=128)),
                (gbr[:], gb_h),
                (br[:], b_h),
            ]):
                gpsimd.wait_ge(gld, i * 16)
                gpsimd.dma_start(out=dst, in_=srcap).then_inc(gld, 16)

        @block.vector
        def _(vector):
            vector.wait_ge(ld0, LD0)
            vector.wait_ge(gld, 80)
            for t in range(NTILE):
                vector.wait_ge(xld[t % 2], (t // 2 + 1) * 16)
                vector.tensor_scalar(xnbf[:], xt2[:, (t % 2) * D:(t % 2) * D + D],
                                     stt8[:, t:t + 1], stt8[:, 8 + t:9 + t],
                                     op0=AluOpType.subtract, op1=AluOpType.mult).then_inc(xn_s, 1)
                for g in range(2):
                    G = t * 2 + g
                    vector.wait_ge(tr_s, G + 1)
                    vector.tensor_copy(
                        xnT[:].rearrange("p (c tt) -> p c tt", tt=TLOC)[:, g * 8:(g + 1) * 8, t * 128:(t + 1) * 128],
                        ptb[:, (G % 2) * 1024:(G % 2) * 1024 + 1024].rearrange("p (c tt) -> p c tt", tt=128)).then_inc(xe_s, 1)
                vector.wait_ge(lgm, t + 1)
                vector.tensor_copy(lg8[:, t * 8:(t + 1) * 8], plg[:, t * 16:t * 16 + 8])
                vector.tensor_copy(xa[:], pj[:, 0:128])
                vector.drain()
                vector.reduce_max(m1[:, t:t + 1], lg8[:, t * 8:(t + 1) * 8], axis=AX)
                vector.drain()
                vector.tensor_scalar(t2_8[:, t * 8:(t + 1) * 8], lg8[:, t * 8:(t + 1) * 8],
                                     m1[:, t:t + 1], -1e30, op0=AluOpType.is_equal, op1=AluOpType.mult)
                vector.drain()
                vector.tensor_tensor(t2_8[:, t * 8:(t + 1) * 8], t2_8[:, t * 8:(t + 1) * 8],
                                     lg8[:, t * 8:(t + 1) * 8], op=AluOpType.add)
                vector.drain()
                vector.reduce_max(m2[:, t:t + 1], t2_8[:, t * 8:(t + 1) * 8], axis=AX)
                vector.drain()
                vector.tensor_tensor(dd[:, t:t + 1], m1[:, t:t + 1], m2[:, t:t + 1], op=AluOpType.subtract)
                vector.drain().then_inc(pp, 1)
                vector.wait_ge(ppa, t + 1)
                vector.tensor_scalar(g2[:, t:t + 1], g1[:, t:t + 1], -1.0, 1.0,
                                     op0=AluOpType.mult, op1=AluOpType.add)
                vector.drain()
                vector.tensor_scalar(wgt8[:, t * 8:(t + 1) * 8], lg8[:, t * 8:(t + 1) * 8],
                                     m1[:, t:t + 1], g1[:, t:t + 1], op0=AluOpType.is_equal, op1=AluOpType.mult)
                vector.tensor_scalar(w2t[:], t2_8[:, t * 8:(t + 1) * 8],
                                     m2[:, t:t + 1], g2[:, t:t + 1], op0=AluOpType.is_equal, op1=AluOpType.mult)
                vector.drain()
                vector.tensor_tensor(wgt8[:, t * 8:(t + 1) * 8], wgt8[:, t * 8:(t + 1) * 8],
                                     w2t[:], op=AluOpType.add)
                vector.drain()
                vector.tensor_tensor(xawbf[:].rearrange("p (e r) -> p e r", r=16),
                                     xa[:].rearrange("p (e r) -> p e r", r=16),
                                     wgt8[:, t * 8:(t + 1) * 8].rearrange("p (e o) -> p e o", o=1)
                                     .broadcast_to((128, 8, 16)),
                                     op=AluOpType.mult)
                vector.drain().then_inc(xn_s, 1)
                vector.wait_ge(xtr, t + 1)
                vector.tensor_copy(xawT[:, t * 128:(t + 1) * 128], ptb[:, 0:128])
                vector.drain().then_inc(xe_s, 1)

        @block.scalar
        def _(scalar):
            for t in range(NTILE):
                scalar.wait_ge(pp, t + 1)
                scalar.activation(g1[:, t:t + 1], dd[:, t:t + 1], AF.Sigmoid)
                scalar.drain().then_inc(ppa, 1)

        @block.tensor
        def _(tensor):
            tensor.wait_ge(ld0, LD0)
            tensor.wait_ge(gld, 80)
            tensor.wait_ge(ms, 1)
            for t in range(NTILE):
                tensor.wait_ge(xn_s, 2 * t + 1)
                for g in range(2):
                    G = t * 2 + g
                    if G >= 2:
                        tensor.wait_ge(xe_s, 3 * (G // 2) if G % 2 == 0 else 3 * ((G - 2) // 2) + 2)
                    for j in range(8):
                        e = g * 8 + j
                        r = tensor.transpose(ptb[:, (G % 2) * 1024 + j * 128:(G % 2) * 1024 + (j + 1) * 128],
                                             xnbf[:, e * 128:(e + 1) * 128], ident[:])
                    r.then_inc(tr_s, 1)
                tensor.wait_ge(xe_s, 3 * t + 2)
                for e in range(NCH):
                    tensor.matmul(plg[:, t * 16:t * 16 + 8],
                                  xnT[:, e * TLOC + t * 128:e * TLOC + (t + 1) * 128],
                                  gwv[:, e], start=(e == 0), stop=False)
                tensor.matmul(plg[:, t * 16:t * 16 + 8], ones_rbf[:], gbr[:], start=False, stop=True)
                for e in range(NCH):
                    r = tensor.matmul(pj[:, 0:128],
                                      xnT[:, e * TLOC + t * 128:e * TLOC + (t + 1) * 128],
                                      lATv[:, e], start=(e == 0), stop=(e == NCH - 1))
                r.then_inc(lgm, 1)
                tensor.wait_ge(xn_s, 2 * t + 2)
                tensor.transpose(ptb[:, 0:128], xawbf[:], ident[:]).then_inc(xtr, 1)
        XE_ALL = 3 * NTILE

        wl = [C.sem("wl0"), C.sem("wl1")]
        xsl2 = [C.sem("xsl20"), C.sem("xsl21")]
        fm = C.sem("fm")
        fsl = C.sem("fsl")
        cmb = C.sem("cmb")
        sto = [C.sem("sto0"), C.sem("sto1")]

        @block.gpsimd
        def _(gpsimd):
            gpsimd.wait_ge(ms, 1)
            for fs in range(4):
                if fs >= 2:
                    gpsimd.wait_ge(fm, (fs - 1) * 8)
                gpsimd.dma_start(out=WTv[:, fs % 2],
                                 in_=WT_h[:, fs * 512:(fs + 1) * 512].rearrange("(c p) f -> p c f", p=128)
                                 ).then_inc(wl[fs % 2], 16)

        @block.sync
        def _(sync):
            for idx in range(34):
                if idx < 32:
                    fs, t = divmod(idx, 8)
                    if idx >= 2:
                        sync.wait_ge(cmb, idx - 1)
                    sync.dma_start(out=x1sl[:, (idx % 2) * 512:(idx % 2) * 512 + 512],
                                   in_=x1_h.rearrange("(t p) f -> p t f", p=128)[:, t, fs * 512:(fs + 1) * 512]
                                   ).then_inc(xsl2[idx % 2], 16)
                    sync.dma_start(out=s_sl[:, (idx % 2) * 512:(idx % 2) * 512 + 512],
                                   in_=s_h.rearrange("(t p) f -> p t f", p=128)[:, t, fs * 512:(fs + 1) * 512]
                                   ).then_inc(xsl2[idx % 2], 16)
                if idx >= 2:
                    j2 = idx - 2
                    fs2, t2 = divmod(j2, 8)
                    sync.wait_ge(cmb, j2 + 1)
                    sync.dma_start(
                        out=x2_o.rearrange("(t p) f -> p t f", p=128)[:, t2, fs2 * 512:(fs2 + 1) * 512],
                        in_=x2sl[:, (j2 % 2) * 512:(j2 % 2) * 512 + 512]).then_inc(sto[j2 % 2], 16)
            sync.wait_ge(sto[0], 16 * 16)
            sync.wait_ge(sto[1], 16 * 16)
            sync.sem_inc(phase, 1)

        XLD_BASE = [0]

        @block.tensor
        def _(tensor):
            tensor.wait_ge(xe_s, XE_ALL)
            tensor.wait_ge(xtr, NTILE)
            for fs in range(4):
                tensor.wait_ge(wl[fs % 2], (fs // 2 + 1) * 16)
                for t in range(NTILE):
                    idx = fs * 8 + t
                    if idx >= 2:
                        tensor.wait_ge(fsl, idx - 1)
                    bank = pj[:, (idx % 2) * 512:(idx % 2) * 512 + 512]
                    for e in range(NCH):
                        tensor.matmul(bank, xnT[:, e * TLOC + t * 128:e * TLOC + (t + 1) * 128],
                                      WTv[:, fs % 2, e], start=(e == 0), stop=False)
                    tensor.matmul(bank, ones_rbf[:], br[0:1, fs * 512:(fs + 1) * 512], start=False, stop=False)
                    r = tensor.matmul(bank, xawT[:, t * 128:(t + 1) * 128],
                                      lBTs[:, fs * 512:(fs + 1) * 512], start=False, stop=True)
                    r.then_inc(fm, 1)

        @block.scalar
        def _(scalar):
            for fs in range(4):
                for t in range(NTILE):
                    idx = fs * 8 + t
                    scalar.wait_ge(fm, idx + 1)
                    if idx >= 2:
                        scalar.wait_ge(cmb, idx - 1)
                    bank = pj[:, (idx % 2) * 512:(idx % 2) * 512 + 512]
                    scalar.activation(gsl[:, (idx % 2) * 512:(idx % 2) * 512 + 512], bank, AF.Silu).then_inc(fsl, 1)

        @block.vector
        def _(vector):
            for fs in range(4):
                for t in range(NTILE):
                    idx = fs * 8 + t
                    vector.wait_ge(fsl, idx + 1)
                    vector.wait_ge(xsl2[idx % 2], (idx // 2 + 1) * 32)
                    if idx >= 2:
                        vector.wait_ge(sto[idx % 2], (idx // 2) * 16)
                    j = idx % 2
                    vector.tensor_tensor(x2sl[:, j * 512:(j + 1) * 512],
                                         s_sl[:, j * 512:(j + 1) * 512],
                                         gsl[:, j * 512:(j + 1) * 512], op=AluOpType.mult)
                    vector.drain()
                    vector.tensor_tensor(x2sl[:, j * 512:(j + 1) * 512],
                                         x2sl[:, j * 512:(j + 1) * 512],
                                         x1sl[:, j * 512:(j + 1) * 512], op=AluOpType.add).then_inc(cmb, 1)




def _prep_core_inputs_l1(inputs, i):
    rows = _global_rows(i)
    x = np.ascontiguousarray(np.asarray(inputs["L0_attn_x"], dtype=np.float32).reshape(T, D)[rows])
    cosq, sinq = _rope_tables(rows, RS)
    cosk, sink = _rope_tables(rows, 1.0)
    return {
        "x": x,
        "wqT": np.ascontiguousarray(np.asarray(inputs["L0_attn_wq"], np.float32).T),
        "wkT": np.ascontiguousarray(np.asarray(inputs["L0_attn_wk"], np.float32).T),
        "wvT": np.ascontiguousarray(np.asarray(inputs["L0_attn_wv"], np.float32).T),
        "wqkvT": np.ascontiguousarray(np.asarray(inputs["L0_ffn_W_qkv"], np.float32).T),
        "wgT": np.ascontiguousarray(np.asarray(inputs["L0_ffn_W_gate"], np.float32).T),
        "cosq": cosq, "sinq": sinq, "cosk": cosk, "sink": sink,
        "ident": np.eye(128, dtype=BF),
    }


def run_l0_pre(inputs):
    nc1 = build_l0_pre()
    in_maps = [_prep_core_inputs_l1(inputs, i) for i in range(NCORE)]
    return run_bass_kernel_spmd(nc1, in_maps, core_ids=list(range(NCORE)))


def _prep_l2(inputs, r1):
    f32 = np.float32
    kTf = np.empty((D, T), BF)
    vf = np.empty((T, NH * 129), BF)
    kshTf = np.empty((128, T), BF)
    vshf = np.empty((T, 132), BF)
    k2ma = np.empty((NH, NCORE), f32)
    k2shma = np.empty((1, NCORE), f32)
    for c in range(NCORE):
        rows = _global_rows(c)
        kTf[:, rows] = np.asarray(r1[c]["kT_o"])
        vf[rows, :] = np.asarray(r1[c]["v_o"])
        kshTf[:, rows] = np.asarray(r1[c]["kshT_o"])
        vshf[rows, :] = np.asarray(r1[c]["vsh_o"])
        k2ma[:, c] = np.asarray(r1[c]["k2m_o"])[:, 0]
        k2shma[0, c] = np.asarray(r1[c]["k2shm_o"])[0, 0]
    woT = np.ascontiguousarray(np.asarray(inputs["L0_attn_wo"], f32).T)
    maskT = np.triu(np.ones((128, 128), f32)).astype(BF)
    ident = np.eye(128, dtype=BF)
    x = np.asarray(inputs["L0_attn_x"], f32).reshape(T, D)
    maps = []
    for c in range(NCORE):
        rows = _global_rows(c)
        bg = _core_blocks(c)
        pm = np.zeros((2, NH, 2, KSLOT), f32)
        pm[:, :, :, 0] = 1.0
        for g in range(2):
            for kb in range(16):
                if kb < bg[g]:
                    pm[:, :, g, 1 + kb] = 1.0
        maps.append({
            "x": np.ascontiguousarray(x[rows]),
            "qT": np.asarray(r1[c]["qT_o"]),
            "q2T": np.asarray(r1[c]["q2T_o"]),
            "kTl": np.asarray(r1[c]["kT_o"]),
            "vl": np.asarray(r1[c]["v_o"]),
            "kshTl": np.asarray(r1[c]["kshT_o"]),
            "vshl": np.asarray(r1[c]["vsh_o"]),
            "kTf": kTf, "vf": vf, "kshTf": kshTf, "vshf": vshf,
            "qn2": np.asarray(r1[c]["qn2_o"]),
            "q2n2": np.asarray(r1[c]["q2n2_o"]),
            "k2ma": k2ma, "k2shma": k2shma,
            "pm": np.ascontiguousarray(pm.reshape(1, -1)),
            "woT": woT, "maskT": maskT, "ident": ident,
        })
    return maps


def _prep_l4(inputs, r3):
    f32 = np.float32
    tots = np.concatenate([np.asarray(r3[c]["tot_o"]) for c in range(NCORE)], axis=1)
    tots = np.ascontiguousarray(tots)
    wswT = np.ascontiguousarray(np.asarray(inputs["L1_attn_w_swiglu"], f32).T)
    woutT = np.ascontiguousarray(np.asarray(inputs["L1_attn_w_out"], f32).T)
    maps = []
    for c in range(NCORE):
        bg = _core_blocks(c)
        selm = np.zeros((2, NBLK), f32)
        for b in range(2):
            selm[b, :bg[b]] = 1.0
        selm = np.broadcast_to(selm.reshape(1, 32), (128, 32)).copy()
        maps.append(dict(tots=tots, selm=selm, wswT=wswT, woutT=woutT))
    return maps


def _prep_l5(inputs):
    f32 = np.float32
    lA = np.asarray(inputs["L1_ffn_lora_A"], f32).reshape(128, D)
    lB = np.asarray(inputs["L1_ffn_lora_B"], f32)
    return {
        "WT": np.ascontiguousarray(np.asarray(inputs["L1_ffn_W"], f32).T),
        "b": np.asarray(inputs["L1_ffn_b"], f32).reshape(1, D),
        "lAT": np.ascontiguousarray(lA.T),
        "lBT": np.ascontiguousarray(np.transpose(lB, (0, 2, 1)).reshape(128, D)),
        "gwT": np.ascontiguousarray(np.asarray(inputs["L1_ffn_gate_w"], f32).T),
        "gb": np.asarray(inputs["L1_ffn_gate_b"], f32).reshape(1, 8),
        "ident": np.eye(128, dtype=BF),
    }


_old_kernel_removed = None



SIM_LOCAL_SHARED = False


def build_fused():
    nc = bass.Bass("TRN2", target_bir_lowering=False, debug=False)
    E = lambda n, s, d: nc.dram_tensor(n, s, d, kind="ExternalInput").ap()
    I = lambda n, s, d: nc.dram_tensor(n, s, d, kind="Internal").ap()
    aspace = "Local" if SIM_LOCAL_SHARED else "Shared"
    S = lambda n, s, d: nc.dram_tensor(n, s, d, kind="Internal", addr_space=aspace).ap()

    x = E("x", [TLOC, D], F32)
    wqT = E("wqT", [D, D], F32); wkT = E("wkT", [D, D], F32); wvT = E("wvT", [D, D], F32)
    wqkvT0 = E("wqkvT0", [D, 3 * D], F32); wgT = E("wgT", [D, 16], F32)
    cosq = E("cosq", [128, TLOC], F32); sinq = E("sinq", [128, TLOC], F32)
    cosk = E("cosk", [128, TLOC], F32); sink = E("sink", [128, TLOC], F32)
    ident = E("ident", [128, 128], BF16)
    pm = E("pm", [1, 2 * NH * 2 * KSLOT], F32)
    woT = E("woT", [D, D], F32)
    maskT = E("maskT", [128, 128], BF16)
    wqkvT1 = E("wqkvT1", [D, 3 * D], F32)
    selm = E("selm", [128, 32], F32)
    wswT = E("wswT", [D, 16384], F32)
    woutT = E("woutT", [8192, D], F32)
    WT = E("WT", [D, D], F32); bb = E("b", [1, D], F32)
    lAT = E("lAT", [D, 128], F32); lBT = E("lBT", [128, D], F32)
    gwT = E("gwT", [D, 8], F32); gb = E("gb", [1, 8], F32)
    x2_o = nc.dram_tensor("x2_o", [TLOC, D], F32, kind="ExternalOutput").ap()

    qT_d = I("qT_d", [D, TLOC], BF16); kT_d = I("kT_d", [D, TLOC], BF16)
    v_d = I("v_d", [TLOC, NH * 129], BF16); q2T_d = I("q2T_d", [D, TLOC], BF16)
    kshT_d = I("kshT_d", [128, TLOC], BF16); vsh_d = I("vsh_d", [TLOC, 132], BF16)
    qn2_d = I("qn2_d", [NH, TLOC], F32); q2n2_d = I("q2n2_d", [NH, TLOC], F32)
    kn_d = I("kn_d", [17, 1], F32)
    knorm_d = I("knorm_d", [NH, TLOC], F32)
    nb_d = I("nb_d", [NH, 4], F32)
    x1_d = I("x1_d", [TLOC, D], F32)
    stats_d = I("stats_d", [128, 16], F32)
    sig_d = I("sig_d", [D, TLOC], BF16)
    wcum_d = I("wcum_d", [D, TLOC], F32); kvcum_d = I("kvcum_d", [D, TLOC], F32)
    tot_d = I("tot_d", [128, 64], F32)
    s_d = I("s_d", [TLOC, D], F32)
    wqkvT1b = I("wqkvT1b", [D, 3 * D], BF16)
    wswb = I("wswb", [D, 16384], BF16)
    woutb = I("woutb", [8192, D], BF16)
    WTb = I("WTb", [D, D], BF16)

    kTg = S("kTg", [NCORE * D, TLOC], BF16)
    vg = S("vg", [NCORE * TLOC, NH * 129], BF16)
    kshg = S("kshg", [NCORE * 128, TLOC], BF16)
    vshg = S("vshg", [NCORE * TLOC, 132], BF16)
    kng = S("kng", [NCORE * 17, 1], F32)
    totg = S("totg", [NCORE * 128, 64], F32)

    with contextlib.ExitStack() as outer:
        phase = outer.enter_context(nc.semaphore("phase"))
        ccA = outer.enter_context(nc.semaphore("ccA"))
        ccB = outer.enter_context(nc.semaphore("ccB"))
        bar = outer.enter_context(nc.semaphore("bar"))
        nbar = [0]
        block = outer.enter_context(nc.Block())
        pool = SemPool(outer)

        def barrier_in(pk):
            nbar[0] += 4

            @block.sync
            def _(sync):
                sync.wait_ge(phase, pk)
                sync.sem_inc(bar, 1)

            @block.tensor
            def _(tensor):
                tensor.wait_ge(phase, pk)
                tensor.sem_inc(bar, 1)

            @block.vector
            def _(vector):
                vector.wait_ge(phase, pk)
                vector.sem_inc(bar, 1)

            @block.scalar
            def _(scalar):
                scalar.wait_ge(phase, pk)
                scalar.sem_inc(bar, 1)

        def gate_all(val):
            @block.sync
            def _(sync):
                sync.wait_ge(phase, val)

            @block.gpsimd
            def _(gpsimd):
                gpsimd.wait_ge(phase, val)

            @block.tensor
            def _(tensor):
                tensor.wait_ge(phase, val)

            @block.vector
            def _(vector):
                vector.wait_ge(phase, val)

            @block.scalar
            def _(scalar):
                scalar.wait_ge(phase, val)

        ios1 = dict(x=x, wqT=wqT, wkT=wkT, wvT=wvT, wqkvT=wqkvT0, wgT=wgT,
                    cosq=cosq, sinq=sinq, cosk=cosk, sink=sink, ident=ident,
                    qT_o=qT_d, kT_o=kT_d, v_o=v_d, q2T_o=q2T_d, kshT_o=kshT_d,
                    vsh_o=vsh_d, qn2_o=qn2_d, q2n2_o=q2n2_d,
                    k2m_o=kn_d[0:16, :], k2shm_o=kn_d[16:17, :], knorm_d=knorm_d)
        pool.cur = []
        emit_l0_pre(nc, block, phase, 0, ios1, "p1_", pool)
        sems1 = pool.cur

        barrier_in(1)

        @block.gpsimd
        def _(gpsimd):
            gpsimd.wait_ge(phase, 1)
            gpsimd.wait_ge(bar, nbar[0])
            nc.clear_and_free_semaphores(sems1)
            for i, (s_ap, d_ap) in enumerate([(kT_d, kTg), (v_d, vg), (kshT_d, kshg),
                                              (vsh_d, vshg), (kn_d, kng)]):
                gpsimd.collective_compute("AllGather", AluOpType.bypass,
                                          replica_groups=[list(range(NCORE))],
                                          ins=[s_ap.opt()], outs=[d_ap.opt()]).then_inc(ccA, 1)
            gpsimd.wait_ge(ccA, 5)
            gpsimd.sem_inc(phase, 1)

        gate_all(2)
        ios2 = dict(x=x, qT=qT_d, q2T=q2T_d, kTl=kT_d, vl=v_d, kshTl=kshT_d,
                    vshl=vsh_d, kTg=kTg, vg=vg, kshg=kshg, vshg=vshg,
                    qn2=qn2_d, q2n2=q2n2_d, kng=kng, pm=pm, woT=woT,
                    maskT=maskT, ident=ident, x1_o=x1_d, nb_d=nb_d,
                    conv=[(wqkvT1, wqkvT1b), (wswT, wswb), (woutT, woutb), (WT, WTb)])
        pool.cur = []
        emit_l0_attn(nc, block, phase, 2, ios2, "p2_", pool)
        sems2 = pool.cur

        barrier_in(3)

        @block.gpsimd
        def _(gpsimd):
            gpsimd.wait_ge(phase, 3)
            gpsimd.wait_ge(bar, nbar[0])
            nc.clear_and_free_semaphores(sems2)
            gpsimd.sem_inc(phase, 1)

        gate_all(4)
        ios3 = dict(x1=x1_d, wqkvT=wqkvT1b, ident=ident, stats_o=stats_d,
                    sig_o=sig_d, wcum_o=wcum_d, kvcum_o=kvcum_d, tot_o=tot_d)
        pool.cur = []
        emit_l1_a(nc, block, phase, 4, ios3, "p3_", pool)
        sems3 = pool.cur

        barrier_in(5)

        @block.gpsimd
        def _(gpsimd):
            gpsimd.wait_ge(phase, 5)
            gpsimd.wait_ge(bar, nbar[0])
            nc.clear_and_free_semaphores(sems3)
            gpsimd.collective_compute("AllGather", AluOpType.bypass,
                                      replica_groups=[list(range(NCORE))],
                                      ins=[tot_d.opt()], outs=[totg.opt()]).then_inc(ccB, 1)
            gpsimd.wait_ge(ccB, 1)
            gpsimd.sem_inc(phase, 1)

        gate_all(6)
        ios4 = dict(x1=x1_d, stats=stats_d, sig=sig_d, wcum=wcum_d, kvcum=kvcum_d,
                    tots=totg, selm=selm, wswT=wswb, woutT=woutb, s_o=s_d)
        pool.cur = []
        emit_l1_b(nc, block, phase, 6, ios4, "p4_", pool)
        sems4 = pool.cur

        barrier_in(7)

        @block.gpsimd
        def _(gpsimd):
            gpsimd.wait_ge(phase, 7)
            gpsimd.wait_ge(bar, nbar[0])
            nc.clear_and_free_semaphores(sems4)
            gpsimd.sem_inc(phase, 1)

        gate_all(8)
        ios5 = dict(x1=x1_d, stats=stats_d, s=s_d, WT=WTb, b=bb, lAT=lAT, lBT=lBT,
                    gwT=gwT, gb=gb, ident=ident, x2_o=x2_o)
        pool.cur = []
        emit_l1_c(nc, block, phase, 8, ios5, "p5_", pool)

    return nc


def _prep_fused_maps(inputs):
    base = {
        "wqT": np.ascontiguousarray(np.asarray(inputs["L0_attn_wq"], np.float32).T),
        "wkT": np.ascontiguousarray(np.asarray(inputs["L0_attn_wk"], np.float32).T),
        "wvT": np.ascontiguousarray(np.asarray(inputs["L0_attn_wv"], np.float32).T),
        "wqkvT0": np.ascontiguousarray(np.asarray(inputs["L0_ffn_W_qkv"], np.float32).T),
        "wgT": np.ascontiguousarray(np.asarray(inputs["L0_ffn_W_gate"], np.float32).T),
        "woT": np.ascontiguousarray(np.asarray(inputs["L0_attn_wo"], np.float32).T),
        "wqkvT1": np.ascontiguousarray(np.asarray(inputs["L1_attn_w_qkv"], np.float32).T),
        "wswT": np.ascontiguousarray(np.asarray(inputs["L1_attn_w_swiglu"], np.float32).T),
        "woutT": np.ascontiguousarray(np.asarray(inputs["L1_attn_w_out"], np.float32).T),
        "maskT": np.triu(np.ones((128, 128), np.float32)).astype(BF),
        "ident": np.eye(128, dtype=BF),
    }
    base.update(_prep_l5(inputs))
    base.pop("stats", None)
    x = np.asarray(inputs["L0_attn_x"], np.float32).reshape(T, D)
    maps = []
    for c in range(NCORE):
        rows = _global_rows(c)
        bg = _core_blocks(c)
        cosq, sinq = _rope_tables(rows, RS)
        cosk, sink = _rope_tables(rows, 1.0)
        pmv = np.zeros((2, NH, 2, KSLOT), np.float32)
        pmv[:, :, :, 0] = 1.0
        for g in range(2):
            for kb in range(16):
                if kb < bg[g]:
                    pmv[:, :, g, 1 + kb] = 1.0
        selmv = np.zeros((2, NBLK), np.float32)
        for b2 in range(2):
            selmv[b2, :bg[b2]] = 1.0
        selmv = np.broadcast_to(selmv.reshape(1, 32), (128, 32)).copy()
        m = dict(base)
        m.update(x=np.ascontiguousarray(x[rows]), cosq=cosq, sinq=sinq, cosk=cosk,
                 sink=sink, pm=np.ascontiguousarray(pmv.reshape(1, -1)), selm=selmv)
        maps.append(m)
    return maps


def kernel(**inputs):
    nc = build_fused()
    maps = _prep_fused_maps(inputs)
    res = run_bass_kernel_spmd(nc, maps, core_ids=list(range(NCORE)))
    out = np.empty((T, D), np.float32)
    for c in range(NCORE):
        out[_global_rows(c)] = np.asarray(res.results[c]["x2_o"])
    return out.reshape(1, T, D)


# version 08
# speedup vs baseline: 1.0508x; 1.0508x over previous
import contextlib
import numpy as np
import ml_dtypes
import concourse.bass as bass
import concourse.mybir as mybir
from concourse.bass_utils import run_bass_kernel_spmd
from concourse.alu_op_type import AluOpType

F32 = mybir.dt.float32
BF16 = mybir.dt.bfloat16
AF = mybir.ActivationFunctionType
AX = mybir.AxisListType.X
BF = ml_dtypes.bfloat16

NCORE = 8
T = 8192
D = 2048
NBLK = 16
BLK = T // NBLK
TLOC = 2 * BLK
NTILE = TLOC // 128
NCH = D // 128
NH = 16
RS = float(1.0 / np.sqrt(128.0))


def _core_blocks(i):
    return (i, NBLK - 1 - i)


def _global_rows(i):
    bA, bB = _core_blocks(i)
    return np.r_[bA * BLK:(bA + 1) * BLK, bB * BLK:(bB + 1) * BLK]


def _rope_tables(pos, scale):
    inv = (1.0 / (1000000.0 ** (np.arange(0, 128, 2, dtype=np.float32) / np.float32(128)))).astype(np.float32)
    f = pos.astype(np.float32)[:, None] * inv[None, :]
    c = np.cos(f).astype(np.float32) * np.float32(scale)
    s = np.sin(f).astype(np.float32) * np.float32(scale)
    cosT = np.ascontiguousarray(np.concatenate([c, c], axis=1).T)
    sinS = np.ascontiguousarray(np.concatenate([-s, s], axis=1).T)
    return cosT, sinS


class SemPool:
    def __init__(self, ectx):
        self.ectx = ectx
        self.cur = []

    def enter_context(self, cm):
        h = self.ectx.enter_context(cm)
        self.cur.append(h)
        return h


class Ctx:
    def __init__(self, nc, ectx, prefix="", semstack=None):
        self.nc = nc
        self.ectx = ectx
        self.prefix = prefix
        self.semstack = semstack if semstack is not None else ectx
        self._ns = 0

    def sb(self, name, shape, dtype):
        return self.ectx.enter_context(self.nc.sbuf_tensor(self.prefix + name, shape, dtype))

    def ps(self, name, shape, dtype):
        return self.ectx.enter_context(self.nc.psum_tensor(self.prefix + name, shape, dtype))

    def sem(self, name=None):
        if name is None:
            self._ns += 1
            name = f"s{self._ns}"
        return self.semstack.enter_context(self.nc.semaphore(self.prefix + name))



def emit_l0_pre(nc, block, phase, pgate, ios, prefix, semstack):
    x_h = ios["x"]
    wqT_h = ios["wqT"]
    wkT_h = ios["wkT"]
    wvT_h = ios["wvT"]
    wqkvT_h = ios["wqkvT"]
    wgT_h = ios["wgT"]
    cosq_h = ios["cosq"]
    sinq_h = ios["sinq"]
    cosk_h = ios["cosk"]
    sink_h = ios["sink"]
    ident_h = ios["ident"]
    qT_o = ios["qT_o"]
    kT_o = ios["kT_o"]
    v_o = ios["v_o"]
    q2T_o = ios["q2T_o"]
    kshT_o = ios["kshT_o"]
    vsh_o = ios["vsh_o"]
    qn2_o = ios["qn2_o"]
    q2n2_o = ios["q2n2_o"]
    k2m_o = ios["k2m_o"]
    k2shm_o = ios["k2shm_o"]
    knorm_d = ios["knorm_d"]
    with contextlib.ExitStack() as ectx:
        C = Ctx(nc, ectx, prefix, semstack)
        xt2 = C.sb("xt2", [128, 2 * D], F32)
        sq_s = C.sb("sq_s", [128, D], F32)
        sqbf = C.sb("sqbf", [128, TLOC], BF16)
        st8 = C.sb("st8", [128, 64], F32)
        sx8 = st8[:, 0:8]; sqs8 = st8[:, 8:16]; m8 = st8[:, 16:24]; e28 = st8[:, 24:32]
        msq8 = st8[:, 32:40]; var8 = st8[:, 40:48]; lnv8 = st8[:, 48:56]; rstd8 = st8[:, 56:64]
        eps1 = C.sb("eps1", [128, 1], F32)
        xnbf = C.sb("xnbf", [128, D], BF16)
        xnT = C.sb("xnT", [128, NCH * TLOC], BF16)
        ident = C.sb("ident_sb", [128, 128], BF16)
        ones_bf = C.sb("ones_bf", [128, 1], BF16)
        cosq = C.sb("cosq_sb", [128, TLOC], F32)
        sinq = C.sb("sinq_sb", [128, TLOC], F32)
        cosk = C.sb("cosk_sb", [128, TLOC], F32)
        sink = C.sb("sink_sb", [128, TLOC], F32)
        ropetmp = C.sb("ropetmp", [128, TLOC], F32)
        stage = C.sb("stage", [128, 2 * TLOC], BF16)
        vsb = C.sb("vsb", [128, NTILE * NH * 129], BF16)
        vsbv = vsb[:].rearrange("p (t h c) -> p t h c", h=NH, c=129)
        nrow = C.sb("nrow", [1, 2 * TLOC], F32)
        k2rows = C.sb("k2rows", [NH, TLOC], F32)
        k2m = C.sb("k2m", [NH, 1], F32)
        kshrow = C.sb("kshrow", [1, TLOC], F32)
        k2shm = C.sb("k2shm", [1, 1], F32)
        wslab = C.sb("wslab", [128, 2 * NCH * 512], BF16)
        wsv = wslab[:].rearrange("p (b c f) -> p b c f", b=2, f=512)
        wsv128 = wslab[:].rearrange("p (b c f) -> p b c f", b=8, f=128)
        wgslab = C.sb("wgslab", [128, NCH * 16], BF16)
        wgv = wgslab[:].rearrange("p (c f) -> p c f", f=16)
        logit8 = C.sb("logit8", [128, 128], F32)
        gx8 = C.sb("gx8", [128, 24], F32)
        mx8 = gx8[:, 0:8]; ngm8 = gx8[:, 8:16]; top8 = gx8[:, 16:24]
        exps = C.sb("exps", [128, 128], F32)
        ind8 = C.sb("ind8", [128, 128], F32)
        svec8 = C.sb("svec8", [128, 128], F32)
        kfsb = C.sb("kfsb", [128, 512], F32)
        seltmp = C.sb("seltmp", [128, 512], F32)
        ksh8 = C.sb("ksh8", [128, 8 * 128], F32)
        vsh8 = C.sb("vsh8", [128, 8 * 128], F32)
        kshbf = C.sb("kshbf", [128, 128], BF16)
        kshT8 = C.sb("kshT8", [128, TLOC], BF16)
        vshst = C.sb("vshst", [128, 132], BF16)

        pj = C.ps("pj", [128, 2048], F32)
        ptb = C.ps("ptb", [128, 2048], BF16)
        nrm = C.ps("nrm", [128, 1024], F32)
        plg = nrm

        tld = C.sem("tld")
        xldA = [C.sem("xldA0"), C.sem("xldA1")]
        xldB = [C.sem("xldB0"), C.sem("xldB1")]
        pav = C.sem("pav")
        pas = C.sem("pas")
        stm = C.sem("stm")
        n_st = [0]

        def St(r):
            r.then_inc(stm, 16)
            n_st[0] += 16


        @block.sync
        def _(sync):
            for t in range(NTILE):
                if t >= 2:
                    sync.wait_ge(pav, t - 1)
                    sync.wait_ge(pas, t - 1)
                sync.dma_start(out=xt2[:, (t % 2) * D:(t % 2) * D + D],
                               in_=x_h.rearrange("(t p) f -> p t f", p=128)[:, t, :]).then_inc(xldA[t % 2], 16)
            sync.dma_start(out=cosq[:], in_=cosq_h).then_inc(tld, 16)
            sync.wait_ge(tld, 16)
            sync.dma_start(out=sinq[:], in_=sinq_h).then_inc(tld, 16)
            sync.wait_ge(tld, 32)
            sync.dma_start(out=cosk[:], in_=cosk_h).then_inc(tld, 16)
            sync.wait_ge(tld, 48)
            sync.dma_start(out=sink[:], in_=sink_h).then_inc(tld, 16)
            sync.wait_ge(tld, 64)
            sync.dma_start(out=ident[:], in_=ident_h).then_inc(tld, 16)

        ms = C.sem("ms")

        @block.gpsimd
        def _(gpsimd):
            gpsimd.memset(eps1[:], 1e-5)
            gpsimd.memset(ones_bf[:], 1.0)
            gpsimd.memset(vsb[:].rearrange("p (a c) -> p a c", c=129)[:, :, 128:129], 1.0)
            gpsimd.memset(vshst[:, 128:129], 1.0)
            gpsimd.memset(vshst[:, 129:132], 0.0)
            gpsimd.drain().then_inc(ms, 1)

        @block.vector
        def _(vector):
            for t in range(NTILE):
                vector.wait_ge(xldA[t % 2], (t // 2 + 1) * 16)
                vector.reduce_sum(sx8[:, t:t + 1], xt2[:, (t % 2) * D:(t % 2) * D + D], axis=AX)
                vector.drain().then_inc(pav, 1)

        @block.scalar
        def _(scalar):
            for t in range(NTILE):
                scalar.wait_ge(xldA[t % 2], (t // 2 + 1) * 16)
                scalar.activation(sq_s[:], xt2[:, (t % 2) * D:(t % 2) * D + D], AF.Square,
                                  accum_out=sqs8[:, t:t + 1])
                scalar.drain().then_inc(pas, 1)

        xn_s = C.sem("xn_s")

        @block.sync
        def _(sync):
            sync.wait_ge(pav, NTILE)
            sync.wait_ge(pas, NTILE)
            for t in range(NTILE):
                if t >= 2:
                    sync.wait_ge(xn_s, t - 1)
                sync.dma_start(out=xt2[:, (t % 2) * D:(t % 2) * D + D],
                               in_=x_h.rearrange("(t p) f -> p t f", p=128)[:, t, :]).then_inc(xldB[t % 2], 16)

        pb = C.sem("pb")

        @block.vector
        def _(vector):
            vector.wait_ge(pav, NTILE)
            vector.wait_ge(pas, NTILE)
            c = 1.0 / D
            vector.tensor_scalar_mul(m8, sx8, c)
            vector.tensor_scalar_mul(e28, sqs8, c)
            vector.drain()
            vector.tensor_tensor(msq8, m8, m8, op=AluOpType.mult)
            vector.drain()
            vector.tensor_tensor(var8, e28, msq8, op=AluOpType.subtract)
            vector.drain().then_inc(pb, 1)

        @block.scalar
        def _(scalar):
            scalar.wait_ge(pb, 1)
            scalar.wait_ge(ms, 1)
            scalar.activation(lnv8, var8, AF.Ln, bias=eps1[:])
            scalar.drain()
            scalar.activation(rstd8, lnv8, AF.Exp, scale=-0.5)
            scalar.drain().then_inc(pb, 1)
        PB_DONE = 2

        tr_s = C.sem("tr_s")
        xe_s = C.sem("xe_s")

        @block.vector
        def _(vector):
            vector.wait_ge(pb, PB_DONE)
            for t in range(NTILE):
                vector.wait_ge(xldB[t % 2], (t // 2 + 1) * 16)
                vector.tensor_scalar(xnbf[:], xt2[:, (t % 2) * D:(t % 2) * D + D],
                                     m8[:, t:t + 1], rstd8[:, t:t + 1],
                                     op0=AluOpType.subtract, op1=AluOpType.mult).then_inc(xn_s, 1)
                for g in range(2):
                    G = t * 2 + g
                    vector.wait_ge(tr_s, G + 1)
                    vector.tensor_copy(
                        xnT[:].rearrange("p (c tt) -> p c tt", tt=TLOC)[:, g * 8:(g + 1) * 8, t * 128:(t + 1) * 128],
                        ptb[:, (G % 2) * 1024:(G % 2) * 1024 + 1024].rearrange("p (c tt) -> p c tt", tt=128)).then_inc(xe_s, 1)

        @block.tensor
        def _(tensor):
            tensor.wait_ge(tld, 80)
            for t in range(NTILE):
                tensor.wait_ge(xn_s, t + 1)
                for g in range(2):
                    G = t * 2 + g
                    if G >= 2:
                        tensor.wait_ge(xe_s, G - 1)
                    for j in range(8):
                        e = g * 8 + j
                        r = tensor.transpose(ptb[:, (G % 2) * 1024 + j * 128:(G % 2) * 1024 + (j + 1) * 128],
                                             xnbf[:, e * 128:(e + 1) * 128], ident[:])
                    r.then_inc(tr_s, 1)
        XE_ALL = 16

        vw_s = [C.sem("vw_s0"), C.sem("vw_s1")]
        vm_s = C.sem("vm_s")
        ve_s = C.sem("ve_s")

        @block.gpsimd
        def _(gpsimd):
            gpsimd.wait_ge(ms, 1)
            for fs in range(4):
                if fs >= 2:
                    gpsimd.wait_ge(vm_s, (fs - 1) * 8)
                gpsimd.dma_start(out=wsv[:, fs % 2],
                                 in_=wvT_h[:, fs * 512:(fs + 1) * 512].rearrange("(c p) f -> p c f", p=128)
                                 ).then_inc(vw_s[fs % 2], 16)

        @block.tensor
        def _(tensor):
            tensor.wait_ge(xe_s, XE_ALL)
            for fs in range(4):
                tensor.wait_ge(vw_s[fs % 2], (fs // 2 + 1) * 16)
                for t in range(NTILE):
                    it = fs * 8 + t
                    if it >= 4:
                        tensor.wait_ge(ve_s, it - 3)
                    bank = pj[:, (it % 4) * 512:(it % 4) * 512 + 512]
                    for e in range(NCH):
                        r = tensor.matmul(bank,
                                          xnT[:, e * TLOC + t * 128: e * TLOC + (t + 1) * 128],
                                          wsv[:, fs % 2, e],
                                          start=(e == 0), stop=(e == NCH - 1))
                    r.then_inc(vm_s, 1)

        @block.scalar
        def _(scalar):
            for fs in range(4):
                for t in range(NTILE):
                    it = fs * 8 + t
                    scalar.wait_ge(vm_s, it + 1)
                    bank = pj[:, (it % 4) * 512:(it % 4) * 512 + 512]
                    for hh in range(4):
                        r = scalar.copy(vsbv[:, t, 4 * fs + hh, 0:128], bank[:, hh * 128:(hh + 1) * 128])
                    r.then_inc(ve_s, 1)
        VE_ALL = 32

        @block.gpsimd
        def _(gpsimd):
            gpsimd.wait_ge(ve_s, VE_ALL)
            St(gpsimd.dma_start(out=v_o.rearrange("(t p) c -> p t c", p=128),
                                in_=vsb[:].rearrange("p (t c) -> p t c", c=NH * 129)))

        def fstream(tag, w_ap, out_ap, rope, cos_sb, sin_sb, scale, nout_ap, first_wait, prev_so, prev_no=None, dve_tld=False):
            lw = [C.sem(f"{tag}lw0"), C.sem(f"{tag}lw1")]
            pm = C.sem(f"{tag}pm")
            se = C.sem(f"{tag}se")
            qs = C.sem(f"{tag}qs")
            nm = C.sem(f"{tag}nm")
            nr = C.sem(f"{tag}nr")
            so = [C.sem(f"{tag}so0"), C.sem(f"{tag}so1")]
            no = [C.sem(f"{tag}no0"), C.sem(f"{tag}no1")]

            @block.gpsimd
            def _(gpsimd):
                gpsimd.wait_ge(ms, 1)
                for h in range(NH + 1):
                    if h < NH:
                        if h >= 2:
                            gpsimd.wait_ge(pm, h - 1)
                        gpsimd.dma_start(out=wsv128[:, h % 2],
                                         in_=w_ap[:, h * 128:(h + 1) * 128].rearrange("(c p) f -> p c f", p=128)
                                         ).then_inc(lw[h % 2], 16)
                    if h >= 1:
                        j = h - 1
                        gpsimd.wait_ge(qs, j + 1)
                        gpsimd.dma_start(out=out_ap[j * 128:(j + 1) * 128, :],
                                         in_=stage[:, (j % 2) * TLOC:(j % 2) * TLOC + TLOC]).then_inc(so[j % 2], 16)
                        gpsimd.wait_ge(nr, j + 1)
                        gpsimd.dma_start(out=nout_ap[j:j + 1, :],
                                         in_=nrow[0:1, (j % 2) * TLOC:(j % 2) * TLOC + TLOC]).then_inc(no[j % 2], 16)

            @block.tensor
            def _(tensor):
                tensor.wait_ge(*first_wait)
                for h in range(NH):
                    tensor.wait_ge(lw[h % 2], (h // 2 + 1) * 16)
                    if h >= 2:
                        tensor.wait_ge(se, h - 1)
                    for half in range(2):
                        bank = pj[:, (h % 2) * 1024 + half * 512:(h % 2) * 1024 + half * 512 + 512]
                        for e in range(NCH):
                            r = tensor.matmul(bank,
                                              wsv128[:, h % 2, e],
                                              xnT[:, e * TLOC + half * 512: e * TLOC + (half + 1) * 512],
                                              start=(e == 0), stop=(e == NCH - 1))
                    r.then_inc(pm, 1)
                    if h >= 1:
                        tensor.wait_ge(qs, h)
                        if h >= 2:
                            tensor.wait_ge(nr, h - 1)
                        for half in range(2):
                            r = tensor.matmul(nrm[0:1, half * 512:half * 512 + 512],
                                              ones_bf[:], sqbf[:, half * 512:(half + 1) * 512],
                                              start=True, stop=True)
                        r.then_inc(nm, 1)
                tensor.wait_ge(qs, NH)
                tensor.wait_ge(nr, NH - 1)
                for half in range(2):
                    r = tensor.matmul(nrm[0:1, half * 512:half * 512 + 512],
                                      ones_bf[:], sqbf[:, half * 512:(half + 1) * 512],
                                      start=True, stop=True)
                r.then_inc(nm, 1)

            @block.vector
            def _(vector):
                if dve_tld:
                    vector.wait_ge(tld, 80)
                for h in range(NH):
                    vector.wait_ge(pm, h + 1)
                    if h < 2 and prev_so is not None:
                        vector.wait_ge(prev_so[0], prev_so[2])
                        vector.wait_ge(prev_so[1], prev_so[2])
                    if h >= 2:
                        vector.wait_ge(so[h % 2], (h // 2) * 16)
                    stg = stage[:, (h % 2) * TLOC:(h % 2) * TLOC + TLOC]
                    pjh = pj[:, (h % 2) * 1024:(h % 2) * 1024 + 1024]
                    if rope:
                        vector.tensor_tensor(ropetmp[0:64, :], pjh[64:128, :], sin_sb[0:64, :], op=AluOpType.mult)
                        vector.tensor_tensor(ropetmp[64:128, :], pjh[0:64, :], sin_sb[64:128, :], op=AluOpType.mult)
                        vector.tensor_tensor(stg, pjh, cos_sb[:], op=AluOpType.mult)
                        vector.drain()
                        vector.tensor_tensor(stg, stg, ropetmp[:], op=AluOpType.add).then_inc(se, 1)
                    else:
                        vector.tensor_scalar_mul(stg, pjh, scale).then_inc(se, 1)
                    vector.wait_ge(nm, h + 1)
                    if h < 2 and prev_no is not None:
                        vector.wait_ge(prev_no[0], prev_no[2])
                        vector.wait_ge(prev_no[1], prev_no[2])
                    if h >= 2:
                        vector.wait_ge(no[h % 2], (h // 2) * 16)
                    vector.tensor_copy(nrow[0:1, (h % 2) * TLOC:(h % 2) * TLOC + TLOC], nrm[0:1, :]).then_inc(nr, 1)

            @block.scalar
            def _(scalar):
                for h in range(NH):
                    scalar.wait_ge(se, h + 1)
                    stg = stage[:, (h % 2) * TLOC:(h % 2) * TLOC + TLOC]
                    scalar.activation(sqbf[:], stg, AF.Square).then_inc(qs, 1)

            return nr, so, no

        qnr, qso, qno = fstream("q", wqT_h, qT_o, True, cosq, sinq, RS, qn2_o,
                                (ve_s, VE_ALL), None, None, dve_tld=True)
        knr, kso, kno = fstream("k", wkT_h, kT_o, True, cosk, sink, 1.0, knorm_d,
                                (qnr, NH), (qso[0], qso[1], 8 * 16), (qno[0], qno[1], 8 * 16), dve_tld=True)
        q2nr, q2so, q2no = fstream("q2", wqkvT_h[:, 0:D], q2T_o, False, None, None, RS, q2n2_o,
                                   (knr, NH), (kso[0], kso[1], 8 * 16), (kno[0], kno[1], 8 * 16))

        nst_s = C.sem("nst_s")

        @block.sync
        def _(sync):
            sync.wait_ge(kno[0], 8 * 16)
            sync.wait_ge(kno[1], 8 * 16)
            sync.dma_start(out=k2rows[:], in_=knorm_d).then_inc(nst_s, 16)

        @block.vector
        def _(vector):
            vector.wait_ge(nst_s, 16)
            vector.reduce_max(k2m[:], k2rows[:], axis=AX)
            vector.drain().then_inc(nst_s, 1)

        @block.gpsimd
        def _(gpsimd):
            gpsimd.wait_ge(nst_s, 17)
            St(gpsimd.dma_start(out=k2m_o, in_=k2m[:]))

        gw_s = C.sem("gw_s")
        gl_s = C.sem("gl_s")

        @block.gpsimd
        def _(gpsimd):
            gpsimd.wait_ge(ms, 1)
            gpsimd.dma_start(out=wgv, in_=wgT_h.rearrange("(c p) f -> p c f", p=128)).then_inc(gw_s, 16)

        @block.tensor
        def _(tensor):
            tensor.wait_ge(gw_s, 16)
            tensor.wait_ge(q2nr, NH)
            for t in range(NTILE):
                for e in range(NCH):
                    r = tensor.matmul(plg[:, t * 16:(t + 1) * 16],
                                      xnT[:, e * TLOC + t * 128: e * TLOC + (t + 1) * 128],
                                      wgv[:, e],
                                      start=(e == 0), stop=(e == NCH - 1))
            r.then_inc(gl_s, 1)

        @block.vector
        def _(vector):
            vector.wait_ge(gl_s, 1)
            vector.tensor_copy(logit8[:], plg[:, 0:128])
            vector.drain()
            for t in range(NTILE):
                vector.reduce_max(mx8[:, t:t + 1], logit8[:, t * 16:(t + 1) * 16], axis=AX)
            vector.drain()
            vector.tensor_scalar_mul(ngm8, mx8, -1.0)
            vector.drain().then_inc(gl_s, 1)

        @block.scalar
        def _(scalar):
            scalar.wait_ge(gl_s, 2)
            for t in range(NTILE):
                scalar.activation(exps[:, t * 16:(t + 1) * 16], logit8[:, t * 16:(t + 1) * 16], AF.Exp,
                                  bias=ngm8[:, t:t + 1], accum_out=top8[:, t:t + 1])
            scalar.drain().then_inc(gl_s, 1)

        @block.vector
        def _(vector):
            vector.wait_ge(gl_s, 3)
            vector.reciprocal(top8, top8)
            vector.drain()
            for t in range(NTILE):
                vector.tensor_scalar(ind8[:, t * 16:(t + 1) * 16], logit8[:, t * 16:(t + 1) * 16],
                                     mx8[:, t:t + 1], None, op0=AluOpType.is_equal)
                vector.tensor_scalar(svec8[:, t * 16:(t + 1) * 16], logit8[:, t * 16:(t + 1) * 16],
                                     mx8[:, t:t + 1], top8[:, t:t + 1],
                                     op0=AluOpType.is_equal, op1=AluOpType.mult)
            vector.drain().then_inc(gl_s, 1)
        GL_DONE = 4

        sw_s = [C.sem("sw_s0"), C.sem("sw_s1")]
        sm_s = C.sem("sm_s")
        sev_s = C.sem("sev_s")
        sel_s = C.sem("sel_s")

        @block.gpsimd
        def _(gpsimd):
            for fs in range(8):
                if fs >= 2:
                    gpsimd.wait_ge(sm_s, (fs - 1) * 8)
                gpsimd.dma_start(out=wsv[:, fs % 2],
                                 in_=wqkvT_h[:, D + fs * 512: D + (fs + 1) * 512].rearrange("(c p) f -> p c f", p=128)
                                 ).then_inc(sw_s[fs % 2], 16)

        @block.tensor
        def _(tensor):
            for fs in range(8):
                tensor.wait_ge(sw_s[fs % 2], (fs // 2 + 1) * 16)
                for t in range(NTILE):
                    it = fs * 8 + t
                    if it >= 4:
                        tensor.wait_ge(sev_s, it - 3)
                    bank = pj[:, (it % 4) * 512:(it % 4) * 512 + 512]
                    for e in range(NCH):
                        r = tensor.matmul(bank,
                                          xnT[:, e * TLOC + t * 128: e * TLOC + (t + 1) * 128],
                                          wsv[:, fs % 2, e],
                                          start=(e == 0), stop=(e == NCH - 1))
                    r.then_inc(sm_s, 1)

        @block.vector
        def _(vector):
            vector.wait_ge(gl_s, GL_DONE)
            for fs in range(8):
                wvec = ind8 if fs < 4 else svec8
                acc = ksh8 if fs < 4 else vsh8
                for t in range(NTILE):
                    it = fs * 8 + t
                    vector.wait_ge(sm_s, it + 1)
                    bank = pj[:, (it % 4) * 512:(it % 4) * 512 + 512]
                    vector.tensor_copy(kfsb[:], bank)
                    vector.drain().then_inc(sev_s, 1)
                    hsel = wvec[:, t * 16 + 4 * (fs % 4): t * 16 + 4 * (fs % 4) + 4]
                    vector.tensor_tensor(seltmp[:].rearrange("p (h d) -> p h d", d=128),
                                         kfsb[:].rearrange("p (h d) -> p h d", d=128),
                                         hsel.rearrange("p (h o) -> p h o", o=1).broadcast_to((128, 4, 128)),
                                         op=AluOpType.mult)
                    vector.drain()
                    vector.tensor_tensor(seltmp[:, 0:256], seltmp[:, 0:256], seltmp[:, 256:512], op=AluOpType.add)
                    vector.drain()
                    if fs % 4 == 0:
                        vector.tensor_tensor(acc[:, t * 128:(t + 1) * 128], seltmp[:, 0:128], seltmp[:, 128:256], op=AluOpType.add)
                    else:
                        vector.tensor_tensor(seltmp[:, 0:128], seltmp[:, 0:128], seltmp[:, 128:256], op=AluOpType.add)
                        vector.drain()
                        vector.tensor_tensor(acc[:, t * 128:(t + 1) * 128], acc[:, t * 128:(t + 1) * 128], seltmp[:, 0:128], op=AluOpType.add)
                    vector.drain().then_inc(sel_s, 1)
        SEL_ALL = 64

        fin_s = C.sem("fin_s")
        ft_s = C.sem("ft_s")
        vsc_s = C.sem("vsc_s")
        vst_s = C.sem("vst_s")

        @block.vector
        def _(vector):
            vector.wait_ge(sel_s, SEL_ALL)
            for t in range(NTILE):
                vector.tensor_copy(kshbf[:], ksh8[:, t * 128:(t + 1) * 128])
                vector.drain().then_inc(fin_s, 1)
                vector.wait_ge(ft_s, t + 1)
                vector.tensor_copy(kshT8[:, t * 128:(t + 1) * 128], ptb[:, 0:128])
                vector.drain().then_inc(fin_s, 1)

        @block.tensor
        def _(tensor):
            for t in range(NTILE):
                tensor.wait_ge(fin_s, 2 * t + 1)
                tensor.transpose(ptb[:, 0:128], kshbf[:], ident[:]).then_inc(ft_s, 1)

        @block.scalar
        def _(scalar):
            scalar.wait_ge(sel_s, SEL_ALL)
            for t in range(NTILE):
                if t > 0:
                    scalar.wait_ge(vst_s, t * 16)
                scalar.copy(vshst[:, 0:128], vsh8[:, t * 128:(t + 1) * 128])
                scalar.drain().then_inc(vsc_s, 1)

        @block.gpsimd
        def _(gpsimd):
            for t in range(NTILE):
                gpsimd.wait_ge(vsc_s, t + 1)
                gpsimd.dma_start(out=vsh_o[t * 128:(t + 1) * 128, :], in_=vshst[:]).then_inc(vst_s, 16)

        ksq_s = C.sem("ksq_s")

        @block.scalar
        def _(scalar):
            scalar.wait_ge(fin_s, 2 * NTILE)
            scalar.activation(sqbf[:], kshT8[:], AF.Square)
            scalar.drain().then_inc(ksq_s, 1)

        @block.tensor
        def _(tensor):
            tensor.wait_ge(ksq_s, 1)
            for half in range(2):
                r = tensor.matmul(nrm[0:1, half * 512:half * 512 + 512],
                                  ones_bf[:], sqbf[:, half * 512:(half + 1) * 512],
                                  start=True, stop=True)
            r.then_inc(ksq_s, 1)

        @block.vector
        def _(vector):
            vector.wait_ge(ksq_s, 2)
            vector.tensor_copy(kshrow[:], nrm[0:1, :])
            vector.drain()
            vector.reduce_max(k2shm[:], kshrow[:], axis=AX)
            vector.drain().then_inc(ksq_s, 1)

        @block.gpsimd
        def _(gpsimd):
            gpsimd.wait_ge(ksq_s, 3)
            St(gpsimd.dma_start(out=kshT_o, in_=kshT8[:]))
            St(gpsimd.dma_start(out=k2shm_o, in_=k2shm[:]))
            gpsimd.wait_ge(vst_s, NTILE * 16)
            for sgrp in (qso, kso, q2so, qno, q2no, kno):
                gpsimd.wait_ge(sgrp[0], 8 * 16)
                gpsimd.wait_ge(sgrp[1], 8 * 16)
            gpsimd.wait_ge(stm, n_st[0])
            gpsimd.sem_inc(phase, 1)



KSLOT = 17


def emit_l0_attn(nc, block, phase, pgate, ios, prefix, semstack):
    x_h = ios["x"]
    qT_h = ios["qT"]
    q2T_h = ios["q2T"]
    kTl_h = ios["kTl"]
    vl_h = ios["vl"]
    kshTl_h = ios["kshTl"]
    vshl_h = ios["vshl"]
    kTg_h = ios["kTg"]
    vg_h = ios["vg"]
    kshg_h = ios["kshg"]
    vshg_h = ios["vshg"]
    qn2_h = ios["qn2"]
    q2n2_h = ios["q2n2"]
    kng_h = ios["kng"]
    pm_h = ios["pm"]
    woT_h = ios["woT"]
    maskT_h = ios["maskT"]
    ident_h = ios["ident"]
    x1_o = ios["x1_o"]
    nb_d = ios["nb_d"]
    NBIAS = 2 * NH * 2 * KSLOT
    with contextlib.ExitStack() as ectx:
        C = Ctx(nc, ectx, prefix, semstack)
        att = C.sb("att", [128, NTILE * D], BF16)
        attv = att[:].rearrange("p (t f) -> p t f", f=D)
        y2g = C.sb("y2g", [128, NTILE * D], BF16)
        y2gv = y2g[:].rearrange("p (t f) -> p t f", f=D)
        ptbuf = C.sb("ptbuf", [128, 4 * 256], BF16)
        kTb = C.sb("kTb", [128, 2 * 2 * 512], BF16)
        kTbv = kTb[:].rearrange("p (b c f) -> p b c f", b=2, f=512)
        vbb = C.sb("vbb", [128, 2 * 4 * 258], BF16)
        vbv = vbb[:].rearrange("p (b t c) -> p b t c", b=2, c=258)
        bias_sb = C.sb("bias_sb", [128, NBIAS], F32)
        nb16 = C.sb("nb16", [NH, 4], F32)
        nbrow = C.sb("nbrow", [1, 64], F32)
        pmrow = C.sb("pmrow", [1, NBIAS], F32)
        brow = C.sb("brow", [1, NBIAS], F32)
        brow_bf = C.sb("brow_bf", [1, NBIAS], BF16)
        k2a = C.sb("k2a", [NH, NCORE], F32)
        k2sha = C.sb("k2sha", [1, NCORE], F32)
        K2 = C.sb("K2", [NH, 1], F32)
        K2shb = C.sb("K2shb", [NH, 1], F32)
        K2sh1 = C.sb("K2sh1", [1, 1], BF16)
        ones_row = C.sb("ones_row", [1, 128], BF16)
        maskT = C.sb("maskT_sb", [128, 128], BF16)
        ident = C.sb("ident_sb", [128, 128], BF16)
        rec1 = C.sb("rec1", [128, 2], F32)
        ytmp = C.sb("ytmp", [128, 2 * 128], F32)
        kshTl = C.sb("kshTl_sb", [128, TLOC], BF16)
        vshl = C.sb("vshl_sb", [128, NTILE * 132], BF16)
        vshlv = vshl[:].rearrange("p (t c) -> p t c", c=132)

        arena = C.sb("arena", [128, 100352], mybir.dt.uint8)
        abase = nc.lookup_mloc(arena).addr

        def at(name, shape, dtype, off):
            return nc.alloc_sbuf_tensor_at(name, shape, dtype, offset=abase + off)

        kTl = at("kTl_sb", [128, NCH * TLOC], BF16, 0)
        vl = at("vl_sb", [128, NTILE * NH * 129], BF16, 32768)
        vlv = vl[:].rearrange("p (t c) -> p t c", c=NH * 129)
        q2Tl = at("q2Tl", [128, NCH * TLOC], BF16, 0)
        kshTs = at("kshTs", [128, T], BF16, 32768)
        vshs = at("vshs", [128, 64 * 132], BF16, 49152)
        vshv = vshs[:].rearrange("p (t c) -> p t c", c=132)
        wof = at("wof", [128, NCH * D], BF16, 0)
        wofv = wof[:].rearrange("p (c f) -> p c f", f=D)
        qTl = at("qTl", [128, NCH * TLOC], BF16, 67584)
        attTt = at("attTt", [128, NCH * 128], BF16, 67584)
        xt2 = at("xt2", [128, 2 * D], F32, 67584 + 4096)
        x1st = at("x1st", [128, D], F32, 67584 + 4096 + 16384)
        etmp = at("etmp", [128, 512], F32, 67584 + 4096 + 16384 + 8192)
        ybase = nc.lookup_mloc(y2g).addr
        qn2s = nc.alloc_sbuf_tensor_at("qn2s", [NH, TLOC], F32, offset=ybase)
        q2n2s = nc.alloc_sbuf_tensor_at("q2n2s", [NH, TLOC], F32, offset=ybase + 4096)
        bnd = nc.alloc_sbuf_tensor_at("bnd", [NH, TLOC], F32, offset=ybase + 8192)

        psA = C.ps("psA", [128, 1024], F32)
        psO = C.ps("psO", [128, 2048], F32)
        ptb = C.ps("ptb", [128, 2048], BF16)

        ld0 = C.sem("ld0")
        ld1 = C.sem("ld1")

        @block.sync
        def _(sync):
            for i, (dst, srcap) in enumerate([
                (qTl[:].rearrange("p (c t) -> p c t", t=TLOC), qT_h.rearrange("(c p) t -> p c t", p=128)),
                (kTl[:].rearrange("p (c t) -> p c t", t=TLOC), kTl_h.rearrange("(c p) t -> p c t", p=128)),
                (vlv, vl_h.rearrange("(t p) c -> p t c", p=128)),
                (kshTl[:], kshTl_h),
                (vshlv, vshl_h.rearrange("(t p) c -> p t c", p=128)),
                (qn2s[:], qn2_h),
                (q2n2s[:], q2n2_h),
                (k2a[:].rearrange("h (r o) -> h r o", o=1), kng_h.rearrange("(r h) c -> h r c", h=17)[0:16]),
                (k2sha[:].rearrange("h (r o) -> h r o", o=1), kng_h.rearrange("(r h) c -> h r c", h=17)[16:17]),
                (maskT[:], maskT_h),
                (ident[:], ident_h),
                (pmrow[:], pm_h),
            ]):
                sync.wait_ge(ld0, i * 16)
                sync.dma_start(out=dst, in_=srcap, allow_slow_non_contiguous=True).then_inc(ld0, 16)
        LD0 = 12 * 16

        ms = C.sem("ms")

        @block.gpsimd
        def _(gpsimd):
            gpsimd.memset(ones_row[:], 1.0)
            gpsimd.drain().then_inc(ms, 1)

        bs = C.sem("bs")

        @block.vector
        def _(vector):
            vector.wait_ge(ld0, LD0)
            vector.reduce_max(K2[:], k2a[:], axis=AX)
            vector.reduce_max(K2shb[0:1, :], k2sha[:], axis=AX)
            vector.drain()
            vector.tensor_scalar(bnd[:], qn2s[:], K2[:], None, op0=AluOpType.mult)
            vector.drain().then_inc(bs, 1)

        @block.scalar
        def _(scalar):
            scalar.wait_ge(bs, 1)
            scalar.activation(bnd[:], bnd[:], AF.Sqrt)
            scalar.drain().then_inc(bs, 1)

        @block.vector
        def _(vector):
            vector.wait_ge(bs, 2)
            for g in range(2):
                vector.reduce_max(nb16[:, g:g + 1], bnd[:, g * 512:(g + 1) * 512], axis=AX)
            vector.tensor_copy(K2sh1[:], K2shb[0:1, 0:1])
            vector.drain().then_inc(bs, 1)

        @block.tensor
        def _(tensor):
            tensor.wait_ge(bs, 3)
            tensor.wait_ge(ms, 1)
            tensor.matmul(psA[:, 0:1], ones_row[:], K2sh1[:], start=True, stop=True).then_inc(bs, 1)

        @block.vector
        def _(vector):
            vector.wait_ge(bs, 4)
            vector.tensor_copy(K2shb[:], psA[0:NH, 0:1])
            vector.drain()
            vector.tensor_scalar(bnd[:], q2n2s[:], K2shb[:], None, op0=AluOpType.mult)
            vector.drain().then_inc(bs, 1)

        @block.scalar
        def _(scalar):
            scalar.wait_ge(bs, 5)
            scalar.activation(bnd[:], bnd[:], AF.Sqrt)
            scalar.drain().then_inc(bs, 1)

        @block.vector
        def _(vector):
            vector.wait_ge(bs, 6)
            for g in range(2):
                vector.reduce_max(nb16[:, 2 + g:3 + g], bnd[:, g * 512:(g + 1) * 512], axis=AX)
            vector.drain()
            vector.tensor_scalar_mul(nb16[:], nb16[:], -1.0)
            vector.drain().then_inc(bs, 1)

        kv_ld = [C.sem("kv0"), C.sem("kv1")]
        stc = C.sem("stc")
        exc = C.sem("exc")
        mkc = C.sem("mkc")
        pvc = C.sem("pvc")
        evc = C.sem("evc")
        ysl = C.sem("ysl")
        bsd1 = C.sem("bsd1")
        bsd2 = C.sem("bsd2")
        cvs = C.sem("cvs")
        NCV = [0]

        @block.gpsimd
        def _(gpsimd):
            gpsimd.wait_ge(bs, 7)
            gpsimd.dma_start(out=nb_d, in_=nb16[:]).then_inc(bsd1, 16)
            ncv = 0
            for (srcT, dstT) in ios.get("conv", []):
                R, Cc = srcT.shape[0], srcT.shape[1]
                step = max(256, (2 * 1024 * 1024) // (Cc * 4))
                r0 = 0
                while r0 < R:
                    r1 = min(R, r0 + step)
                    gpsimd.wait_ge(cvs, ncv * 16)
                    gpsimd.wait_ge(pvc, min(ncv * 100, 3100))
                    gpsimd.dma_start(out=dstT[r0:r1], in_=srcT[r0:r1]).then_inc(cvs, 16)
                    ncv += 1
                    r0 = r1
            NCV[0] = ncv

        @block.sync
        def _(sync):
            sync.wait_ge(bsd1, 16)
            sync.dma_start(out=nbrow[0:1, :].rearrange("o (h g) -> o h g", g=4),
                           in_=nb_d.rearrange("(o h) g -> o h g", o=1)).then_inc(bsd2, 16)

        @block.vector
        def _(vector):
            vector.wait_ge(bsd2, 16)
            nbv = nbrow[0:1, :].rearrange("o (h wg) -> o h wg", wg=4)
            bv = brow[0:1, :].rearrange("o (w h g s) -> o w h g s", w=2, h=NH, g=2)
            for w in range(2):
                for g in range(2):
                    vector.tensor_tensor(
                        bv[:, w, :, g, :],
                        nbv[:, :, 2 * w + g:2 * w + g + 1].broadcast_to((1, NH, KSLOT)),
                        pmrow[0:1, :].rearrange("o (w h g s) -> o w h g s", w=2, h=NH, g=2)[:, w, :, g, :],
                        op=AluOpType.mult)
            vector.drain()
            vector.tensor_scalar(pmrow[:], pmrow[:], 1e9, 1e9, op0=AluOpType.mult, op1=AluOpType.subtract)
            vector.drain()
            vector.tensor_tensor(brow[:], brow[:], pmrow[:], op=AluOpType.add)
            vector.drain()
            vector.tensor_copy(brow_bf[:], brow[:])
            vector.drain().then_inc(bs, 1)

        @block.tensor
        def _(tensor):
            tensor.wait_ge(bs, 8)
            for seg in range((NBIAS + 511) // 512):
                w = min(512, NBIAS - seg * 512)
                if seg > 0:
                    tensor.wait_ge(bs, 8 + 2 * seg)
                tensor.matmul(psA[:, 0:w], ones_row[:], brow_bf[0:1, seg * 512:seg * 512 + w],
                              start=True, stop=True).then_inc(bs, 1)

        @block.vector
        def _(vector):
            for seg in range((NBIAS + 511) // 512):
                w = min(512, NBIAS - seg * 512)
                vector.wait_ge(bs, 9 + seg * 2)
                vector.tensor_copy(bias_sb[:, seg * 512:seg * 512 + w], psA[:, 0:w])
                vector.drain().then_inc(bs, 1)
        BS_DONE = 8 + 2 * ((NBIAS + 511) // 512)


        CNT = dict(st=0, mk=0, pvg=0, m=0, ev=0, ysl=0)
        CNT_EX = [0]
        CNT_DV = [0]

        def attention(which):
            groups = []
            for hg in range(8):
                for g in range(2):
                    for qq in range(2):
                        steps = []
                        for kt in range(2 * qq + 2):
                            for hh in range(2):
                                steps.append(("diag", kt, hh))
                        for kb in range(8 if g == 0 else 16):
                            for kt in range(4):
                                for hh in range(2):
                                    steps.append((kb, kt, hh))
                        groups.append((hg, g, qq, steps))

            g_meta = []
            st = CNT["st"]; pvg = CNT["pvg"]; m = CNT["m"]
            ev = CNT["ev"]; ysl_n = CNT["ysl"]
            for (hg, g, qq, steps) in groups:
                meta = dict(pvg_base=pvg, ev_base=ev, m_base=m, ysl_base=ysl_n,
                            nsteps=len(steps))
                st += len(steps)
                pvg += len(steps)
                if which == 0:
                    m += 8 if g == 0 else 16
                ev += 4
                ysl_n += 4
                g_meta.append(meta)

            if which == 0:
                ldgate = []
                pv = CNT["pvg"]
                for (hg, g, qq, steps) in groups:
                    ndiag = (2 * qq + 2) * 2
                    pv += ndiag
                    nkb = 8 if g == 0 else 16
                    for kb in range(nkb):
                        pv += 8
                        ldgate.append(pv)

                @block.sync
                def _(sync):
                    mm = CNT["m"]
                    li = 0
                    for (hg, g, qq, steps) in groups:
                        h0 = 2 * hg
                        nkb = 8 if g == 0 else 16
                        for kb in range(nkb):
                            rk = kb if kb < 8 else 15 - kb
                            hf = 0 if kb < 8 else 1
                            if li >= 2:
                                sync.wait_ge(pvc, ldgate[li - 2])
                            sync.dma_start(
                                out=kTbv[:, mm % 2],
                                in_=kTg_h[rk * 2048 + h0 * 128: rk * 2048 + (h0 + 2) * 128,
                                          hf * 512:(hf + 1) * 512]
                                .rearrange("(c p) f -> p c f", p=128)).then_inc(kv_ld[mm % 2], 16)
                            sync.dma_start(
                                out=vbv[:, mm % 2],
                                in_=vg_h[rk * 1024 + hf * 512: rk * 1024 + hf * 512 + 512,
                                         h0 * 129:(h0 + 2) * 129]
                                .rearrange("(t p) c -> p t c", p=128)).then_inc(kv_ld[mm % 2], 16)
                            mm += 1
                            li += 1
            else:
                @block.sync
                def _(sync):
                    sync.wait_ge(evc, CNT["ev"])
                    n1 = 0
                    sync.dma_start(out=q2Tl[:].rearrange("p (c t) -> p c t", t=TLOC),
                                   in_=q2T_h.rearrange("(c p) t -> p c t", p=128)).then_inc(ld1, 16)
                    n1 += 16
                    for gb in range(NBLK):
                        rk = gb if gb < 8 else 15 - gb
                        hf = 0 if gb < 8 else 1
                        sync.wait_ge(ld1, n1)
                        sync.dma_start(out=kshTs[:, gb * 512:(gb + 1) * 512],
                                       in_=kshg_h[rk * 128:(rk + 1) * 128, hf * 512:(hf + 1) * 512]
                                       ).then_inc(ld1, 16)
                        n1 += 16
                        sync.wait_ge(ld1, n1)
                        sync.dma_start(out=vshv[:, gb * 4:(gb + 1) * 4, :],
                                       in_=vshg_h[rk * 1024 + hf * 512: rk * 1024 + hf * 512 + 512, :]
                                       .rearrange("(t p) c -> p t c", p=128)).then_inc(ld1, 16)
                        n1 += 16

            qsrc = qTl if which == 0 else q2Tl

            @block.tensor
            def _(tensor):
                tensor.wait_ge(bs, BS_DONE)
                if which == 1:
                    tensor.wait_ge(ld1, 16 + 32 * 16)
                st0 = CNT["st"]
                flat = []
                mkcnt = CNT["mk"]
                mloc = CNT["m"]
                for gi, (hg, g, qq, steps) in enumerate(groups):
                    h0 = 2 * hg
                    for si, (kb, kt, hh) in enumerate(steps):
                        diag = (kb == "diag")
                        if diag and kt >= 2 * qq:
                            mkcnt += 1
                        flat.append(dict(gi=gi, g=g, qq=qq, h=h0 + hh, hh=hh, kb=kb, kt=kt,
                                         diag=diag, first=(si == 0), mk=mkcnt,
                                         m=(None if diag or which == 1 else mloc + kb)))
                    if which == 0:
                        mloc += 8 if g == 0 else 16

                def emit_st(ii, info):
                    st = st0 + ii
                    if info["m"] is not None and info["kt"] == 0 and info["hh"] == 0:
                        tensor.wait_ge(kv_ld[info["m"] % 2], (info["m"] // 2 + 1) * 32)
                    if st >= 2:
                        tensor.wait_ge(exc, st - 1)
                    g, qq, h, kt, kb = info["g"], info["qq"], info["h"], info["kt"], info["kb"]
                    if info["diag"]:
                        if which == 0:
                            lhsT = kTl[:, h * TLOC + (g * 4 + kt) * 128: h * TLOC + (g * 4 + kt + 1) * 128]
                        else:
                            lhsT = kshTl[:, (g * 4 + kt) * 128:(g * 4 + kt + 1) * 128]
                    else:
                        if which == 0:
                            lhsT = kTbv[:, info["m"] % 2, info["hh"], kt * 128:(kt + 1) * 128]
                        else:
                            lhsT = kshTs[:, (kb * 4 + kt) * 128:(kb * 4 + kt + 1) * 128]
                    qoff = h * TLOC + g * 512 + qq * 256
                    tensor.matmul(psA[:, (st % 2) * 512:(st % 2) * 512 + 256],
                                  lhsT, qsrc[:, qoff:qoff + 256],
                                  start=True, stop=True).then_inc(stc, 1)

                def emit_pv(ii, info):
                    st = st0 + ii
                    if info["first"]:
                        tensor.wait_ge(evc, g_meta[info["gi"]]["ev_base"])
                    if info["diag"] and info["kt"] >= 2 * info["qq"]:
                        tensor.wait_ge(mkc, info["mk"])
                    else:
                        tensor.wait_ge(exc, st + 1)
                    g, qq, h, kt, kb = info["g"], info["qq"], info["h"], info["kt"], info["kb"]
                    if info["diag"]:
                        jlist = [j for j in range(2) if qq * 2 + j >= kt]
                    else:
                        jlist = [0, 1]
                    for j in jlist:
                        slot = info["hh"] * 2 + j
                        first = (info["diag"] and kt == 0)
                        last = (kb == (7 if g == 0 else 15) and kt == 3)
                        if info["diag"]:
                            if which == 0:
                                rhs = vlv[:, g * 4 + kt, h * 129:h * 129 + 129]
                            else:
                                rhs = vshlv[:, g * 4 + kt, 0:129]
                        else:
                            if which == 0:
                                rhs = vbv[:, info["m"] % 2, kt, h * 129 - info["hh"] * 2 * 129 if False else info["hh"] * 129:info["hh"] * 129 + 129]
                            else:
                                rhs = vshv[:, kb * 4 + kt, 0:129]
                        r = tensor.matmul(psO[:, slot * 512:slot * 512 + 129],
                                          ptbuf[:, (st % 4) * 256 + j * 128:(st % 4) * 256 + (j + 1) * 128],
                                          rhs, start=first, stop=last)
                    r.then_inc(pvc, 1)

                for ii, info in enumerate(flat):
                    emit_st(ii, info)
                    if ii > 1:
                        emit_pv(ii - 2, flat[ii - 2])
                emit_pv(len(flat) - 2, flat[-2])
                emit_pv(len(flat) - 1, flat[-1])
                CNT["st"] = st0 + len(flat)
                CNT["mk"] = mkcnt
                CNT["m"] = mloc
                CNT["pvg"] = CNT["st"]

            @block.scalar
            def _(scalar):
                scalar.wait_ge(bs, BS_DONE)
                st = CNT_EX[0]
                for gi, (hg, g, qq, steps) in enumerate(groups):
                    h0 = 2 * hg
                    for (kb, kt, hh) in steps:
                        h = h0 + hh
                        slotk = 0 if kb == "diag" else 1 + kb
                        bcol = (0 if which == 0 else NH * 2 * KSLOT) + h * 2 * KSLOT + g * KSLOT + slotk
                        scalar.wait_ge(stc, st + 1)
                        if st >= 4:
                            scalar.wait_ge(pvc, st - 3)
                        scalar.activation(ptbuf[:, (st % 4) * 256:(st % 4) * 256 + 256],
                                          psA[:, (st % 2) * 512:(st % 2) * 512 + 256], AF.Exp,
                                          bias=bias_sb[:, bcol:bcol + 1]).then_inc(exc, 1)
                        st += 1
                    if which == 1:
                        for s in range(4):
                            hh, j = s // 2, s % 2
                            h = h0 + hh
                            tloc = g * 4 + qq * 2 + j
                            scalar.wait_ge(evc, g_meta[gi]["ev_base"] + s + 1)
                            scalar.activation(y2gv[:, tloc, h * 128:(h + 1) * 128],
                                              ytmp[:, (s % 2) * 128:(s % 2) * 128 + 128], AF.Silu).then_inc(ysl, 1)
                CNT_EX[0] = st

            @block.vector
            def _(vector):
                vector.wait_ge(bs, BS_DONE)
                st = CNT_DV[0]
                for gi, (hg, g, qq, steps) in enumerate(groups):
                    h0 = 2 * hg
                    for (kb, kt, hh) in steps:
                        if kb == "diag" and kt >= 2 * qq:
                            j = kt - 2 * qq
                            vector.wait_ge(exc, st + 1)
                            vector.tensor_tensor(
                                ptbuf[:, (st % 4) * 256 + j * 128:(st % 4) * 256 + (j + 1) * 128],
                                ptbuf[:, (st % 4) * 256 + j * 128:(st % 4) * 256 + (j + 1) * 128],
                                maskT[:], op=AluOpType.mult).then_inc(mkc, 1)
                        st += 1
                    vector.wait_ge(pvc, g_meta[gi]["pvg_base"] + g_meta[gi]["nsteps"])
                    for s in range(4):
                        hh, j = s // 2, s % 2
                        h = h0 + hh
                        tloc = g * 4 + qq * 2 + j
                        rs = rec1[:, s % 2:s % 2 + 1]
                        vector.reciprocal(rs, psO[:, s * 512 + 128:s * 512 + 129])
                        vector.drain()
                        if which == 0:
                            vector.tensor_scalar(attv[:, tloc, h * 128:(h + 1) * 128],
                                                 psO[:, s * 512:s * 512 + 128],
                                                 rs, None, op0=AluOpType.mult).then_inc(evc, 1)
                        else:
                            if s >= 2:
                                vector.wait_ge(ysl, g_meta[gi]["ysl_base"] + s - 1)
                            vector.tensor_scalar(ytmp[:, (s % 2) * 128:(s % 2) * 128 + 128],
                                                 psO[:, s * 512:s * 512 + 128],
                                                 rs, None, op0=AluOpType.mult).then_inc(evc, 1)
                CNT_DV[0] = st
            CNT["ev"] += len(groups) * 4
            if which == 1:
                CNT["ysl"] += len(groups) * 4

        attention(0)
        ATT_EVC = CNT["ev"]
        attention(1)
        ALL_EVC = CNT["ev"]
        ALL_YSL = CNT["ysl"]

        wo_ld = C.sem("wo_ld")
        xld = [C.sem("xl0"), C.sem("xl1")]
        trs = C.sem("trs")
        tev = C.sem("tev")
        epm = C.sem("epm")
        ecb = C.sem("ecb")
        x1s = C.sem("x1s")

        @block.gpsimd
        def _(gpsimd):
            gpsimd.wait_ge(ysl, ALL_YSL)
            gpsimd.dma_start(out=wofv, in_=woT_h.rearrange("(c p) f -> p c f", p=128)).then_inc(wo_ld, 16)

        @block.sync
        def _(sync):
            for t in range(NTILE):
                if t >= 2:
                    sync.wait_ge(ecb, (t - 1) * 4)
                sync.dma_start(out=xt2[:, (t % 2) * D:(t % 2) * D + D],
                               in_=x_h.rearrange("(t p) f -> p t f", p=128)[:, t, :]).then_inc(xld[t % 2], 16)

        @block.tensor
        def _(tensor):
            tensor.wait_ge(evc, ALL_EVC)
            tensor.wait_ge(wo_ld, 16)
            for t in range(NTILE):
                for gtr in range(2):
                    G = t * 2 + gtr
                    if G >= 2:
                        tensor.wait_ge(tev, G - 1)
                    for jj in range(8):
                        e = gtr * 8 + jj
                        r = tensor.transpose(ptb[:, (G % 2) * 1024 + jj * 128:(G % 2) * 1024 + (jj + 1) * 128],
                                             attv[:, t, e * 128:(e + 1) * 128], ident[:])
                    r.then_inc(trs, 1)
                tensor.wait_ge(tev, 2 * t + 2)
                for fs in range(4):
                    it = t * 4 + fs
                    if it >= 2:
                        tensor.wait_ge(ecb, it - 1)
                    for e in range(NCH):
                        r = tensor.matmul(psA[:, (it % 2) * 512:(it % 2) * 512 + 512],
                                          attTt[:, e * 128:(e + 1) * 128],
                                          wofv[:, e, fs * 512:(fs + 1) * 512],
                                          start=(e == 0), stop=(e == NCH - 1))
                    r.then_inc(epm, 1)

        @block.vector
        def _(vector):
            for t in range(NTILE):
                for gtr in range(2):
                    G = t * 2 + gtr
                    vector.wait_ge(trs, G + 1)
                    if t >= 1 and gtr == 0:
                        vector.wait_ge(epm, t * 4)
                    vector.tensor_copy(
                        attTt[:].rearrange("p (c tt) -> p c tt", tt=128)[:, gtr * 8:(gtr + 1) * 8, :],
                        ptb[:, (G % 2) * 1024:(G % 2) * 1024 + 1024].rearrange("p (c tt) -> p c tt", tt=128))
                    vector.drain().then_inc(tev, 1)
                vector.wait_ge(xld[t % 2], (t // 2 + 1) * 16)
                vector.wait_ge(ysl, ALL_YSL)
                for fs in range(4):
                    it = t * 4 + fs
                    vector.wait_ge(epm, it + 1)
                    if t > 0 and fs == 0:
                        vector.wait_ge(x1s, t * 16)
                    vector.tensor_tensor(etmp[:], psA[:, (it % 2) * 512:(it % 2) * 512 + 512],
                                         y2gv[:, t, fs * 512:(fs + 1) * 512], op=AluOpType.mult)
                    vector.drain()
                    vector.tensor_tensor(x1st[:, fs * 512:(fs + 1) * 512], etmp[:],
                                         xt2[:, (t % 2) * D + fs * 512:(t % 2) * D + (fs + 1) * 512],
                                         op=AluOpType.add)
                    vector.drain().then_inc(ecb, 1)

        @block.gpsimd
        def _(gpsimd):
            for t in range(NTILE):
                gpsimd.wait_ge(ecb, (t + 1) * 4)
                gpsimd.dma_start(out=x1_o.rearrange("(t p) f -> p t f", p=128)[:, t, :],
                                 in_=x1st[:]).then_inc(x1s, 16)
            gpsimd.wait_ge(x1s, NTILE * 16)
            gpsimd.wait_ge(cvs, NCV[0] * 16)
            gpsimd.sem_inc(phase, 1)



EPSR = 1.1920929e-07


def emit_l1_a(nc, block, phase, pgate, ios, prefix, semstack):
    x1_h = ios["x1"]
    wqkvT_h = ios["wqkvT"]
    ident_h = ios["ident"]
    stats_o = ios["stats_o"]
    sig_o = ios["sig_o"]
    wcum_o = ios["wcum_o"]
    kvcum_o = ios["kvcum_o"]
    tot_o = ios["tot_o"]
    with contextlib.ExitStack() as ectx:
        C = Ctx(nc, ectx, prefix, semstack)
        xt2 = C.sb("xt2", [128, 2 * D], F32)
        xnf = C.sb("xnf", [128, D], F32)
        sq_s = C.sb("sq_s", [128, D], F32)
        sqbf = C.sb("sqbf", [128, TLOC], BF16)
        st8 = C.sb("st8", [128, 64], F32)
        sx8 = st8[:, 0:8]; sqs8 = st8[:, 8:16]; m8 = st8[:, 16:24]; e28 = st8[:, 24:32]
        msq8 = st8[:, 32:40]; var8 = st8[:, 40:48]; lnv8 = st8[:, 48:56]; rstd8 = st8[:, 56:64]
        stst = C.sb("stst", [128, 16], F32)
        rr8 = C.sb("rr8", [128, 24], F32)
        ssr8 = rr8[:, 0:8]; lnr8 = rr8[:, 8:16]; rsr8 = rr8[:, 16:24]
        eps1 = C.sb("eps1", [128, 1], F32)
        epsr = C.sb("epsr", [128, 1], F32)
        hbf = C.sb("hbf", [128, D], BF16)
        hT = C.sb("hT", [128, NCH * TLOC], BF16)
        ident = C.sb("ident_sb", [128, 128], BF16)
        ones_bf = C.sb("ones_bf", [128, 1], BF16)
        ones_f = C.sb("ones_f", [1, 128], F32)
        qsb = C.sb("qsb", [128, NCH * TLOC], BF16)
        ksb = C.sb("ksb", [128, NCH * TLOC], BF16)
        vsb = C.sb("vsb", [128, NCH * TLOC], BF16)
        wslab = C.sb("wslab", [128, 2 * NCH * 128], BF16)
        wsv = wslab[:].rearrange("p (b c f) -> p b c f", b=2, f=128)
        nqrow = C.sb("nqrow", [1, TLOC], F32)
        nkrow = C.sb("nkrow", [1, TLOC], F32)
        rqrow = C.sb("rqrow", [1, TLOC], F32)
        rkrow = C.sb("rkrow", [1, TLOC], F32)
        rqb = C.sb("rqb", [128, TLOC], F32)
        rkb = C.sb("rkb", [128, TLOC], F32)
        tot = C.sb("tot", [128, 64], F32)
        hbase = nc.lookup_mloc(hT).addr
        ktmp = nc.alloc_sbuf_tensor_at("ktmp", [128, TLOC], F32, offset=hbase)
        wtmp = nc.alloc_sbuf_tensor_at("wtmp", [128, TLOC], F32, offset=hbase + 4096)
        wvtmp = nc.alloc_sbuf_tensor_at("wvtmp", [128, TLOC], F32, offset=hbase + 8192)
        qtmp = nc.alloc_sbuf_tensor_at("qtmp", [128, TLOC], F32, offset=hbase + 12288)
        xbase = nc.lookup_mloc(xt2).addr
        cst = nc.alloc_sbuf_tensor_at("cst", [128, 4 * TLOC], F32, offset=xbase)
        sgst = nc.alloc_sbuf_tensor_at("sgst", [128, 2 * TLOC], BF16, offset=nc.lookup_mloc(sq_s).addr)

        pj = C.ps("pj", [128, 2048], F32)
        ptb_ctx = contextlib.ExitStack()
        ptb = ptb_ctx.enter_context(nc.psum_tensor("p3_ptb", [128, 2048], BF16))

        xldA = [C.sem("xldA0"), C.sem("xldA1")]
        xldB = [C.sem("xldB0"), C.sem("xldB1")]
        pav = C.sem("pav")
        pas = C.sem("pas")
        tld = C.sem("tld")
        ms = C.sem("ms")
        stm = C.sem("stm")
        n_st = [0]

        def St(r):
            r.then_inc(stm, 16)
            n_st[0] += 16


        @block.sync
        def _(sync):
            for t in range(NTILE):
                if t >= 2:
                    sync.wait_ge(pav, t - 1)
                    sync.wait_ge(pas, t - 1)
                sync.dma_start(out=xt2[:, (t % 2) * D:(t % 2) * D + D],
                               in_=x1_h.rearrange("(t p) f -> p t f", p=128)[:, t, :]).then_inc(xldA[t % 2], 16)
            sync.dma_start(out=ident[:], in_=ident_h).then_inc(tld, 16)

        @block.gpsimd
        def _(gpsimd):
            gpsimd.memset(eps1[:], 1e-5)
            gpsimd.memset(epsr[:], EPSR)
            gpsimd.memset(ones_bf[:], 1.0)
            gpsimd.memset(ones_f[:], 1.0)
            gpsimd.drain().then_inc(ms, 1)

        @block.vector
        def _(vector):
            for t in range(NTILE):
                vector.wait_ge(xldA[t % 2], (t // 2 + 1) * 16)
                vector.reduce_sum(sx8[:, t:t + 1], xt2[:, (t % 2) * D:(t % 2) * D + D], axis=AX)
                vector.drain().then_inc(pav, 1)

        @block.scalar
        def _(scalar):
            for t in range(NTILE):
                scalar.wait_ge(xldA[t % 2], (t // 2 + 1) * 16)
                scalar.activation(sq_s[:], xt2[:, (t % 2) * D:(t % 2) * D + D], AF.Square,
                                  accum_out=sqs8[:, t:t + 1])
                scalar.drain().then_inc(pas, 1)

        xn_s = C.sem("xn_s")
        pb = C.sem("pb")

        @block.sync
        def _(sync):
            sync.wait_ge(pav, NTILE)
            sync.wait_ge(pas, NTILE)
            for t in range(NTILE):
                if t >= 2:
                    sync.wait_ge(xn_s, 2 * (t - 1))
                sync.dma_start(out=xt2[:, (t % 2) * D:(t % 2) * D + D],
                               in_=x1_h.rearrange("(t p) f -> p t f", p=128)[:, t, :]).then_inc(xldB[t % 2], 16)

        @block.vector
        def _(vector):
            vector.wait_ge(pav, NTILE)
            vector.wait_ge(pas, NTILE)
            c = 1.0 / D
            vector.tensor_scalar_mul(m8, sx8, c)
            vector.tensor_scalar_mul(e28, sqs8, c)
            vector.drain()
            vector.tensor_tensor(msq8, m8, m8, op=AluOpType.mult)
            vector.drain()
            vector.tensor_tensor(var8, e28, msq8, op=AluOpType.subtract)
            vector.drain().then_inc(pb, 1)

        @block.scalar
        def _(scalar):
            scalar.wait_ge(pb, 1)
            scalar.wait_ge(ms, 1)
            scalar.activation(lnv8, var8, AF.Ln, bias=eps1[:])
            scalar.drain()
            scalar.activation(rstd8, lnv8, AF.Exp, scale=-0.5)
            scalar.drain().then_inc(pb, 1)

        @block.vector
        def _(vector):
            vector.wait_ge(pb, 2)
            vector.tensor_copy(stst[:, 0:8], m8)
            vector.tensor_copy(stst[:, 8:16], rstd8)
            vector.drain().then_inc(pb, 1)

        @block.gpsimd
        def _(gpsimd):
            gpsimd.wait_ge(pb, 3)
            St(gpsimd.dma_start(out=stats_o, in_=stst[:]))

        tr_s = C.sem("tr_s")
        xe_s = C.sem("xe_s")
        rq_s = C.sem("rq_s")

        @block.vector
        def _(vector):
            vector.wait_ge(pb, 2)
            for t in range(NTILE):
                vector.wait_ge(xldB[t % 2], (t // 2 + 1) * 16)
                vector.tensor_scalar(xnf[:], xt2[:, (t % 2) * D:(t % 2) * D + D],
                                     m8[:, t:t + 1], rstd8[:, t:t + 1],
                                     op0=AluOpType.subtract, op1=AluOpType.mult)
                vector.drain().then_inc(xn_s, 1)
                vector.wait_ge(rq_s, t + 1)
                vector.tensor_scalar(hbf[:], xnf[:], rsr8[:, t:t + 1], None, op0=AluOpType.mult)
                vector.drain().then_inc(xn_s, 1)
                for g in range(2):
                    G = t * 2 + g
                    vector.wait_ge(tr_s, G + 1)
                    vector.tensor_copy(
                        hT[:].rearrange("p (c tt) -> p c tt", tt=TLOC)[:, g * 8:(g + 1) * 8, t * 128:(t + 1) * 128],
                        ptb[:, (G % 2) * 1024:(G % 2) * 1024 + 1024].rearrange("p (c tt) -> p c tt", tt=128)).then_inc(xe_s, 1)

        @block.scalar
        def _(scalar):
            for t in range(NTILE):
                scalar.wait_ge(xn_s, 2 * t + 1)
                scalar.activation(sq_s[:], xnf[:], AF.Square, accum_out=ssr8[:, t:t + 1])
                scalar.drain()
                scalar.activation(lnr8[:, t:t + 1], ssr8[:, t:t + 1], AF.Ln, bias=epsr[:], scale=1.0 / D)
                scalar.drain()
                scalar.activation(rsr8[:, t:t + 1], lnr8[:, t:t + 1], AF.Exp, scale=-0.5)
                scalar.drain().then_inc(rq_s, 1)

        @block.tensor
        def _(tensor):
            tensor.wait_ge(tld, 16)
            for t in range(NTILE):
                tensor.wait_ge(xn_s, 2 * t + 2)
                for g in range(2):
                    G = t * 2 + g
                    if G >= 2:
                        tensor.wait_ge(xe_s, G - 1)
                    for j in range(8):
                        e = g * 8 + j
                        r = tensor.transpose(ptb[:, (G % 2) * 1024 + j * 128:(G % 2) * 1024 + (j + 1) * 128],
                                             hbf[:, e * 128:(e + 1) * 128], ident[:])
                    r.then_inc(tr_s, 1)
        XE_ALL = 16
        ptb_ctx.close()
        nrm = C.ps("nrm", [128, 1024], F32)
        nr2 = C.ps("nr2", [128, 1024], F32)

        lw = [C.sem("lw0"), C.sem("lw1")]
        pm = C.sem("pm")
        se = C.sem("se")
        qs2 = C.sem("qs2")
        nmq = C.sem("nmq")

        @block.gpsimd
        def _(gpsimd):
            gpsimd.wait_ge(ms, 1)
            for i in range(3 * NCH):
                if i >= 2:
                    gpsimd.wait_ge(pm, i - 1)
                gpsimd.dma_start(out=wsv[:, i % 2],
                                 in_=wqkvT_h[:, i * 128:(i + 1) * 128].rearrange("(c p) f -> p c f", p=128)
                                 ).then_inc(lw[i % 2], 16)

        @block.tensor
        def _(tensor):
            tensor.wait_ge(xe_s, XE_ALL)
            for i in range(3 * NCH):
                sect, hc = divmod(i, NCH)
                tensor.wait_ge(lw[i % 2], (i // 2 + 1) * 16)
                if i >= 2:
                    tensor.wait_ge(se, i - 1)
                for half in range(2):
                    bank = pj[:, (i % 2) * 1024 + half * 512:(i % 2) * 1024 + half * 512 + 512]
                    for e in range(NCH):
                        r = tensor.matmul(bank,
                                          wsv[:, i % 2, e],
                                          hT[:, e * TLOC + half * 512: e * TLOC + (half + 1) * 512],
                                          start=(e == 0), stop=(e == NCH - 1))
                r.then_inc(pm, 1)
                if sect < 2:
                    tensor.wait_ge(qs2, i + 1)
                    bank2 = nrm if sect == 0 else nr2
                    for half in range(2):
                        r = tensor.matmul(bank2[0:1, half * 512:half * 512 + 512],
                                          ones_bf[:], sqbf[:, half * 512:(half + 1) * 512],
                                          start=(hc == 0), stop=(hc == NCH - 1))
                    r.then_inc(nmq, 1)

        @block.vector
        def _(vector):
            for i in range(3 * NCH):
                sect, hc = divmod(i, NCH)
                dst = (qsb, ksb, vsb)[sect]
                vector.wait_ge(pm, i + 1)
                vector.tensor_copy(dst[:, hc * TLOC:(hc + 1) * TLOC],
                                   pj[:, (i % 2) * 1024:(i % 2) * 1024 + 1024]).then_inc(se, 1)

        @block.scalar
        def _(scalar):
            for i in range(3 * NCH):
                sect, hc = divmod(i, NCH)
                if sect < 2:
                    scalar.wait_ge(se, i + 1)
                    dst = (qsb, ksb)[sect]
                    scalar.activation(sqbf[:], dst[:, hc * TLOC:(hc + 1) * TLOC], AF.Square).then_inc(qs2, 1)
                else:
                    scalar.drain().then_inc(qs2, 1)

        rw = C.sem("rw")

        @block.vector
        def _(vector):
            vector.wait_ge(nmq, 2 * NCH)
            vector.tensor_copy(nqrow[:], nrm[0:1, :])
            vector.tensor_copy(nkrow[:], nr2[0:1, :])
            vector.drain().then_inc(rw, 1)

        @block.scalar
        def _(scalar):
            scalar.wait_ge(rw, 1)
            scalar.activation(nqrow[:], nqrow[:], AF.Ln, bias=epsr[0:1, :], scale=1.0 / D)
            scalar.activation(nkrow[:], nkrow[:], AF.Ln, bias=epsr[0:1, :], scale=1.0 / D)
            scalar.drain()
            scalar.activation(rqrow[:], nqrow[:], AF.Exp, scale=-0.5)
            scalar.activation(rkrow[:], nkrow[:], AF.Exp, scale=-0.5)
            scalar.drain().then_inc(rw, 1)

        @block.tensor
        def _(tensor):
            tensor.wait_ge(rw, 2)
            for half in range(2):
                tensor.matmul(nrm[:, half * 512:half * 512 + 512], ones_f[:],
                              rqrow[0:1, half * 512:(half + 1) * 512], start=True, stop=True)
                r = tensor.matmul(nr2[:, half * 512:half * 512 + 512], ones_f[:],
                                  rkrow[0:1, half * 512:(half + 1) * 512], start=True, stop=True)
            r.then_inc(rw, 1)

        @block.vector
        def _(vector):
            vector.wait_ge(rw, 3)
            vector.tensor_copy(rqb[:], nrm[:])
            vector.tensor_copy(rkb[:], nr2[:])
            vector.drain().then_inc(rw, 1)

        ck = C.sem("ck")
        cka = C.sem("cka")
        cst_s = [C.sem("cst_s0"), C.sem("cst_s1")]
        sg_s = [C.sem("sg_s0"), C.sem("sg_s1")]

        @block.vector
        def _(vector):
            vector.wait_ge(rw, 4)
            for e in range(NCH):
                if e >= 2:
                    vector.wait_ge(cst_s[e % 2], (e // 2) * 32)
                vector.tensor_tensor(ktmp[:], ksb[:, e * TLOC:(e + 1) * TLOC], rkb[:], op=AluOpType.mult)
                vector.drain().then_inc(ck, 1)
                vector.wait_ge(cka, 2 * e + 1)
                vector.tensor_tensor(wvtmp[:], wtmp[:], vsb[:, e * TLOC:(e + 1) * TLOC], op=AluOpType.mult)
                vector.drain()
                cslot = cst[:, (e % 2) * 2 * TLOC:(e % 2) * 2 * TLOC + 2 * TLOC]
                for b in range(2):
                    sl = slice(b * BLK, (b + 1) * BLK)
                    vector.tensor_tensor_scan(cslot[:, 0:TLOC][:, sl], wtmp[:, sl], wtmp[:, sl], 0.0,
                                              op0=AluOpType.add, op1=AluOpType.bypass)
                    vector.tensor_tensor_scan(cslot[:, TLOC:2 * TLOC][:, sl], wvtmp[:, sl], wvtmp[:, sl], 0.0,
                                              op0=AluOpType.add, op1=AluOpType.bypass)
                vector.drain()
                vector.tensor_copy(tot[:, e:e + 1], cslot[:, 511:512])
                vector.tensor_copy(tot[:, 16 + e:17 + e], cslot[:, 1023:1024])
                vector.tensor_copy(tot[:, 32 + e:33 + e], cslot[:, TLOC + 511:TLOC + 512])
                vector.tensor_copy(tot[:, 48 + e:49 + e], cslot[:, TLOC + 1023:TLOC + 1024])
                vector.drain().then_inc(ck, 1)
                vector.tensor_tensor(qtmp[:], qsb[:, e * TLOC:(e + 1) * TLOC], rqb[:], op=AluOpType.mult)
                vector.drain().then_inc(ck, 1)

        @block.scalar
        def _(scalar):
            for e in range(NCH):
                scalar.wait_ge(ck, 3 * e + 1)
                scalar.activation(wtmp[:], ktmp[:], AF.Exp).then_inc(cka, 1)
                scalar.wait_ge(ck, 3 * e + 3)
                if e >= 2:
                    scalar.wait_ge(sg_s[e % 2], (e // 2) * 16)
                scalar.activation(sgst[:, (e % 2) * TLOC:(e % 2) * TLOC + TLOC], qtmp[:], AF.Sigmoid).then_inc(cka, 1)

        @block.gpsimd
        def _(gpsimd):
            for e in range(NCH):
                gpsimd.wait_ge(ck, 3 * e + 2)
                cslot = cst[:, (e % 2) * 2 * TLOC:(e % 2) * 2 * TLOC + 2 * TLOC]
                gpsimd.dma_start(out=wcum_o[e * 128:(e + 1) * 128, :], in_=cslot[:, 0:TLOC]).then_inc(cst_s[e % 2], 16)
                gpsimd.dma_start(out=kvcum_o[e * 128:(e + 1) * 128, :], in_=cslot[:, TLOC:2 * TLOC]).then_inc(cst_s[e % 2], 16)
                gpsimd.wait_ge(cka, 2 * e + 2)
                gpsimd.dma_start(out=sig_o[e * 128:(e + 1) * 128, :],
                                 in_=sgst[:, (e % 2) * TLOC:(e % 2) * TLOC + TLOC]).then_inc(sg_s[e % 2], 16)
            gpsimd.wait_ge(ck, 3 * NCH)
            St(gpsimd.dma_start(out=tot_o, in_=tot[:]))
            for par in range(2):
                gpsimd.wait_ge(cst_s[par], NCH * 16)
                gpsimd.wait_ge(sg_s[par], NCH * 8)
            gpsimd.wait_ge(stm, n_st[0])
            gpsimd.sem_inc(phase, 1)



def emit_l1_b(nc, block, phase, pgate, ios, prefix, semstack):
    x1_h = ios["x1"]
    stats_h = ios["stats"]
    sig_h = ios["sig"]
    wcum_h = ios["wcum"]
    kvcum_h = ios["kvcum"]
    tots_h = ios["tots"]
    selm_h = ios["selm"]
    wswT_h = ios["wswT"]
    woutT_h = ios["woutT"]
    s_o = ios["s_o"]
    with contextlib.ExitStack() as ectx:
        C = Ctx(nc, ectx, prefix, semstack)
        tots = C.sb("tots_sb", [128, NCORE * 64], F32)
        selm = C.sb("selm_sb", [128, 32], F32)
        carry = C.sb("carry", [128, 64], F32)
        stt8 = C.sb("stt8", [128, 16], F32)
        cch = C.sb("cch", [128, 2 * 2 * TLOC], F32)
        sgch = C.sb("sgch", [128, 2 * TLOC], BF16)
        wg_t = C.sb("wg_t", [128, TLOC], F32)
        kvg = C.sb("kvg", [128, TLOC], F32)
        rec = C.sb("rec", [128, TLOC], F32)
        yT = C.sb("yT", [128, NCH * TLOC], BF16)
        zbuf = C.sb("zbuf", [128, 64 * 512], BF16)
        zbv = zbuf[:].rearrange("p (c f) -> p c f", f=512)
        swsl = C.sb("swsl", [128, 4 * NCH * 128], BF16)
        swv = swsl[:].rearrange("p (b c f) -> p b c f", b=4, f=128)
        wosl = C.sb("wosl", [128, 2 * 512], BF16)
        wov = wosl[:].rearrange("p (b f) -> p b f", b=2)
        silt = C.sb("silt", [128, 2 * 512], F32)
        x1t = C.sb("x1t", [128, 4 * D], F32)
        xnsl = C.sb("xnsl", [128, 2 * 512], F32)
        ssl = C.sb("ssl", [128, 2 * 512], F32)

        pj = C.ps("pj", [128, 1024], F32)
        po = C.ps("po", [128, 2048], F32)

        ld0 = C.sem("ld0")

        @block.sync
        def _(sync):
            sync.dma_start(out=tots[:].rearrange("p (r c) -> p r c", c=64),
                           in_=tots_h.rearrange("(r p) c -> p r c", p=128)).then_inc(ld0, 16)
            sync.wait_ge(ld0, 16)
            sync.dma_start(out=selm[:], in_=selm_h).then_inc(ld0, 16)
            sync.wait_ge(ld0, 32)
            sync.dma_start(out=stt8[:], in_=stats_h).then_inc(ld0, 16)

        cr = C.sem("cr")

        @block.vector
        def _(vector):
            vector.wait_ge(ld0, 48)
            vector.memset(carry[:], 0.0)
            vector.drain()
            for b in range(2):
                for qty in range(2):
                    for gblk in range(NBLK):
                        corec = gblk if gblk < 8 else 15 - gblk
                        blki = 0 if gblk < 8 else 1
                        src0 = tots[:, corec * 64 + qty * 32 + blki * 16: corec * 64 + qty * 32 + blki * 16 + 16]
                        vector.scalar_tensor_tensor(
                            carry[:, qty * 32 + b * 16: qty * 32 + b * 16 + 16],
                            src0, selm[:, b * 16 + gblk: b * 16 + gblk + 1],
                            carry[:, qty * 32 + b * 16: qty * 32 + b * 16 + 16],
                            op0=AluOpType.mult, op1=AluOpType.add)
                        vector.drain()
            vector.drain().then_inc(cr, 1)

        ych = [C.sem("ych0"), C.sem("ych1")]
        yc = C.sem("yc")

        @block.sync
        def _(sync):
            for e in range(NCH):
                if e >= 2:
                    sync.wait_ge(yc, e - 1)
                sync.dma_start(out=cch[:, (e % 2) * 2 * TLOC:(e % 2) * 2 * TLOC + TLOC],
                               in_=wcum_h[e * 128:(e + 1) * 128, :]).then_inc(ych[e % 2], 16)
                sync.dma_start(out=cch[:, (e % 2) * 2 * TLOC + TLOC:(e % 2) * 2 * TLOC + 2 * TLOC],
                               in_=kvcum_h[e * 128:(e + 1) * 128, :]).then_inc(ych[e % 2], 16)
                sync.dma_start(out=sgch[:, (e % 2) * TLOC:(e % 2) * TLOC + TLOC],
                               in_=sig_h[e * 128:(e + 1) * 128, :]).then_inc(ych[e % 2], 16)

        @block.vector
        def _(vector):
            vector.wait_ge(cr, 1)
            for e in range(NCH):
                vector.wait_ge(ych[e % 2], (e // 2 + 1) * 48)
                cslot = cch[:, (e % 2) * 2 * TLOC:(e % 2) * 2 * TLOC + 2 * TLOC]
                for b in range(2):
                    sl = slice(b * BLK, (b + 1) * BLK)
                    vector.tensor_scalar(wg_t[:, sl], cslot[:, 0:TLOC][:, sl],
                                         carry[:, 0 * 32 + b * 16 + e:0 * 32 + b * 16 + e + 1], 1e-6,
                                         op0=AluOpType.add, op1=AluOpType.add)
                    vector.tensor_scalar(kvg[:, sl], cslot[:, TLOC:2 * TLOC][:, sl],
                                         carry[:, 1 * 32 + b * 16 + e:1 * 32 + b * 16 + e + 1], None,
                                         op0=AluOpType.add)
                vector.drain()
                vector.reciprocal(rec[:], wg_t[:])
                vector.drain()
                vector.tensor_tensor(kvg[:], kvg[:], rec[:], op=AluOpType.mult)
                vector.drain()
                vector.tensor_tensor(yT[:, e * TLOC:(e + 1) * TLOC], kvg[:],
                                     sgch[:, (e % 2) * TLOC:(e % 2) * TLOC + TLOC], op=AluOpType.mult)
                vector.drain().then_inc(yc, 1)
        YC_ALL = NCH

        swl = [C.sem("swl0"), C.sem("swl1")]
        wol = [C.sem("wol0"), C.sem("wol1")]
        zm = C.sem("zm")
        zs = C.sem("zs")
        zv = C.sem("zv")
        om = C.sem("om")
        oe = C.sem("oe")
        xls = C.sem("xls")
        sso = [C.sem("sso0"), C.sem("sso1")]

        @block.gpsimd
        def _(gpsimd):
            for tp in range(2):
                for zc in range(64):
                    i = tp * 64 + zc
                    if i >= 2:
                        gpsimd.wait_ge(zm, i - 1)
                    gpsimd.dma_start(out=swv[:, (i % 2) * 2],
                                     in_=wswT_h[:, zc * 128:(zc + 1) * 128].rearrange("(c p) f -> p c f", p=128)
                                     ).then_inc(swl[i % 2], 16)
                    gpsimd.dma_start(out=swv[:, (i % 2) * 2 + 1],
                                     in_=wswT_h[:, 8192 + zc * 128:8192 + (zc + 1) * 128].rearrange("(c p) f -> p c f", p=128)
                                     ).then_inc(swl[i % 2], 16)
                for fs in range(4):
                    for zc in range(64):
                        i2 = tp * 256 + fs * 64 + zc
                        if i2 >= 2:
                            gpsimd.wait_ge(om, i2 - 1)
                        gpsimd.dma_start(out=wov[:, i2 % 2],
                                         in_=woutT_h[zc * 128:(zc + 1) * 128, fs * 512:(fs + 1) * 512]
                                         .rearrange("(c p) f -> p c f", p=128)[:, 0, :] if False else
                                         woutT_h[zc * 128:(zc + 1) * 128, fs * 512:(fs + 1) * 512]
                                         ).then_inc(wol[i2 % 2], 16)

        @block.tensor
        def _(tensor):
            tensor.wait_ge(yc, YC_ALL)
            for tp in range(2):
                for zc in range(64):
                    i = tp * 64 + zc
                    tensor.wait_ge(swl[i % 2], (i // 2 + 1) * 32)
                    if i >= 1:
                        tensor.wait_ge(zv, i)
                        tensor.wait_ge(zs, i)
                    if tp == 1 and zc == 0:
                        tensor.wait_ge(oe, 16)
                    for e in range(NCH):
                        tensor.matmul(pj[:, 0:512], swv[:, (i % 2) * 2, e],
                                      yT[:, e * TLOC + tp * 512: e * TLOC + tp * 512 + 512],
                                      start=(e == 0), stop=(e == NCH - 1))
                    for e in range(NCH):
                        r = tensor.matmul(pj[:, 512:1024], swv[:, (i % 2) * 2 + 1, e],
                                          yT[:, e * TLOC + tp * 512: e * TLOC + tp * 512 + 512],
                                          start=(e == 0), stop=(e == NCH - 1))
                    r.then_inc(zm, 1)
                for fs in range(4):
                    for zc in range(64):
                        i2 = tp * 256 + fs * 64 + zc
                        tensor.wait_ge(wol[i2 % 2], (i2 // 2 + 1) * 16)
                        if zc == 0:
                            tensor.wait_ge(zv, tp * 64 + 64)
                            if fs > 0 or tp > 0:
                                tensor.wait_ge(oe, (tp * 4 + fs) * 4)
                        for t4 in range(4):
                            r = tensor.matmul(po[:, t4 * 512:(t4 + 1) * 512],
                                              zbv[:, zc, t4 * 128:(t4 + 1) * 128],
                                              wov[:, i2 % 2],
                                              start=(zc == 0), stop=(zc == 63))
                        r.then_inc(om, 1)

        @block.scalar
        def _(scalar):
            for tp in range(2):
                for zc in range(64):
                    i = tp * 64 + zc
                    scalar.wait_ge(zm, i + 1)
                    if i >= 2:
                        scalar.wait_ge(zv, i - 1)
                    scalar.activation(silt[:, (i % 2) * 512:(i % 2) * 512 + 512], pj[:, 512:1024], AF.Silu).then_inc(zs, 1)

        @block.sync
        def _(sync):
            for tp in range(2):
                for t4 in range(4):
                    tg = tp * 4 + t4
                    if tg >= 4:
                        sync.wait_ge(oe, tg + 9)
                    sync.dma_start(out=x1t[:, t4 * D:(t4 + 1) * D],
                                   in_=x1_h.rearrange("(t p) f -> p t f", p=128)[:, tg, :]).then_inc(xls, 16)
                for fs in range(4):
                    for t4 in range(4):
                        tg = tp * 4 + t4
                        idx = tp * 16 + fs * 4 + t4
                        j = (fs * 4 + t4) % 2
                        sync.wait_ge(oe, idx + 1)
                        sync.dma_start(
                            out=s_o.rearrange("(t p) f -> p t f", p=128)[:, tg, fs * 512:(fs + 1) * 512],
                            in_=ssl[:, j * 512:(j + 1) * 512]).then_inc(sso[idx % 2], 16)
            sync.wait_ge(sso[0], 16 * 16)
            sync.wait_ge(sso[1], 16 * 16)
            sync.sem_inc(phase, 1)

        @block.vector
        def _(vector):
            for tp in range(2):
                for zc in range(64):
                    i = tp * 64 + zc
                    vector.wait_ge(zs, i + 1)
                    if tp == 1 and zc == 0:
                        vector.wait_ge(om, 256)
                    vector.tensor_tensor(zbv[:, zc, :], pj[:, 0:512],
                                         silt[:, (i % 2) * 512:(i % 2) * 512 + 512], op=AluOpType.mult).then_inc(zv, 1)
                for fs in range(4):
                    for t4 in range(4):
                        tg = tp * 4 + t4
                        vector.wait_ge(om, tp * 256 + fs * 64 + 64)
                        vector.wait_ge(xls, (tp + 1) * 64)
                        j = (fs * 4 + t4) % 2
                        idx2 = tp * 16 + fs * 4 + t4
                        if idx2 >= 2:
                            vector.wait_ge(sso[idx2 % 2], (idx2 // 2) * 16)
                        xsl_ = xnsl[:, j * 512:(j + 1) * 512]
                        vector.tensor_scalar(xsl_, x1t[:, t4 * D + fs * 512: t4 * D + fs * 512 + 512],
                                             stt8[:, tg:tg + 1], stt8[:, 8 + tg:9 + tg],
                                             op0=AluOpType.subtract, op1=AluOpType.mult)
                        vector.drain()
                        vector.tensor_tensor(ssl[:, j * 512:(j + 1) * 512],
                                             po[:, t4 * 512:(t4 + 1) * 512], xsl_, op=AluOpType.add).then_inc(oe, 1)




def emit_l1_c(nc, block, phase, pgate, ios, prefix, semstack):
    x1_h = ios["x1"]
    stats_h = ios["stats"]
    s_h = ios["s"]
    WT_h = ios["WT"]
    b_h = ios["b"]
    lAT_h = ios["lAT"]
    lBT_h = ios["lBT"]
    gwT_h = ios["gwT"]
    gb_h = ios["gb"]
    ident_h = ios["ident"]
    x2_o = ios["x2_o"]
    with contextlib.ExitStack() as ectx:
        C = Ctx(nc, ectx, prefix, semstack)
        stt8 = C.sb("stt8", [128, 16], F32)
        xt2 = C.sb("xt2", [128, 2 * D], F32)
        xnbf = C.sb("xnbf", [128, D], BF16)
        xnT = C.sb("xnT", [128, NCH * TLOC], BF16)
        ident = C.sb("ident_sb", [128, 128], BF16)
        ones_bf = C.sb("ones_bf", [128, 1], BF16)
        ones_rbf = C.sb("ones_rbf", [1, 128], BF16)
        lATs = C.sb("lATs", [128, NCH * 128], BF16)
        lATv = lATs[:].rearrange("p (c f) -> p c f", f=128)
        lBTs = C.sb("lBTs", [128, D], BF16)
        gws = C.sb("gws", [128, NCH * 8], BF16)
        gwv = gws[:].rearrange("p (c f) -> p c f", f=8)
        gbr = C.sb("gbr", [1, 8], BF16)
        br = C.sb("br", [1, D], BF16)
        WTsl = C.sb("WTsl", [128, 2 * NCH * 512], BF16)
        WTv = WTsl[:].rearrange("p (b c f) -> p b c f", b=2, f=512)
        lg8 = C.sb("lg8", [128, 8 * 8], F32)
        t2_8 = C.sb("t2_8", [128, 8 * 8], F32)
        gx = C.sb("gx", [128, 6 * 8], F32)
        m1 = gx[:, 0:8]; m2 = gx[:, 8:16]; dd = gx[:, 16:24]
        g1 = gx[:, 24:32]; g2 = gx[:, 32:40]; mneg = gx[:, 40:48]
        wgt8 = C.sb("wgt8", [128, 8 * 8], F32)
        w2t = C.sb("w2t", [128, 8], F32)
        xa = C.sb("xa", [128, 128], F32)
        xawbf = C.sb("xawbf", [128, 128], BF16)
        xawT = C.sb("xawT", [128, NTILE * 128], BF16)
        gsl = C.sb("gsl", [128, 2 * 512], BF16)
        x1sl = C.sb("x1sl", [128, 2 * 512], F32)
        s_sl = C.sb("s_sl", [128, 2 * 512], F32)
        x2sl = C.sb("x2sl", [128, 2 * 512], F32)

        pj = C.ps("pj", [128, 1024], F32)
        plg = C.ps("plg", [128, 512], F32)
        ptb = C.ps("ptb", [128, 2048], BF16)

        ld0 = C.sem("ld0")
        xld = [C.sem("xld0"), C.sem("xld1")]
        ms = C.sem("ms")

        gld = C.sem("gld")
        xn_s = C.sem("xn_s")
        tr_s = C.sem("tr_s")
        xe_s = C.sem("xe_s")
        lgm = C.sem("lgm")
        pp = C.sem("pp")
        ppa = C.sem("ppa")
        xtr = C.sem("xtr")

        @block.sync
        def _(sync):
            sync.dma_start(out=stt8[:], in_=stats_h).then_inc(ld0, 16)
            sync.wait_ge(ld0, 16)
            sync.dma_start(out=ident[:], in_=ident_h).then_inc(ld0, 16)
            for t in range(NTILE):
                if t >= 2:
                    sync.wait_ge(xn_s, 2 * t - 3)
                sync.dma_start(out=xt2[:, (t % 2) * D:(t % 2) * D + D],
                               in_=x1_h.rearrange("(t p) f -> p t f", p=128)[:, t, :]).then_inc(xld[t % 2], 16)
        LD0 = 32

        @block.gpsimd
        def _(gpsimd):
            gpsimd.memset(ones_bf[:], 1.0)
            gpsimd.memset(ones_rbf[:], 1.0)
            gpsimd.drain().then_inc(ms, 1)
            for i, (dst, srcap) in enumerate([
                (lATv, lAT_h.rearrange("(c p) f -> p c f", p=128)),
                (lBTs[:], lBT_h),
                (gwv, gwT_h.rearrange("(c p) f -> p c f", p=128)),
                (gbr[:], gb_h),
                (br[:], b_h),
            ]):
                gpsimd.wait_ge(gld, i * 16)
                gpsimd.dma_start(out=dst, in_=srcap).then_inc(gld, 16)

        @block.vector
        def _(vector):
            vector.wait_ge(ld0, LD0)
            vector.wait_ge(gld, 80)
            for t in range(NTILE):
                vector.wait_ge(xld[t % 2], (t // 2 + 1) * 16)
                vector.tensor_scalar(xnbf[:], xt2[:, (t % 2) * D:(t % 2) * D + D],
                                     stt8[:, t:t + 1], stt8[:, 8 + t:9 + t],
                                     op0=AluOpType.subtract, op1=AluOpType.mult).then_inc(xn_s, 1)
                for g in range(2):
                    G = t * 2 + g
                    vector.wait_ge(tr_s, G + 1)
                    vector.tensor_copy(
                        xnT[:].rearrange("p (c tt) -> p c tt", tt=TLOC)[:, g * 8:(g + 1) * 8, t * 128:(t + 1) * 128],
                        ptb[:, (G % 2) * 1024:(G % 2) * 1024 + 1024].rearrange("p (c tt) -> p c tt", tt=128)).then_inc(xe_s, 1)
                vector.wait_ge(lgm, t + 1)
                vector.tensor_copy(lg8[:, t * 8:(t + 1) * 8], plg[:, t * 16:t * 16 + 8])
                vector.tensor_copy(xa[:], pj[:, 0:128])
                vector.drain()
                vector.reduce_max(m1[:, t:t + 1], lg8[:, t * 8:(t + 1) * 8], axis=AX)
                vector.drain()
                vector.tensor_scalar(t2_8[:, t * 8:(t + 1) * 8], lg8[:, t * 8:(t + 1) * 8],
                                     m1[:, t:t + 1], -1e30, op0=AluOpType.is_equal, op1=AluOpType.mult)
                vector.drain()
                vector.tensor_tensor(t2_8[:, t * 8:(t + 1) * 8], t2_8[:, t * 8:(t + 1) * 8],
                                     lg8[:, t * 8:(t + 1) * 8], op=AluOpType.add)
                vector.drain()
                vector.reduce_max(m2[:, t:t + 1], t2_8[:, t * 8:(t + 1) * 8], axis=AX)
                vector.drain()
                vector.tensor_tensor(dd[:, t:t + 1], m1[:, t:t + 1], m2[:, t:t + 1], op=AluOpType.subtract)
                vector.drain().then_inc(pp, 1)
                vector.wait_ge(ppa, t + 1)
                vector.tensor_scalar(g2[:, t:t + 1], g1[:, t:t + 1], -1.0, 1.0,
                                     op0=AluOpType.mult, op1=AluOpType.add)
                vector.drain()
                vector.tensor_scalar(wgt8[:, t * 8:(t + 1) * 8], lg8[:, t * 8:(t + 1) * 8],
                                     m1[:, t:t + 1], g1[:, t:t + 1], op0=AluOpType.is_equal, op1=AluOpType.mult)
                vector.tensor_scalar(w2t[:], t2_8[:, t * 8:(t + 1) * 8],
                                     m2[:, t:t + 1], g2[:, t:t + 1], op0=AluOpType.is_equal, op1=AluOpType.mult)
                vector.drain()
                vector.tensor_tensor(wgt8[:, t * 8:(t + 1) * 8], wgt8[:, t * 8:(t + 1) * 8],
                                     w2t[:], op=AluOpType.add)
                vector.drain()
                vector.tensor_tensor(xawbf[:].rearrange("p (e r) -> p e r", r=16),
                                     xa[:].rearrange("p (e r) -> p e r", r=16),
                                     wgt8[:, t * 8:(t + 1) * 8].rearrange("p (e o) -> p e o", o=1)
                                     .broadcast_to((128, 8, 16)),
                                     op=AluOpType.mult)
                vector.drain().then_inc(xn_s, 1)
                vector.wait_ge(xtr, t + 1)
                vector.tensor_copy(xawT[:, t * 128:(t + 1) * 128], ptb[:, 0:128])
                vector.drain().then_inc(xe_s, 1)

        @block.scalar
        def _(scalar):
            for t in range(NTILE):
                scalar.wait_ge(pp, t + 1)
                scalar.activation(g1[:, t:t + 1], dd[:, t:t + 1], AF.Sigmoid)
                scalar.drain().then_inc(ppa, 1)

        @block.tensor
        def _(tensor):
            tensor.wait_ge(ld0, LD0)
            tensor.wait_ge(gld, 80)
            tensor.wait_ge(ms, 1)
            for t in range(NTILE):
                tensor.wait_ge(xn_s, 2 * t + 1)
                for g in range(2):
                    G = t * 2 + g
                    if G >= 2:
                        tensor.wait_ge(xe_s, 3 * (G // 2) if G % 2 == 0 else 3 * ((G - 2) // 2) + 2)
                    for j in range(8):
                        e = g * 8 + j
                        r = tensor.transpose(ptb[:, (G % 2) * 1024 + j * 128:(G % 2) * 1024 + (j + 1) * 128],
                                             xnbf[:, e * 128:(e + 1) * 128], ident[:])
                    r.then_inc(tr_s, 1)
                tensor.wait_ge(xe_s, 3 * t + 2)
                for e in range(NCH):
                    tensor.matmul(plg[:, t * 16:t * 16 + 8],
                                  xnT[:, e * TLOC + t * 128:e * TLOC + (t + 1) * 128],
                                  gwv[:, e], start=(e == 0), stop=False)
                tensor.matmul(plg[:, t * 16:t * 16 + 8], ones_rbf[:], gbr[:], start=False, stop=True)
                for e in range(NCH):
                    r = tensor.matmul(pj[:, 0:128],
                                      xnT[:, e * TLOC + t * 128:e * TLOC + (t + 1) * 128],
                                      lATv[:, e], start=(e == 0), stop=(e == NCH - 1))
                r.then_inc(lgm, 1)
                tensor.wait_ge(xn_s, 2 * t + 2)
                tensor.transpose(ptb[:, 0:128], xawbf[:], ident[:]).then_inc(xtr, 1)
        XE_ALL = 3 * NTILE

        wl = [C.sem("wl0"), C.sem("wl1")]
        xsl2 = [C.sem("xsl20"), C.sem("xsl21")]
        fm = C.sem("fm")
        fsl = C.sem("fsl")
        cmb = C.sem("cmb")
        sto = [C.sem("sto0"), C.sem("sto1")]

        @block.gpsimd
        def _(gpsimd):
            gpsimd.wait_ge(ms, 1)
            for fs in range(4):
                if fs >= 2:
                    gpsimd.wait_ge(fm, (fs - 1) * 8)
                gpsimd.dma_start(out=WTv[:, fs % 2],
                                 in_=WT_h[:, fs * 512:(fs + 1) * 512].rearrange("(c p) f -> p c f", p=128)
                                 ).then_inc(wl[fs % 2], 16)

        @block.sync
        def _(sync):
            for idx in range(34):
                if idx < 32:
                    fs, t = divmod(idx, 8)
                    if idx >= 2:
                        sync.wait_ge(cmb, idx - 1)
                    sync.dma_start(out=x1sl[:, (idx % 2) * 512:(idx % 2) * 512 + 512],
                                   in_=x1_h.rearrange("(t p) f -> p t f", p=128)[:, t, fs * 512:(fs + 1) * 512]
                                   ).then_inc(xsl2[idx % 2], 16)
                    sync.dma_start(out=s_sl[:, (idx % 2) * 512:(idx % 2) * 512 + 512],
                                   in_=s_h.rearrange("(t p) f -> p t f", p=128)[:, t, fs * 512:(fs + 1) * 512]
                                   ).then_inc(xsl2[idx % 2], 16)
                if idx >= 2:
                    j2 = idx - 2
                    fs2, t2 = divmod(j2, 8)
                    sync.wait_ge(cmb, j2 + 1)
                    sync.dma_start(
                        out=x2_o.rearrange("(t p) f -> p t f", p=128)[:, t2, fs2 * 512:(fs2 + 1) * 512],
                        in_=x2sl[:, (j2 % 2) * 512:(j2 % 2) * 512 + 512]).then_inc(sto[j2 % 2], 16)
            sync.wait_ge(sto[0], 16 * 16)
            sync.wait_ge(sto[1], 16 * 16)
            sync.sem_inc(phase, 1)

        XLD_BASE = [0]

        @block.tensor
        def _(tensor):
            tensor.wait_ge(xe_s, XE_ALL)
            tensor.wait_ge(xtr, NTILE)
            for fs in range(4):
                tensor.wait_ge(wl[fs % 2], (fs // 2 + 1) * 16)
                for t in range(NTILE):
                    idx = fs * 8 + t
                    if idx >= 2:
                        tensor.wait_ge(fsl, idx - 1)
                    bank = pj[:, (idx % 2) * 512:(idx % 2) * 512 + 512]
                    for e in range(NCH):
                        tensor.matmul(bank, xnT[:, e * TLOC + t * 128:e * TLOC + (t + 1) * 128],
                                      WTv[:, fs % 2, e], start=(e == 0), stop=False)
                    tensor.matmul(bank, ones_rbf[:], br[0:1, fs * 512:(fs + 1) * 512], start=False, stop=False)
                    r = tensor.matmul(bank, xawT[:, t * 128:(t + 1) * 128],
                                      lBTs[:, fs * 512:(fs + 1) * 512], start=False, stop=True)
                    r.then_inc(fm, 1)

        @block.scalar
        def _(scalar):
            for fs in range(4):
                for t in range(NTILE):
                    idx = fs * 8 + t
                    scalar.wait_ge(fm, idx + 1)
                    if idx >= 2:
                        scalar.wait_ge(cmb, idx - 1)
                    bank = pj[:, (idx % 2) * 512:(idx % 2) * 512 + 512]
                    scalar.activation(gsl[:, (idx % 2) * 512:(idx % 2) * 512 + 512], bank, AF.Silu).then_inc(fsl, 1)

        @block.vector
        def _(vector):
            for fs in range(4):
                for t in range(NTILE):
                    idx = fs * 8 + t
                    vector.wait_ge(fsl, idx + 1)
                    vector.wait_ge(xsl2[idx % 2], (idx // 2 + 1) * 32)
                    if idx >= 2:
                        vector.wait_ge(sto[idx % 2], (idx // 2) * 16)
                    j = idx % 2
                    vector.tensor_tensor(x2sl[:, j * 512:(j + 1) * 512],
                                         s_sl[:, j * 512:(j + 1) * 512],
                                         gsl[:, j * 512:(j + 1) * 512], op=AluOpType.mult)
                    vector.drain()
                    vector.tensor_tensor(x2sl[:, j * 512:(j + 1) * 512],
                                         x2sl[:, j * 512:(j + 1) * 512],
                                         x1sl[:, j * 512:(j + 1) * 512], op=AluOpType.add).then_inc(cmb, 1)




def _prep_core_inputs_l1(inputs, i):
    rows = _global_rows(i)
    x = np.ascontiguousarray(np.asarray(inputs["L0_attn_x"], dtype=np.float32).reshape(T, D)[rows])
    cosq, sinq = _rope_tables(rows, RS)
    cosk, sink = _rope_tables(rows, 1.0)
    return {
        "x": x,
        "wqT": np.ascontiguousarray(np.asarray(inputs["L0_attn_wq"], np.float32).T),
        "wkT": np.ascontiguousarray(np.asarray(inputs["L0_attn_wk"], np.float32).T),
        "wvT": np.ascontiguousarray(np.asarray(inputs["L0_attn_wv"], np.float32).T),
        "wqkvT": np.ascontiguousarray(np.asarray(inputs["L0_ffn_W_qkv"], np.float32).T),
        "wgT": np.ascontiguousarray(np.asarray(inputs["L0_ffn_W_gate"], np.float32).T),
        "cosq": cosq, "sinq": sinq, "cosk": cosk, "sink": sink,
        "ident": np.eye(128, dtype=BF),
    }


def run_l0_pre(inputs):
    nc1 = build_l0_pre()
    in_maps = [_prep_core_inputs_l1(inputs, i) for i in range(NCORE)]
    return run_bass_kernel_spmd(nc1, in_maps, core_ids=list(range(NCORE)))


def _prep_l2(inputs, r1):
    f32 = np.float32
    kTf = np.empty((D, T), BF)
    vf = np.empty((T, NH * 129), BF)
    kshTf = np.empty((128, T), BF)
    vshf = np.empty((T, 132), BF)
    k2ma = np.empty((NH, NCORE), f32)
    k2shma = np.empty((1, NCORE), f32)
    for c in range(NCORE):
        rows = _global_rows(c)
        kTf[:, rows] = np.asarray(r1[c]["kT_o"])
        vf[rows, :] = np.asarray(r1[c]["v_o"])
        kshTf[:, rows] = np.asarray(r1[c]["kshT_o"])
        vshf[rows, :] = np.asarray(r1[c]["vsh_o"])
        k2ma[:, c] = np.asarray(r1[c]["k2m_o"])[:, 0]
        k2shma[0, c] = np.asarray(r1[c]["k2shm_o"])[0, 0]
    woT = np.ascontiguousarray(np.asarray(inputs["L0_attn_wo"], f32).T)
    maskT = np.triu(np.ones((128, 128), f32)).astype(BF)
    ident = np.eye(128, dtype=BF)
    x = np.asarray(inputs["L0_attn_x"], f32).reshape(T, D)
    maps = []
    for c in range(NCORE):
        rows = _global_rows(c)
        bg = _core_blocks(c)
        pm = np.zeros((2, NH, 2, KSLOT), f32)
        pm[:, :, :, 0] = 1.0
        for g in range(2):
            for kb in range(16):
                if kb < bg[g]:
                    pm[:, :, g, 1 + kb] = 1.0
        maps.append({
            "x": np.ascontiguousarray(x[rows]),
            "qT": np.asarray(r1[c]["qT_o"]),
            "q2T": np.asarray(r1[c]["q2T_o"]),
            "kTl": np.asarray(r1[c]["kT_o"]),
            "vl": np.asarray(r1[c]["v_o"]),
            "kshTl": np.asarray(r1[c]["kshT_o"]),
            "vshl": np.asarray(r1[c]["vsh_o"]),
            "kTf": kTf, "vf": vf, "kshTf": kshTf, "vshf": vshf,
            "qn2": np.asarray(r1[c]["qn2_o"]),
            "q2n2": np.asarray(r1[c]["q2n2_o"]),
            "k2ma": k2ma, "k2shma": k2shma,
            "pm": np.ascontiguousarray(pm.reshape(1, -1)),
            "woT": woT, "maskT": maskT, "ident": ident,
        })
    return maps


def _prep_l4(inputs, r3):
    f32 = np.float32
    tots = np.concatenate([np.asarray(r3[c]["tot_o"]) for c in range(NCORE)], axis=1)
    tots = np.ascontiguousarray(tots)
    wswT = np.ascontiguousarray(np.asarray(inputs["L1_attn_w_swiglu"], f32).T)
    woutT = np.ascontiguousarray(np.asarray(inputs["L1_attn_w_out"], f32).T)
    maps = []
    for c in range(NCORE):
        bg = _core_blocks(c)
        selm = np.zeros((2, NBLK), f32)
        for b in range(2):
            selm[b, :bg[b]] = 1.0
        selm = np.broadcast_to(selm.reshape(1, 32), (128, 32)).copy()
        maps.append(dict(tots=tots, selm=selm, wswT=wswT, woutT=woutT))
    return maps


def _prep_l5(inputs):
    f32 = np.float32
    lA = np.asarray(inputs["L1_ffn_lora_A"], f32).reshape(128, D)
    lB = np.asarray(inputs["L1_ffn_lora_B"], f32)
    return {
        "WT": np.ascontiguousarray(np.asarray(inputs["L1_ffn_W"], f32).T),
        "b": np.asarray(inputs["L1_ffn_b"], f32).reshape(1, D),
        "lAT": np.ascontiguousarray(lA.T),
        "lBT": np.ascontiguousarray(np.transpose(lB, (0, 2, 1)).reshape(128, D)),
        "gwT": np.ascontiguousarray(np.asarray(inputs["L1_ffn_gate_w"], f32).T),
        "gb": np.asarray(inputs["L1_ffn_gate_b"], f32).reshape(1, 8),
        "ident": np.eye(128, dtype=BF),
    }


_old_kernel_removed = None



SIM_LOCAL_SHARED = False


def build_fused():
    nc = bass.Bass("TRN2", target_bir_lowering=False, debug=False)
    E = lambda n, s, d: nc.dram_tensor(n, s, d, kind="ExternalInput").ap()
    I = lambda n, s, d: nc.dram_tensor(n, s, d, kind="Internal").ap()
    aspace = "Local" if SIM_LOCAL_SHARED else "Shared"
    S = lambda n, s, d: nc.dram_tensor(n, s, d, kind="Internal", addr_space=aspace).ap()

    x = E("x", [TLOC, D], F32)
    wqT = E("wqT", [D, D], F32); wkT = E("wkT", [D, D], F32); wvT = E("wvT", [D, D], F32)
    wqkvT0 = E("wqkvT0", [D, 3 * D], F32); wgT = E("wgT", [D, 16], F32)
    cosq = E("cosq", [128, TLOC], F32); sinq = E("sinq", [128, TLOC], F32)
    cosk = E("cosk", [128, TLOC], F32); sink = E("sink", [128, TLOC], F32)
    ident = E("ident", [128, 128], BF16)
    pm = E("pm", [1, 2 * NH * 2 * KSLOT], F32)
    woT = E("woT", [D, D], F32)
    maskT = E("maskT", [128, 128], BF16)
    wqkvT1 = E("wqkvT1", [D, 3 * D], F32)
    selm = E("selm", [128, 32], F32)
    wswT = E("wswT", [D, 16384], F32)
    woutT = E("woutT", [8192, D], F32)
    WT = E("WT", [D, D], F32); bb = E("b", [1, D], F32)
    lAT = E("lAT", [D, 128], F32); lBT = E("lBT", [128, D], F32)
    gwT = E("gwT", [D, 8], F32); gb = E("gb", [1, 8], F32)
    x2_o = nc.dram_tensor("x2_o", [TLOC, D], F32, kind="ExternalOutput").ap()

    qT_d = I("qT_d", [D, TLOC], BF16); kT_d = I("kT_d", [D, TLOC], BF16)
    v_d = I("v_d", [TLOC, NH * 129], BF16); q2T_d = I("q2T_d", [D, TLOC], BF16)
    kshT_d = I("kshT_d", [128, TLOC], BF16); vsh_d = I("vsh_d", [TLOC, 132], BF16)
    qn2_d = I("qn2_d", [NH, TLOC], F32); q2n2_d = I("q2n2_d", [NH, TLOC], F32)
    kn_d = I("kn_d", [17, 1], F32)
    knorm_d = I("knorm_d", [NH, TLOC], F32)
    nb_d = I("nb_d", [NH, 4], F32)
    x1_d = I("x1_d", [TLOC, D], F32)
    stats_d = I("stats_d", [128, 16], F32)
    sig_d = I("sig_d", [D, TLOC], BF16)
    wcum_d = I("wcum_d", [D, TLOC], F32); kvcum_d = I("kvcum_d", [D, TLOC], F32)
    tot_d = I("tot_d", [128, 64], F32)
    s_d = I("s_d", [TLOC, D], F32)
    wqkvT1b = I("wqkvT1b", [D, 3 * D], BF16)
    wswb = I("wswb", [D, 16384], BF16)
    woutb = I("woutb", [8192, D], BF16)
    WTb = I("WTb", [D, D], BF16)

    kTg = S("kTg", [NCORE * D, TLOC], BF16)
    vg = S("vg", [NCORE * TLOC, NH * 129], BF16)
    kshg = S("kshg", [NCORE * 128, TLOC], BF16)
    vshg = S("vshg", [NCORE * TLOC, 132], BF16)
    kng = S("kng", [NCORE * 17, 1], F32)
    totg = S("totg", [NCORE * 128, 64], F32)

    with contextlib.ExitStack() as outer:
        phase = outer.enter_context(nc.semaphore("phase"))
        ccA = outer.enter_context(nc.semaphore("ccA"))
        ccB = outer.enter_context(nc.semaphore("ccB"))
        bar = outer.enter_context(nc.semaphore("bar"))
        nbar = [0]
        block = outer.enter_context(nc.Block())
        pool = SemPool(outer)

        def barrier_in(pk):
            nbar[0] += 4

            @block.sync
            def _(sync):
                sync.wait_ge(phase, pk)
                sync.sem_inc(bar, 1)

            @block.tensor
            def _(tensor):
                tensor.wait_ge(phase, pk)
                tensor.sem_inc(bar, 1)

            @block.vector
            def _(vector):
                vector.wait_ge(phase, pk)
                vector.sem_inc(bar, 1)

            @block.scalar
            def _(scalar):
                scalar.wait_ge(phase, pk)
                scalar.sem_inc(bar, 1)

        def gate_all(val):
            @block.sync
            def _(sync):
                sync.wait_ge(phase, val)

            @block.gpsimd
            def _(gpsimd):
                gpsimd.wait_ge(phase, val)

            @block.tensor
            def _(tensor):
                tensor.wait_ge(phase, val)

            @block.vector
            def _(vector):
                vector.wait_ge(phase, val)

            @block.scalar
            def _(scalar):
                scalar.wait_ge(phase, val)

        ios1 = dict(x=x, wqT=wqT, wkT=wkT, wvT=wvT, wqkvT=wqkvT0, wgT=wgT,
                    cosq=cosq, sinq=sinq, cosk=cosk, sink=sink, ident=ident,
                    qT_o=qT_d, kT_o=kT_d, v_o=v_d, q2T_o=q2T_d, kshT_o=kshT_d,
                    vsh_o=vsh_d, qn2_o=qn2_d, q2n2_o=q2n2_d,
                    k2m_o=kn_d[0:16, :], k2shm_o=kn_d[16:17, :], knorm_d=knorm_d)
        pool.cur = []
        emit_l0_pre(nc, block, phase, 0, ios1, "p1_", pool)
        sems1 = pool.cur

        barrier_in(1)

        @block.gpsimd
        def _(gpsimd):
            gpsimd.wait_ge(phase, 1)
            gpsimd.wait_ge(bar, nbar[0])
            nc.clear_and_free_semaphores(sems1)
            for i, (s_ap, d_ap) in enumerate([(kT_d, kTg), (v_d, vg), (kshT_d, kshg),
                                              (vsh_d, vshg), (kn_d, kng)]):
                gpsimd.collective_compute("AllGather", AluOpType.bypass,
                                          replica_groups=[list(range(NCORE))],
                                          ins=[s_ap.opt()], outs=[d_ap.opt()]).then_inc(ccA, 1)
            gpsimd.wait_ge(ccA, 5)
            gpsimd.sem_inc(phase, 1)

        gate_all(2)
        ios2 = dict(x=x, qT=qT_d, q2T=q2T_d, kTl=kT_d, vl=v_d, kshTl=kshT_d,
                    vshl=vsh_d, kTg=kTg, vg=vg, kshg=kshg, vshg=vshg,
                    qn2=qn2_d, q2n2=q2n2_d, kng=kng, pm=pm, woT=woT,
                    maskT=maskT, ident=ident, x1_o=x1_d, nb_d=nb_d,
                    conv=[(wqkvT1, wqkvT1b), (wswT, wswb), (woutT, woutb), (WT, WTb)])
        pool.cur = []
        emit_l0_attn(nc, block, phase, 2, ios2, "p2_", pool)
        sems2 = pool.cur

        barrier_in(3)

        @block.gpsimd
        def _(gpsimd):
            gpsimd.wait_ge(phase, 3)
            gpsimd.wait_ge(bar, nbar[0])
            nc.clear_and_free_semaphores(sems2)
            gpsimd.sem_inc(phase, 1)

        gate_all(4)
        ios3 = dict(x1=x1_d, wqkvT=wqkvT1b, ident=ident, stats_o=stats_d,
                    sig_o=sig_d, wcum_o=wcum_d, kvcum_o=kvcum_d, tot_o=tot_d)
        pool.cur = []
        emit_l1_a(nc, block, phase, 4, ios3, "p3_", pool)
        sems3 = pool.cur

        barrier_in(5)

        @block.gpsimd
        def _(gpsimd):
            gpsimd.wait_ge(phase, 5)
            gpsimd.wait_ge(bar, nbar[0])
            nc.clear_and_free_semaphores(sems3)
            gpsimd.collective_compute("AllGather", AluOpType.bypass,
                                      replica_groups=[list(range(NCORE))],
                                      ins=[tot_d.opt()], outs=[totg.opt()]).then_inc(ccB, 1)
            gpsimd.wait_ge(ccB, 1)
            gpsimd.sem_inc(phase, 1)

        gate_all(6)
        ios4 = dict(x1=x1_d, stats=stats_d, sig=sig_d, wcum=wcum_d, kvcum=kvcum_d,
                    tots=totg, selm=selm, wswT=wswb, woutT=woutb, s_o=s_d)
        pool.cur = []
        emit_l1_b(nc, block, phase, 6, ios4, "p4_", pool)
        sems4 = pool.cur

        barrier_in(7)

        @block.gpsimd
        def _(gpsimd):
            gpsimd.wait_ge(phase, 7)
            gpsimd.wait_ge(bar, nbar[0])
            nc.clear_and_free_semaphores(sems4)
            gpsimd.sem_inc(phase, 1)

        gate_all(8)
        ios5 = dict(x1=x1_d, stats=stats_d, s=s_d, WT=WTb, b=bb, lAT=lAT, lBT=lBT,
                    gwT=gwT, gb=gb, ident=ident, x2_o=x2_o)
        pool.cur = []
        emit_l1_c(nc, block, phase, 8, ios5, "p5_", pool)

    return nc


def _prep_fused_maps(inputs):
    base = {
        "wqT": np.ascontiguousarray(np.asarray(inputs["L0_attn_wq"], np.float32).T),
        "wkT": np.ascontiguousarray(np.asarray(inputs["L0_attn_wk"], np.float32).T),
        "wvT": np.ascontiguousarray(np.asarray(inputs["L0_attn_wv"], np.float32).T),
        "wqkvT0": np.ascontiguousarray(np.asarray(inputs["L0_ffn_W_qkv"], np.float32).T),
        "wgT": np.ascontiguousarray(np.asarray(inputs["L0_ffn_W_gate"], np.float32).T),
        "woT": np.ascontiguousarray(np.asarray(inputs["L0_attn_wo"], np.float32).T),
        "wqkvT1": np.ascontiguousarray(np.asarray(inputs["L1_attn_w_qkv"], np.float32).T),
        "wswT": np.ascontiguousarray(np.asarray(inputs["L1_attn_w_swiglu"], np.float32).T),
        "woutT": np.ascontiguousarray(np.asarray(inputs["L1_attn_w_out"], np.float32).T),
        "maskT": np.triu(np.ones((128, 128), np.float32)).astype(BF),
        "ident": np.eye(128, dtype=BF),
    }
    base.update(_prep_l5(inputs))
    base.pop("stats", None)
    x = np.asarray(inputs["L0_attn_x"], np.float32).reshape(T, D)
    maps = []
    for c in range(NCORE):
        rows = _global_rows(c)
        bg = _core_blocks(c)
        cosq, sinq = _rope_tables(rows, RS)
        cosk, sink = _rope_tables(rows, 1.0)
        pmv = np.zeros((2, NH, 2, KSLOT), np.float32)
        pmv[:, :, :, 0] = 1.0
        for g in range(2):
            for kb in range(16):
                if kb < bg[g]:
                    pmv[:, :, g, 1 + kb] = 1.0
        selmv = np.zeros((2, NBLK), np.float32)
        for b2 in range(2):
            selmv[b2, :bg[b2]] = 1.0
        selmv = np.broadcast_to(selmv.reshape(1, 32), (128, 32)).copy()
        m = dict(base)
        m.update(x=np.ascontiguousarray(x[rows]), cosq=cosq, sinq=sinq, cosk=cosk,
                 sink=sink, pm=np.ascontiguousarray(pmv.reshape(1, -1)), selm=selmv)
        maps.append(m)
    return maps


def kernel(**inputs):
    nc = build_fused()
    maps = _prep_fused_maps(inputs)
    res = run_bass_kernel_spmd(nc, maps, core_ids=list(range(NCORE)))
    out = np.empty((T, D), np.float32)
    for c in range(NCORE):
        out[_global_rows(c)] = np.asarray(res.results[c]["x2_o"])
    return out.reshape(1, T, D)


# version 09 (final)
# speedup vs baseline: 1.1247x; 1.0703x over previous
import contextlib
import numpy as np
import ml_dtypes
import concourse.bass as bass
import concourse.mybir as mybir
from concourse.bass_utils import run_bass_kernel_spmd
from concourse.alu_op_type import AluOpType

F32 = mybir.dt.float32
BF16 = mybir.dt.bfloat16
AF = mybir.ActivationFunctionType
AX = mybir.AxisListType.X
BF = ml_dtypes.bfloat16

NCORE = 8
T = 8192
D = 2048
NBLK = 16
BLK = T // NBLK
TLOC = 2 * BLK
NTILE = TLOC // 128
NCH = D // 128
NH = 16
RS = float(1.0 / np.sqrt(128.0))


def _core_blocks(i):
    return (i, NBLK - 1 - i)


def _global_rows(i):
    bA, bB = _core_blocks(i)
    return np.r_[bA * BLK:(bA + 1) * BLK, bB * BLK:(bB + 1) * BLK]


def _rope_tables(pos, scale):
    inv = (1.0 / (1000000.0 ** (np.arange(0, 128, 2, dtype=np.float32) / np.float32(128)))).astype(np.float32)
    f = pos.astype(np.float32)[:, None] * inv[None, :]
    c = np.cos(f).astype(np.float32) * np.float32(scale)
    s = np.sin(f).astype(np.float32) * np.float32(scale)
    cosT = np.ascontiguousarray(np.concatenate([c, c], axis=1).T)
    sinS = np.ascontiguousarray(np.concatenate([-s, s], axis=1).T)
    return cosT, sinS


class SemPool:
    def __init__(self, ectx):
        self.ectx = ectx
        self.cur = []

    def enter_context(self, cm):
        h = self.ectx.enter_context(cm)
        self.cur.append(h)
        return h


class Ctx:
    def __init__(self, nc, ectx, prefix="", semstack=None):
        self.nc = nc
        self.ectx = ectx
        self.prefix = prefix
        self.semstack = semstack if semstack is not None else ectx
        self._ns = 0

    def sb(self, name, shape, dtype):
        return self.ectx.enter_context(self.nc.sbuf_tensor(self.prefix + name, shape, dtype))

    def ps(self, name, shape, dtype):
        return self.ectx.enter_context(self.nc.psum_tensor(self.prefix + name, shape, dtype))

    def sem(self, name=None):
        if name is None:
            self._ns += 1
            name = f"s{self._ns}"
        return self.semstack.enter_context(self.nc.semaphore(self.prefix + name))



def emit_l0_pre(nc, block, phase, pgate, ios, prefix, semstack):
    x_h = ios["x"]
    wqT_h = ios["wqT"]
    wkT_h = ios["wkT"]
    wvT_h = ios["wvT"]
    wqkvT_h = ios["wqkvT"]
    wgT_h = ios["wgT"]
    cosq_h = ios["cosq"]
    sinq_h = ios["sinq"]
    cosk_h = ios["cosk"]
    sink_h = ios["sink"]
    ident_h = ios["ident"]
    qT_o = ios["qT_o"]
    kT_o = ios["kT_o"]
    v_o = ios["v_o"]
    q2T_o = ios["q2T_o"]
    kshT_o = ios["kshT_o"]
    vsh_o = ios["vsh_o"]
    qn2_o = ios["qn2_o"]
    q2n2_o = ios["q2n2_o"]
    k2m_o = ios["k2m_o"]
    k2shm_o = ios["k2shm_o"]
    knorm_d = ios["knorm_d"]
    with contextlib.ExitStack() as ectx:
        C = Ctx(nc, ectx, prefix, semstack)
        xt2 = C.sb("xt2", [128, 2 * D], F32)
        sq_s = C.sb("sq_s", [128, D], F32)
        sqbf = C.sb("sqbf", [128, TLOC], BF16)
        st8 = C.sb("st8", [128, 64], F32)
        sx8 = st8[:, 0:8]; sqs8 = st8[:, 8:16]; m8 = st8[:, 16:24]; e28 = st8[:, 24:32]
        msq8 = st8[:, 32:40]; var8 = st8[:, 40:48]; lnv8 = st8[:, 48:56]; rstd8 = st8[:, 56:64]
        eps1 = C.sb("eps1", [128, 1], F32)
        xnbf = C.sb("xnbf", [128, D], BF16)
        xnT = C.sb("xnT", [128, NCH * TLOC], BF16)
        ident = C.sb("ident_sb", [128, 128], BF16)
        ones_bf = C.sb("ones_bf", [128, 1], BF16)
        cosq = C.sb("cosq_sb", [128, TLOC], F32)
        sinq = C.sb("sinq_sb", [128, TLOC], F32)
        cosk = C.sb("cosk_sb", [128, TLOC], F32)
        sink = C.sb("sink_sb", [128, TLOC], F32)
        ropetmp = C.sb("ropetmp", [128, TLOC], F32)
        stage = C.sb("stage", [128, 2 * TLOC], BF16)
        vsb = C.sb("vsb", [128, NTILE * NH * 129], BF16)
        vsbv = vsb[:].rearrange("p (t h c) -> p t h c", h=NH, c=129)
        nrow = C.sb("nrow", [1, 2 * TLOC], F32)
        k2rows = C.sb("k2rows", [NH, TLOC], F32)
        k2m = C.sb("k2m", [NH, 1], F32)
        kshrow = C.sb("kshrow", [1, TLOC], F32)
        k2shm = C.sb("k2shm", [1, 1], F32)
        wslab = C.sb("wslab", [128, 2 * NCH * 512], BF16)
        wsv = wslab[:].rearrange("p (b c f) -> p b c f", b=2, f=512)
        wsv128 = wslab[:].rearrange("p (b c f) -> p b c f", b=8, f=128)
        wgslab = C.sb("wgslab", [128, NCH * 16], BF16)
        wgv = wgslab[:].rearrange("p (c f) -> p c f", f=16)
        logit8 = C.sb("logit8", [128, 128], F32)
        gx8 = C.sb("gx8", [128, 24], F32)
        mx8 = gx8[:, 0:8]; ngm8 = gx8[:, 8:16]; top8 = gx8[:, 16:24]
        exps = C.sb("exps", [128, 128], F32)
        ind8 = C.sb("ind8", [128, 128], F32)
        svec8 = C.sb("svec8", [128, 128], F32)
        kfsb = C.sb("kfsb", [128, 512], F32)
        seltmp = C.sb("seltmp", [128, 512], F32)
        ksh8 = C.sb("ksh8", [128, 8 * 128], F32)
        vsh8 = C.sb("vsh8", [128, 8 * 128], F32)
        kshbf = C.sb("kshbf", [128, 128], BF16)
        kshT8 = C.sb("kshT8", [128, TLOC], BF16)
        vshst = C.sb("vshst", [128, 132], BF16)

        pj = C.ps("pj", [128, 2048], F32)
        ptb = C.ps("ptb", [128, 2048], BF16)
        nrm = C.ps("nrm", [128, 1024], F32)
        plg = nrm

        tld = C.sem("tld")
        xldA = [C.sem("xldA0"), C.sem("xldA1")]
        xldB = [C.sem("xldB0"), C.sem("xldB1")]
        pav = C.sem("pav")
        pas = C.sem("pas")
        stm = C.sem("stm")
        n_st = [0]

        def St(r):
            r.then_inc(stm, 16)
            n_st[0] += 16


        @block.sync
        def _(sync):
            for t in range(NTILE):
                if t >= 2:
                    sync.wait_ge(pav, t - 1)
                    sync.wait_ge(pas, t - 1)
                sync.dma_start(out=xt2[:, (t % 2) * D:(t % 2) * D + D],
                               in_=x_h.rearrange("(t p) f -> p t f", p=128)[:, t, :]).then_inc(xldA[t % 2], 16)
            sync.dma_start(out=cosq[:], in_=cosq_h).then_inc(tld, 16)
            sync.wait_ge(tld, 16)
            sync.dma_start(out=sinq[:], in_=sinq_h).then_inc(tld, 16)
            sync.wait_ge(tld, 32)
            sync.dma_start(out=cosk[:], in_=cosk_h).then_inc(tld, 16)
            sync.wait_ge(tld, 48)
            sync.dma_start(out=sink[:], in_=sink_h).then_inc(tld, 16)
            sync.wait_ge(tld, 64)
            sync.dma_start(out=ident[:], in_=ident_h).then_inc(tld, 16)

        ms = C.sem("ms")

        @block.gpsimd
        def _(gpsimd):
            gpsimd.memset(eps1[:], 1e-5)
            gpsimd.memset(ones_bf[:], 1.0)
            gpsimd.memset(vsb[:].rearrange("p (a c) -> p a c", c=129)[:, :, 128:129], 1.0)
            gpsimd.memset(vshst[:, 128:129], 1.0)
            gpsimd.memset(vshst[:, 129:132], 0.0)
            gpsimd.drain().then_inc(ms, 1)

        @block.vector
        def _(vector):
            for t in range(NTILE):
                vector.wait_ge(xldA[t % 2], (t // 2 + 1) * 16)
                vector.reduce_sum(sx8[:, t:t + 1], xt2[:, (t % 2) * D:(t % 2) * D + D], axis=AX)
                vector.drain().then_inc(pav, 1)

        @block.scalar
        def _(scalar):
            for t in range(NTILE):
                scalar.wait_ge(xldA[t % 2], (t // 2 + 1) * 16)
                scalar.activation(sq_s[:], xt2[:, (t % 2) * D:(t % 2) * D + D], AF.Square,
                                  accum_out=sqs8[:, t:t + 1])
                scalar.drain().then_inc(pas, 1)

        xn_s = C.sem("xn_s")

        @block.sync
        def _(sync):
            sync.wait_ge(pav, NTILE)
            sync.wait_ge(pas, NTILE)
            for t in range(NTILE):
                if t >= 2:
                    sync.wait_ge(xn_s, t - 1)
                sync.dma_start(out=xt2[:, (t % 2) * D:(t % 2) * D + D],
                               in_=x_h.rearrange("(t p) f -> p t f", p=128)[:, t, :]).then_inc(xldB[t % 2], 16)

        pb = C.sem("pb")

        @block.vector
        def _(vector):
            vector.wait_ge(pav, NTILE)
            vector.wait_ge(pas, NTILE)
            c = 1.0 / D
            vector.tensor_scalar_mul(m8, sx8, c)
            vector.tensor_scalar_mul(e28, sqs8, c)
            vector.drain()
            vector.tensor_tensor(msq8, m8, m8, op=AluOpType.mult)
            vector.drain()
            vector.tensor_tensor(var8, e28, msq8, op=AluOpType.subtract)
            vector.drain().then_inc(pb, 1)

        @block.scalar
        def _(scalar):
            scalar.wait_ge(pb, 1)
            scalar.wait_ge(ms, 1)
            scalar.activation(lnv8, var8, AF.Ln, bias=eps1[:])
            scalar.drain()
            scalar.activation(rstd8, lnv8, AF.Exp, scale=-0.5)
            scalar.drain().then_inc(pb, 1)
        PB_DONE = 2

        tr_s = C.sem("tr_s")
        xe_s = C.sem("xe_s")

        @block.vector
        def _(vector):
            vector.wait_ge(pb, PB_DONE)
            for t in range(NTILE):
                vector.wait_ge(xldB[t % 2], (t // 2 + 1) * 16)
                vector.tensor_scalar(xnbf[:], xt2[:, (t % 2) * D:(t % 2) * D + D],
                                     m8[:, t:t + 1], rstd8[:, t:t + 1],
                                     op0=AluOpType.subtract, op1=AluOpType.mult).then_inc(xn_s, 1)
                for g in range(2):
                    G = t * 2 + g
                    vector.wait_ge(tr_s, G + 1)
                    vector.tensor_copy(
                        xnT[:].rearrange("p (c tt) -> p c tt", tt=TLOC)[:, g * 8:(g + 1) * 8, t * 128:(t + 1) * 128],
                        ptb[:, (G % 2) * 1024:(G % 2) * 1024 + 1024].rearrange("p (c tt) -> p c tt", tt=128)).then_inc(xe_s, 1)

        @block.tensor
        def _(tensor):
            tensor.wait_ge(tld, 80)
            for t in range(NTILE):
                tensor.wait_ge(xn_s, t + 1)
                for g in range(2):
                    G = t * 2 + g
                    if G >= 2:
                        tensor.wait_ge(xe_s, G - 1)
                    for j in range(8):
                        e = g * 8 + j
                        r = tensor.transpose(ptb[:, (G % 2) * 1024 + j * 128:(G % 2) * 1024 + (j + 1) * 128],
                                             xnbf[:, e * 128:(e + 1) * 128], ident[:])
                    r.then_inc(tr_s, 1)
        XE_ALL = 16

        vw_s = [C.sem("vw_s0"), C.sem("vw_s1")]
        vm_s = C.sem("vm_s")
        ve_s = C.sem("ve_s")

        @block.gpsimd
        def _(gpsimd):
            gpsimd.wait_ge(ms, 1)
            for fs in range(4):
                if fs >= 2:
                    gpsimd.wait_ge(vm_s, (fs - 1) * 8)
                gpsimd.dma_start(out=wsv[:, fs % 2],
                                 in_=wvT_h[:, fs * 512:(fs + 1) * 512].rearrange("(c p) f -> p c f", p=128)
                                 ).then_inc(vw_s[fs % 2], 16)

        @block.tensor
        def _(tensor):
            tensor.wait_ge(xe_s, XE_ALL)
            for fs in range(4):
                tensor.wait_ge(vw_s[fs % 2], (fs // 2 + 1) * 16)
                for t in range(NTILE):
                    it = fs * 8 + t
                    if it >= 4:
                        tensor.wait_ge(ve_s, it - 3)
                    bank = pj[:, (it % 4) * 512:(it % 4) * 512 + 512]
                    for e in range(NCH):
                        r = tensor.matmul(bank,
                                          xnT[:, e * TLOC + t * 128: e * TLOC + (t + 1) * 128],
                                          wsv[:, fs % 2, e],
                                          start=(e == 0), stop=(e == NCH - 1))
                    r.then_inc(vm_s, 1)

        @block.scalar
        def _(scalar):
            for fs in range(4):
                for t in range(NTILE):
                    it = fs * 8 + t
                    scalar.wait_ge(vm_s, it + 1)
                    bank = pj[:, (it % 4) * 512:(it % 4) * 512 + 512]
                    for hh in range(4):
                        r = scalar.copy(vsbv[:, t, 4 * fs + hh, 0:128], bank[:, hh * 128:(hh + 1) * 128])
                    r.then_inc(ve_s, 1)
        VE_ALL = 32

        @block.gpsimd
        def _(gpsimd):
            gpsimd.wait_ge(ve_s, VE_ALL)
            St(gpsimd.dma_start(out=v_o.rearrange("(t p) c -> p t c", p=128),
                                in_=vsb[:].rearrange("p (t c) -> p t c", c=NH * 129)))

        def fstream(tag, w_ap, out_ap, rope, cos_sb, sin_sb, scale, nout_ap, first_wait, prev_so, prev_no=None, dve_tld=False):
            lw = [C.sem(f"{tag}lw0"), C.sem(f"{tag}lw1")]
            pm = C.sem(f"{tag}pm")
            se = C.sem(f"{tag}se")
            qs = C.sem(f"{tag}qs")
            nm = C.sem(f"{tag}nm")
            nr = C.sem(f"{tag}nr")
            so = [C.sem(f"{tag}so0"), C.sem(f"{tag}so1")]
            no = [C.sem(f"{tag}no0"), C.sem(f"{tag}no1")]

            @block.gpsimd
            def _(gpsimd):
                gpsimd.wait_ge(ms, 1)
                for h in range(NH + 1):
                    if h < NH:
                        if h >= 2:
                            gpsimd.wait_ge(pm, h - 1)
                        gpsimd.dma_start(out=wsv128[:, h % 2],
                                         in_=w_ap[:, h * 128:(h + 1) * 128].rearrange("(c p) f -> p c f", p=128)
                                         ).then_inc(lw[h % 2], 16)
                    if h >= 1:
                        j = h - 1
                        gpsimd.wait_ge(qs, j + 1)
                        gpsimd.dma_start(out=out_ap[j * 128:(j + 1) * 128, :],
                                         in_=stage[:, (j % 2) * TLOC:(j % 2) * TLOC + TLOC]).then_inc(so[j % 2], 16)
                        gpsimd.wait_ge(nr, j + 1)
                        gpsimd.dma_start(out=nout_ap[j:j + 1, :],
                                         in_=nrow[0:1, (j % 2) * TLOC:(j % 2) * TLOC + TLOC]).then_inc(no[j % 2], 16)

            @block.tensor
            def _(tensor):
                tensor.wait_ge(*first_wait)
                for h in range(NH):
                    tensor.wait_ge(lw[h % 2], (h // 2 + 1) * 16)
                    if h >= 2:
                        tensor.wait_ge(se, h - 1)
                    for half in range(2):
                        bank = pj[:, (h % 2) * 1024 + half * 512:(h % 2) * 1024 + half * 512 + 512]
                        for e in range(NCH):
                            r = tensor.matmul(bank,
                                              wsv128[:, h % 2, e],
                                              xnT[:, e * TLOC + half * 512: e * TLOC + (half + 1) * 512],
                                              start=(e == 0), stop=(e == NCH - 1))
                    r.then_inc(pm, 1)
                    if h >= 1:
                        tensor.wait_ge(qs, h)
                        if h >= 2:
                            tensor.wait_ge(nr, h - 1)
                        for half in range(2):
                            r = tensor.matmul(nrm[0:1, half * 512:half * 512 + 512],
                                              ones_bf[:], sqbf[:, half * 512:(half + 1) * 512],
                                              start=True, stop=True)
                        r.then_inc(nm, 1)
                tensor.wait_ge(qs, NH)
                tensor.wait_ge(nr, NH - 1)
                for half in range(2):
                    r = tensor.matmul(nrm[0:1, half * 512:half * 512 + 512],
                                      ones_bf[:], sqbf[:, half * 512:(half + 1) * 512],
                                      start=True, stop=True)
                r.then_inc(nm, 1)

            @block.vector
            def _(vector):
                if dve_tld:
                    vector.wait_ge(tld, 80)
                for h in range(NH):
                    vector.wait_ge(pm, h + 1)
                    if h < 2 and prev_so is not None:
                        vector.wait_ge(prev_so[0], prev_so[2])
                        vector.wait_ge(prev_so[1], prev_so[2])
                    if h >= 2:
                        vector.wait_ge(so[h % 2], (h // 2) * 16)
                    stg = stage[:, (h % 2) * TLOC:(h % 2) * TLOC + TLOC]
                    pjh = pj[:, (h % 2) * 1024:(h % 2) * 1024 + 1024]
                    if rope:
                        vector.tensor_tensor(ropetmp[0:64, :], pjh[64:128, :], sin_sb[0:64, :], op=AluOpType.mult)
                        vector.tensor_tensor(ropetmp[64:128, :], pjh[0:64, :], sin_sb[64:128, :], op=AluOpType.mult)
                        vector.tensor_tensor(stg, pjh, cos_sb[:], op=AluOpType.mult)
                        vector.drain()
                        vector.tensor_tensor(stg, stg, ropetmp[:], op=AluOpType.add).then_inc(se, 1)
                    else:
                        vector.tensor_scalar_mul(stg, pjh, scale).then_inc(se, 1)
                    vector.wait_ge(nm, h + 1)
                    if h < 2 and prev_no is not None:
                        vector.wait_ge(prev_no[0], prev_no[2])
                        vector.wait_ge(prev_no[1], prev_no[2])
                    if h >= 2:
                        vector.wait_ge(no[h % 2], (h // 2) * 16)
                    vector.tensor_copy(nrow[0:1, (h % 2) * TLOC:(h % 2) * TLOC + TLOC], nrm[0:1, :]).then_inc(nr, 1)

            @block.scalar
            def _(scalar):
                for h in range(NH):
                    scalar.wait_ge(se, h + 1)
                    stg = stage[:, (h % 2) * TLOC:(h % 2) * TLOC + TLOC]
                    scalar.activation(sqbf[:], stg, AF.Square).then_inc(qs, 1)

            return nr, so, no

        qnr, qso, qno = fstream("q", wqT_h, qT_o, True, cosq, sinq, RS, qn2_o,
                                (ve_s, VE_ALL), None, None, dve_tld=True)
        knr, kso, kno = fstream("k", wkT_h, kT_o, True, cosk, sink, 1.0, knorm_d,
                                (qnr, NH), (qso[0], qso[1], 8 * 16), (qno[0], qno[1], 8 * 16), dve_tld=True)
        q2nr, q2so, q2no = fstream("q2", wqkvT_h[:, 0:D], q2T_o, False, None, None, RS, q2n2_o,
                                   (knr, NH), (kso[0], kso[1], 8 * 16), (kno[0], kno[1], 8 * 16))

        nst_s = C.sem("nst_s")

        @block.sync
        def _(sync):
            sync.wait_ge(kno[0], 8 * 16)
            sync.wait_ge(kno[1], 8 * 16)
            sync.dma_start(out=k2rows[:], in_=knorm_d).then_inc(nst_s, 16)

        @block.vector
        def _(vector):
            vector.wait_ge(nst_s, 16)
            vector.reduce_max(k2m[:], k2rows[:], axis=AX)
            vector.drain().then_inc(nst_s, 1)

        @block.gpsimd
        def _(gpsimd):
            gpsimd.wait_ge(nst_s, 17)
            St(gpsimd.dma_start(out=k2m_o, in_=k2m[:]))

        gw_s = C.sem("gw_s")
        gl_s = C.sem("gl_s")

        @block.gpsimd
        def _(gpsimd):
            gpsimd.wait_ge(ms, 1)
            gpsimd.dma_start(out=wgv, in_=wgT_h.rearrange("(c p) f -> p c f", p=128)).then_inc(gw_s, 16)

        @block.tensor
        def _(tensor):
            tensor.wait_ge(gw_s, 16)
            tensor.wait_ge(q2nr, NH)
            for t in range(NTILE):
                for e in range(NCH):
                    r = tensor.matmul(plg[:, t * 16:(t + 1) * 16],
                                      xnT[:, e * TLOC + t * 128: e * TLOC + (t + 1) * 128],
                                      wgv[:, e],
                                      start=(e == 0), stop=(e == NCH - 1))
            r.then_inc(gl_s, 1)

        @block.vector
        def _(vector):
            vector.wait_ge(gl_s, 1)
            vector.tensor_copy(logit8[:], plg[:, 0:128])
            vector.drain()
            for t in range(NTILE):
                vector.reduce_max(mx8[:, t:t + 1], logit8[:, t * 16:(t + 1) * 16], axis=AX)
            vector.drain()
            vector.tensor_scalar_mul(ngm8, mx8, -1.0)
            vector.drain().then_inc(gl_s, 1)

        @block.scalar
        def _(scalar):
            scalar.wait_ge(gl_s, 2)
            for t in range(NTILE):
                scalar.activation(exps[:, t * 16:(t + 1) * 16], logit8[:, t * 16:(t + 1) * 16], AF.Exp,
                                  bias=ngm8[:, t:t + 1], accum_out=top8[:, t:t + 1])
            scalar.drain().then_inc(gl_s, 1)

        @block.vector
        def _(vector):
            vector.wait_ge(gl_s, 3)
            vector.reciprocal(top8, top8)
            vector.drain()
            for t in range(NTILE):
                vector.tensor_scalar(ind8[:, t * 16:(t + 1) * 16], logit8[:, t * 16:(t + 1) * 16],
                                     mx8[:, t:t + 1], None, op0=AluOpType.is_equal)
                vector.tensor_scalar(svec8[:, t * 16:(t + 1) * 16], logit8[:, t * 16:(t + 1) * 16],
                                     mx8[:, t:t + 1], top8[:, t:t + 1],
                                     op0=AluOpType.is_equal, op1=AluOpType.mult)
            vector.drain().then_inc(gl_s, 1)
        GL_DONE = 4

        sw_s = [C.sem("sw_s0"), C.sem("sw_s1")]
        sm_s = C.sem("sm_s")
        sev_s = C.sem("sev_s")
        sel_s = C.sem("sel_s")

        @block.gpsimd
        def _(gpsimd):
            for fs in range(8):
                if fs >= 2:
                    gpsimd.wait_ge(sm_s, (fs - 1) * 8)
                gpsimd.dma_start(out=wsv[:, fs % 2],
                                 in_=wqkvT_h[:, D + fs * 512: D + (fs + 1) * 512].rearrange("(c p) f -> p c f", p=128)
                                 ).then_inc(sw_s[fs % 2], 16)

        @block.tensor
        def _(tensor):
            for fs in range(8):
                tensor.wait_ge(sw_s[fs % 2], (fs // 2 + 1) * 16)
                for t in range(NTILE):
                    it = fs * 8 + t
                    if it >= 4:
                        tensor.wait_ge(sev_s, it - 3)
                    bank = pj[:, (it % 4) * 512:(it % 4) * 512 + 512]
                    for e in range(NCH):
                        r = tensor.matmul(bank,
                                          xnT[:, e * TLOC + t * 128: e * TLOC + (t + 1) * 128],
                                          wsv[:, fs % 2, e],
                                          start=(e == 0), stop=(e == NCH - 1))
                    r.then_inc(sm_s, 1)

        @block.vector
        def _(vector):
            vector.wait_ge(gl_s, GL_DONE)
            for fs in range(8):
                wvec = ind8 if fs < 4 else svec8
                acc = ksh8 if fs < 4 else vsh8
                for t in range(NTILE):
                    it = fs * 8 + t
                    vector.wait_ge(sm_s, it + 1)
                    bank = pj[:, (it % 4) * 512:(it % 4) * 512 + 512]
                    vector.tensor_copy(kfsb[:], bank)
                    vector.drain().then_inc(sev_s, 1)
                    hsel = wvec[:, t * 16 + 4 * (fs % 4): t * 16 + 4 * (fs % 4) + 4]
                    vector.tensor_tensor(seltmp[:].rearrange("p (h d) -> p h d", d=128),
                                         kfsb[:].rearrange("p (h d) -> p h d", d=128),
                                         hsel.rearrange("p (h o) -> p h o", o=1).broadcast_to((128, 4, 128)),
                                         op=AluOpType.mult)
                    vector.drain()
                    vector.tensor_tensor(seltmp[:, 0:256], seltmp[:, 0:256], seltmp[:, 256:512], op=AluOpType.add)
                    vector.drain()
                    if fs % 4 == 0:
                        vector.tensor_tensor(acc[:, t * 128:(t + 1) * 128], seltmp[:, 0:128], seltmp[:, 128:256], op=AluOpType.add)
                    else:
                        vector.tensor_tensor(seltmp[:, 0:128], seltmp[:, 0:128], seltmp[:, 128:256], op=AluOpType.add)
                        vector.drain()
                        vector.tensor_tensor(acc[:, t * 128:(t + 1) * 128], acc[:, t * 128:(t + 1) * 128], seltmp[:, 0:128], op=AluOpType.add)
                    vector.drain().then_inc(sel_s, 1)
        SEL_ALL = 64

        fin_s = C.sem("fin_s")
        ft_s = C.sem("ft_s")
        vsc_s = C.sem("vsc_s")
        vst_s = C.sem("vst_s")

        @block.vector
        def _(vector):
            vector.wait_ge(sel_s, SEL_ALL)
            for t in range(NTILE):
                vector.tensor_copy(kshbf[:], ksh8[:, t * 128:(t + 1) * 128])
                vector.drain().then_inc(fin_s, 1)
                vector.wait_ge(ft_s, t + 1)
                vector.tensor_copy(kshT8[:, t * 128:(t + 1) * 128], ptb[:, 0:128])
                vector.drain().then_inc(fin_s, 1)

        @block.tensor
        def _(tensor):
            for t in range(NTILE):
                tensor.wait_ge(fin_s, 2 * t + 1)
                tensor.transpose(ptb[:, 0:128], kshbf[:], ident[:]).then_inc(ft_s, 1)

        @block.scalar
        def _(scalar):
            scalar.wait_ge(sel_s, SEL_ALL)
            for t in range(NTILE):
                if t > 0:
                    scalar.wait_ge(vst_s, t * 16)
                scalar.copy(vshst[:, 0:128], vsh8[:, t * 128:(t + 1) * 128])
                scalar.drain().then_inc(vsc_s, 1)

        @block.gpsimd
        def _(gpsimd):
            for t in range(NTILE):
                gpsimd.wait_ge(vsc_s, t + 1)
                gpsimd.dma_start(out=vsh_o[t * 128:(t + 1) * 128, :], in_=vshst[:]).then_inc(vst_s, 16)

        ksq_s = C.sem("ksq_s")

        @block.scalar
        def _(scalar):
            scalar.wait_ge(fin_s, 2 * NTILE)
            scalar.activation(sqbf[:], kshT8[:], AF.Square)
            scalar.drain().then_inc(ksq_s, 1)

        @block.tensor
        def _(tensor):
            tensor.wait_ge(ksq_s, 1)
            for half in range(2):
                r = tensor.matmul(nrm[0:1, half * 512:half * 512 + 512],
                                  ones_bf[:], sqbf[:, half * 512:(half + 1) * 512],
                                  start=True, stop=True)
            r.then_inc(ksq_s, 1)

        @block.vector
        def _(vector):
            vector.wait_ge(ksq_s, 2)
            vector.tensor_copy(kshrow[:], nrm[0:1, :])
            vector.drain()
            vector.reduce_max(k2shm[:], kshrow[:], axis=AX)
            vector.drain().then_inc(ksq_s, 1)

        @block.gpsimd
        def _(gpsimd):
            gpsimd.wait_ge(ksq_s, 3)
            St(gpsimd.dma_start(out=kshT_o, in_=kshT8[:]))
            St(gpsimd.dma_start(out=k2shm_o, in_=k2shm[:]))
            gpsimd.wait_ge(vst_s, NTILE * 16)
            for sgrp in (qso, kso, q2so, qno, q2no, kno):
                gpsimd.wait_ge(sgrp[0], 8 * 16)
                gpsimd.wait_ge(sgrp[1], 8 * 16)
            gpsimd.wait_ge(stm, n_st[0])
            gpsimd.sem_inc(phase, 1)



KSLOT = 17


def emit_l0_attn(nc, block, phase, pgate, ios, prefix, semstack):
    x_h = ios["x"]
    qT_h = ios["qT"]
    q2T_h = ios["q2T"]
    kTl_h = ios["kTl"]
    vl_h = ios["vl"]
    kshTl_h = ios["kshTl"]
    vshl_h = ios["vshl"]
    kTg_h = ios["kTg"]
    vg_h = ios["vg"]
    kshg_h = ios["kshg"]
    vshg_h = ios["vshg"]
    qn2_h = ios["qn2"]
    q2n2_h = ios["q2n2"]
    kng_h = ios["kng"]
    pm_h = ios["pm"]
    woT_h = ios["woT"]
    maskT_h = ios["maskT"]
    ident_h = ios["ident"]
    x1_o = ios["x1_o"]
    nb_d = ios["nb_d"]
    NBIAS = 2 * NH * 2 * KSLOT
    with contextlib.ExitStack() as ectx:
        C = Ctx(nc, ectx, prefix, semstack)
        att = C.sb("att", [128, NTILE * D], BF16)
        attv = att[:].rearrange("p (t f) -> p t f", f=D)
        y2g = C.sb("y2g", [128, NTILE * D], BF16)
        y2gv = y2g[:].rearrange("p (t f) -> p t f", f=D)
        ptbuf = C.sb("ptbuf", [128, 4 * 256], BF16)
        kTb = C.sb("kTb", [128, 4 * 2 * 512], BF16)
        kTbv = kTb[:].rearrange("p (b c f) -> p b c f", b=4, f=512)
        vbb = C.sb("vbb", [128, 4 * 4 * 258], BF16)
        vbv = vbb[:].rearrange("p (b t c) -> p b t c", b=4, c=258)
        bias_sb = C.sb("bias_sb", [128, NBIAS], F32)
        nb16 = C.sb("nb16", [NH, 4], F32)
        nbrow = C.sb("nbrow", [1, 64], F32)
        pmrow = C.sb("pmrow", [1, NBIAS], F32)
        brow = C.sb("brow", [1, NBIAS], F32)
        brow_bf = C.sb("brow_bf", [1, NBIAS], BF16)
        k2a = C.sb("k2a", [NH, NCORE], F32)
        k2sha = C.sb("k2sha", [1, NCORE], F32)
        K2 = C.sb("K2", [NH, 1], F32)
        K2shb = C.sb("K2shb", [NH, 1], F32)
        K2sh1 = C.sb("K2sh1", [1, 1], BF16)
        ones_row = C.sb("ones_row", [1, 128], BF16)
        maskT = C.sb("maskT_sb", [128, 128], BF16)
        ident = C.sb("ident_sb", [128, 128], BF16)
        rec1 = C.sb("rec1", [128, 2], F32)
        ytmp = C.sb("ytmp", [128, 2 * 128], F32)
        kshTl = C.sb("kshTl_sb", [128, TLOC], BF16)
        vshl = C.sb("vshl_sb", [128, NTILE * 132], BF16)
        vshlv = vshl[:].rearrange("p (t c) -> p t c", c=132)

        arena = C.sb("arena", [128, 100352], mybir.dt.uint8)
        abase = nc.lookup_mloc(arena).addr

        def at(name, shape, dtype, off):
            return nc.alloc_sbuf_tensor_at(name, shape, dtype, offset=abase + off)

        kTl = at("kTl_sb", [128, NCH * TLOC], BF16, 0)
        vl = at("vl_sb", [128, NTILE * NH * 129], BF16, 32768)
        vlv = vl[:].rearrange("p (t c) -> p t c", c=NH * 129)
        q2Tl = at("q2Tl", [128, NCH * TLOC], BF16, 0)
        kshTs = at("kshTs", [128, T], BF16, 32768)
        vshs = at("vshs", [128, 64 * 132], BF16, 49152)
        vshv = vshs[:].rearrange("p (t c) -> p t c", c=132)
        wof = at("wof", [128, NCH * D], BF16, 0)
        wofv = wof[:].rearrange("p (c f) -> p c f", f=D)
        qTl = at("qTl", [128, NCH * TLOC], BF16, 67584)
        attTt = at("attTt", [128, NCH * 128], BF16, 67584)
        xt2 = at("xt2", [128, 2 * D], F32, 67584 + 4096)
        x1st = at("x1st", [128, D], F32, 67584 + 4096 + 16384)
        etmp = at("etmp", [128, 512], F32, 67584 + 4096 + 16384 + 8192)
        ybase = nc.lookup_mloc(y2g).addr
        qn2s = nc.alloc_sbuf_tensor_at("qn2s", [NH, TLOC], F32, offset=ybase)
        q2n2s = nc.alloc_sbuf_tensor_at("q2n2s", [NH, TLOC], F32, offset=ybase + 4096)
        bnd = nc.alloc_sbuf_tensor_at("bnd", [NH, TLOC], F32, offset=ybase + 8192)

        psA = C.ps("psA", [128, 1024], F32)
        psO = C.ps("psO", [128, 2048], F32)
        ptb = C.ps("ptb", [128, 2048], BF16)

        ld0 = ios.get("pfsem") if ios.get("pfsem") is not None else C.sem("ld0")
        ld1 = C.sem("ld1")

        @block.sync
        def _(sync):
            for i, (dst, srcap) in enumerate([
                (qTl[:].rearrange("p (c t) -> p c t", t=TLOC), qT_h.rearrange("(c p) t -> p c t", p=128)),
                (kTl[:].rearrange("p (c t) -> p c t", t=TLOC), kTl_h.rearrange("(c p) t -> p c t", p=128)),
                (vlv, vl_h.rearrange("(t p) c -> p t c", p=128)),
                (kshTl[:], kshTl_h),
                (vshlv, vshl_h.rearrange("(t p) c -> p t c", p=128)),
                (qn2s[:], qn2_h),
                (q2n2s[:], q2n2_h),
                (k2a[:].rearrange("h (r o) -> h r o", o=1), kng_h.rearrange("(r h) c -> h r c", h=17)[0:16]),
                (k2sha[:].rearrange("h (r o) -> h r o", o=1), kng_h.rearrange("(r h) c -> h r c", h=17)[16:17]),
                (maskT[:], maskT_h),
                (ident[:], ident_h),
                (pmrow[:], pm_h),
            ]):
                sync.wait_ge(ld0, i * 16)
                sync.dma_start(out=dst, in_=srcap, allow_slow_non_contiguous=True).then_inc(ld0, 16)
        LD0 = 12 * 16

        ms = C.sem("ms")

        @block.gpsimd
        def _(gpsimd):
            gpsimd.memset(ones_row[:], 1.0)
            gpsimd.drain().then_inc(ms, 1)

        bs = C.sem("bs")

        @block.vector
        def _(vector):
            vector.wait_ge(ld0, LD0)
            vector.reduce_max(K2[:], k2a[:], axis=AX)
            vector.reduce_max(K2shb[0:1, :], k2sha[:], axis=AX)
            vector.drain()
            vector.tensor_scalar(bnd[:], qn2s[:], K2[:], None, op0=AluOpType.mult)
            vector.drain().then_inc(bs, 1)

        @block.scalar
        def _(scalar):
            scalar.wait_ge(bs, 1)
            scalar.activation(bnd[:], bnd[:], AF.Sqrt)
            scalar.drain().then_inc(bs, 1)

        @block.vector
        def _(vector):
            vector.wait_ge(bs, 2)
            for g in range(2):
                vector.reduce_max(nb16[:, g:g + 1], bnd[:, g * 512:(g + 1) * 512], axis=AX)
            vector.tensor_copy(K2sh1[:], K2shb[0:1, 0:1])
            vector.drain().then_inc(bs, 1)

        @block.tensor
        def _(tensor):
            tensor.wait_ge(bs, 3)
            tensor.wait_ge(ms, 1)
            tensor.matmul(psA[:, 0:1], ones_row[:], K2sh1[:], start=True, stop=True).then_inc(bs, 1)

        @block.vector
        def _(vector):
            vector.wait_ge(bs, 4)
            vector.tensor_copy(K2shb[:], psA[0:NH, 0:1])
            vector.drain()
            vector.tensor_scalar(bnd[:], q2n2s[:], K2shb[:], None, op0=AluOpType.mult)
            vector.drain().then_inc(bs, 1)

        @block.scalar
        def _(scalar):
            scalar.wait_ge(bs, 5)
            scalar.activation(bnd[:], bnd[:], AF.Sqrt)
            scalar.drain().then_inc(bs, 1)

        @block.vector
        def _(vector):
            vector.wait_ge(bs, 6)
            for g in range(2):
                vector.reduce_max(nb16[:, 2 + g:3 + g], bnd[:, g * 512:(g + 1) * 512], axis=AX)
            vector.drain()
            vector.tensor_scalar_mul(nb16[:], nb16[:], -1.0)
            vector.drain().then_inc(bs, 1)

        kv_ld = [C.sem("kv0"), C.sem("kv1"), C.sem("kv2"), C.sem("kv3")]
        stc = C.sem("stc")
        exc = C.sem("exc")
        mkc = C.sem("mkc")
        pvc = C.sem("pvc")
        evc = C.sem("evc")
        ysl = C.sem("ysl")
        bsd1 = C.sem("bsd1")
        bsd2 = C.sem("bsd2")
        cvs = C.sem("cvs")
        NCV = [0]

        @block.gpsimd
        def _(gpsimd):
            gpsimd.wait_ge(bs, 7)
            gpsimd.dma_start(out=nb_d, in_=nb16[:]).then_inc(bsd1, 16)
            ncv = 0
            for (srcT, dstT) in ios.get("conv", []):
                R, Cc = srcT.shape[0], srcT.shape[1]
                step = max(256, (2 * 1024 * 1024) // (Cc * 4))
                r0 = 0
                while r0 < R:
                    r1 = min(R, r0 + step)
                    gpsimd.wait_ge(cvs, ncv * 16)
                    gpsimd.wait_ge(pvc, min(ncv * 100, 3100))
                    gpsimd.dma_start(out=dstT[r0:r1], in_=srcT[r0:r1]).then_inc(cvs, 16)
                    ncv += 1
                    r0 = r1
            NCV[0] = ncv

        @block.sync
        def _(sync):
            sync.wait_ge(bsd1, 16)
            sync.dma_start(out=nbrow[0:1, :].rearrange("o (h g) -> o h g", g=4),
                           in_=nb_d.rearrange("(o h) g -> o h g", o=1)).then_inc(bsd2, 16)

        @block.vector
        def _(vector):
            vector.wait_ge(bsd2, 16)
            nbv = nbrow[0:1, :].rearrange("o (h wg) -> o h wg", wg=4)
            bv = brow[0:1, :].rearrange("o (w h g s) -> o w h g s", w=2, h=NH, g=2)
            for w in range(2):
                for g in range(2):
                    vector.tensor_tensor(
                        bv[:, w, :, g, :],
                        nbv[:, :, 2 * w + g:2 * w + g + 1].broadcast_to((1, NH, KSLOT)),
                        pmrow[0:1, :].rearrange("o (w h g s) -> o w h g s", w=2, h=NH, g=2)[:, w, :, g, :],
                        op=AluOpType.mult)
            vector.drain()
            vector.tensor_scalar(pmrow[:], pmrow[:], 1e9, 1e9, op0=AluOpType.mult, op1=AluOpType.subtract)
            vector.drain()
            vector.tensor_tensor(brow[:], brow[:], pmrow[:], op=AluOpType.add)
            vector.drain()
            vector.tensor_copy(brow_bf[:], brow[:])
            vector.drain().then_inc(bs, 1)

        @block.tensor
        def _(tensor):
            tensor.wait_ge(bs, 8)
            for seg in range((NBIAS + 511) // 512):
                w = min(512, NBIAS - seg * 512)
                if seg > 0:
                    tensor.wait_ge(bs, 8 + 2 * seg)
                tensor.matmul(psA[:, 0:w], ones_row[:], brow_bf[0:1, seg * 512:seg * 512 + w],
                              start=True, stop=True).then_inc(bs, 1)

        @block.vector
        def _(vector):
            for seg in range((NBIAS + 511) // 512):
                w = min(512, NBIAS - seg * 512)
                vector.wait_ge(bs, 9 + seg * 2)
                vector.tensor_copy(bias_sb[:, seg * 512:seg * 512 + w], psA[:, 0:w])
                vector.drain().then_inc(bs, 1)
        BS_DONE = 8 + 2 * ((NBIAS + 511) // 512)


        CNT = dict(st=0, mk=0, pvg=0, m=0, ev=0, ysl=0)
        CNT_EX = [0]
        CNT_DV = [0]

        def attention(which):
            groups = []
            for hg in range(8):
                for g in range(2):
                    for qq in range(2):
                        steps = []
                        for kt in range(2 * qq + 2):
                            for hh in range(2):
                                steps.append(("diag", kt, hh))
                        for kb in range(8 if g == 0 else 16):
                            for kt in range(4):
                                for hh in range(2):
                                    steps.append((kb, kt, hh))
                        groups.append((hg, g, qq, steps))

            g_meta = []
            st = CNT["st"]; pvg = CNT["pvg"]; m = CNT["m"]
            ev = CNT["ev"]; ysl_n = CNT["ysl"]
            for (hg, g, qq, steps) in groups:
                meta = dict(pvg_base=pvg, ev_base=ev, m_base=m, ysl_base=ysl_n,
                            nsteps=len(steps))
                st += len(steps)
                pvg += len(steps)
                if which == 0:
                    m += 8 if g == 0 else 16
                ev += 4
                ysl_n += 4
                g_meta.append(meta)

            if which == 0:
                ldgate = []
                pv = CNT["pvg"]
                for (hg, g, qq, steps) in groups:
                    ndiag = (2 * qq + 2) * 2
                    pv += ndiag
                    nkb = 8 if g == 0 else 16
                    for kb in range(nkb):
                        pv += 8
                        ldgate.append(pv)

                @block.sync
                def _(sync):
                    mm = CNT["m"]
                    li = 0
                    for (hg, g, qq, steps) in groups:
                        h0 = 2 * hg
                        nkb = 8 if g == 0 else 16
                        for kb in range(nkb):
                            rk = kb if kb < 8 else 15 - kb
                            hf = 0 if kb < 8 else 1
                            if li >= 4:
                                sync.wait_ge(pvc, ldgate[li - 4])
                            sync.dma_start(
                                out=kTbv[:, mm % 4],
                                in_=kTg_h[rk * 2048 + h0 * 128: rk * 2048 + (h0 + 2) * 128,
                                          hf * 512:(hf + 1) * 512]
                                .rearrange("(c p) f -> p c f", p=128)).then_inc(kv_ld[mm % 4], 16)
                            sync.dma_start(
                                out=vbv[:, mm % 4],
                                in_=vg_h[rk * 1024 + hf * 512: rk * 1024 + hf * 512 + 512,
                                         h0 * 129:(h0 + 2) * 129]
                                .rearrange("(t p) c -> p t c", p=128)).then_inc(kv_ld[mm % 4], 16)
                            mm += 1
                            li += 1
            else:
                @block.sync
                def _(sync):
                    sync.wait_ge(evc, CNT["ev"])
                    n1 = 0
                    sync.dma_start(out=q2Tl[:].rearrange("p (c t) -> p c t", t=TLOC),
                                   in_=q2T_h.rearrange("(c p) t -> p c t", p=128)).then_inc(ld1, 16)
                    n1 += 16
                    for gb in range(NBLK):
                        rk = gb if gb < 8 else 15 - gb
                        hf = 0 if gb < 8 else 1
                        sync.wait_ge(ld1, n1)
                        sync.dma_start(out=kshTs[:, gb * 512:(gb + 1) * 512],
                                       in_=kshg_h[rk * 128:(rk + 1) * 128, hf * 512:(hf + 1) * 512]
                                       ).then_inc(ld1, 16)
                        n1 += 16
                        sync.wait_ge(ld1, n1)
                        sync.dma_start(out=vshv[:, gb * 4:(gb + 1) * 4, :],
                                       in_=vshg_h[rk * 1024 + hf * 512: rk * 1024 + hf * 512 + 512, :]
                                       .rearrange("(t p) c -> p t c", p=128)).then_inc(ld1, 16)
                        n1 += 16

            qsrc = qTl if which == 0 else q2Tl

            @block.tensor
            def _(tensor):
                tensor.wait_ge(bs, BS_DONE)
                if which == 1:
                    tensor.wait_ge(ld1, 16 + 32 * 16)
                st0 = CNT["st"]
                flat = []
                mkcnt = CNT["mk"]
                mloc = CNT["m"]
                for gi, (hg, g, qq, steps) in enumerate(groups):
                    h0 = 2 * hg
                    for si, (kb, kt, hh) in enumerate(steps):
                        diag = (kb == "diag")
                        if diag and kt >= 2 * qq:
                            mkcnt += 1
                        flat.append(dict(gi=gi, g=g, qq=qq, h=h0 + hh, hh=hh, kb=kb, kt=kt,
                                         diag=diag, first=(si == 0), mk=mkcnt,
                                         m=(None if diag or which == 1 else mloc + kb)))
                    if which == 0:
                        mloc += 8 if g == 0 else 16

                def emit_st(ii, info):
                    st = st0 + ii
                    if info["m"] is not None and info["kt"] == 0 and info["hh"] == 0:
                        tensor.wait_ge(kv_ld[info["m"] % 4], (info["m"] // 4 + 1) * 32)
                    if st >= 2:
                        tensor.wait_ge(exc, st - 1)
                    g, qq, h, kt, kb = info["g"], info["qq"], info["h"], info["kt"], info["kb"]
                    if info["diag"]:
                        if which == 0:
                            lhsT = kTl[:, h * TLOC + (g * 4 + kt) * 128: h * TLOC + (g * 4 + kt + 1) * 128]
                        else:
                            lhsT = kshTl[:, (g * 4 + kt) * 128:(g * 4 + kt + 1) * 128]
                    else:
                        if which == 0:
                            lhsT = kTbv[:, info["m"] % 4, info["hh"], kt * 128:(kt + 1) * 128]
                        else:
                            lhsT = kshTs[:, (kb * 4 + kt) * 128:(kb * 4 + kt + 1) * 128]
                    qoff = h * TLOC + g * 512 + qq * 256
                    tensor.matmul(psA[:, (st % 2) * 512:(st % 2) * 512 + 256],
                                  lhsT, qsrc[:, qoff:qoff + 256],
                                  start=True, stop=True).then_inc(stc, 1)

                def emit_pv(ii, info):
                    st = st0 + ii
                    if info["first"]:
                        tensor.wait_ge(evc, g_meta[info["gi"]]["ev_base"])
                    if info["diag"] and info["kt"] >= 2 * info["qq"]:
                        tensor.wait_ge(mkc, info["mk"])
                    else:
                        tensor.wait_ge(exc, st + 1)
                    g, qq, h, kt, kb = info["g"], info["qq"], info["h"], info["kt"], info["kb"]
                    if info["diag"]:
                        jlist = [j for j in range(2) if qq * 2 + j >= kt]
                    else:
                        jlist = [0, 1]
                    for j in jlist:
                        slot = info["hh"] * 2 + j
                        first = (info["diag"] and kt == 0)
                        last = (kb == (7 if g == 0 else 15) and kt == 3)
                        if info["diag"]:
                            if which == 0:
                                rhs = vlv[:, g * 4 + kt, h * 129:h * 129 + 129]
                            else:
                                rhs = vshlv[:, g * 4 + kt, 0:129]
                        else:
                            if which == 0:
                                rhs = vbv[:, info["m"] % 4, kt, info["hh"] * 129:info["hh"] * 129 + 129]
                            else:
                                rhs = vshv[:, kb * 4 + kt, 0:129]
                        r = tensor.matmul(psO[:, slot * 512:slot * 512 + 129],
                                          ptbuf[:, (st % 4) * 256 + j * 128:(st % 4) * 256 + (j + 1) * 128],
                                          rhs, start=first, stop=last)
                    r.then_inc(pvc, 1)

                for ii, info in enumerate(flat):
                    emit_st(ii, info)
                    if ii > 1:
                        emit_pv(ii - 2, flat[ii - 2])
                emit_pv(len(flat) - 2, flat[-2])
                emit_pv(len(flat) - 1, flat[-1])
                CNT["st"] = st0 + len(flat)
                CNT["mk"] = mkcnt
                CNT["m"] = mloc
                CNT["pvg"] = CNT["st"]

            @block.scalar
            def _(scalar):
                scalar.wait_ge(bs, BS_DONE)
                st = CNT_EX[0]
                for gi, (hg, g, qq, steps) in enumerate(groups):
                    h0 = 2 * hg
                    for (kb, kt, hh) in steps:
                        h = h0 + hh
                        slotk = 0 if kb == "diag" else 1 + kb
                        bcol = (0 if which == 0 else NH * 2 * KSLOT) + h * 2 * KSLOT + g * KSLOT + slotk
                        scalar.wait_ge(stc, st + 1)
                        if st >= 4:
                            scalar.wait_ge(pvc, st - 3)
                        scalar.activation(ptbuf[:, (st % 4) * 256:(st % 4) * 256 + 256],
                                          psA[:, (st % 2) * 512:(st % 2) * 512 + 256], AF.Exp,
                                          bias=bias_sb[:, bcol:bcol + 1]).then_inc(exc, 1)
                        st += 1
                    if which == 1:
                        for s in range(4):
                            hh, j = s // 2, s % 2
                            h = h0 + hh
                            tloc = g * 4 + qq * 2 + j
                            scalar.wait_ge(evc, g_meta[gi]["ev_base"] + s + 1)
                            scalar.activation(y2gv[:, tloc, h * 128:(h + 1) * 128],
                                              ytmp[:, (s % 2) * 128:(s % 2) * 128 + 128], AF.Silu).then_inc(ysl, 1)
                CNT_EX[0] = st

            @block.vector
            def _(vector):
                vector.wait_ge(bs, BS_DONE)
                st = CNT_DV[0]
                for gi, (hg, g, qq, steps) in enumerate(groups):
                    h0 = 2 * hg
                    for (kb, kt, hh) in steps:
                        if kb == "diag" and kt >= 2 * qq:
                            j = kt - 2 * qq
                            vector.wait_ge(exc, st + 1)
                            vector.tensor_tensor(
                                ptbuf[:, (st % 4) * 256 + j * 128:(st % 4) * 256 + (j + 1) * 128],
                                ptbuf[:, (st % 4) * 256 + j * 128:(st % 4) * 256 + (j + 1) * 128],
                                maskT[:], op=AluOpType.mult).then_inc(mkc, 1)
                        st += 1
                    vector.wait_ge(pvc, g_meta[gi]["pvg_base"] + g_meta[gi]["nsteps"])
                    for s in range(4):
                        hh, j = s // 2, s % 2
                        h = h0 + hh
                        tloc = g * 4 + qq * 2 + j
                        rs = rec1[:, s % 2:s % 2 + 1]
                        vector.reciprocal(rs, psO[:, s * 512 + 128:s * 512 + 129])
                        vector.drain()
                        if which == 0:
                            vector.tensor_scalar(attv[:, tloc, h * 128:(h + 1) * 128],
                                                 psO[:, s * 512:s * 512 + 128],
                                                 rs, None, op0=AluOpType.mult).then_inc(evc, 1)
                        else:
                            if s >= 2:
                                vector.wait_ge(ysl, g_meta[gi]["ysl_base"] + s - 1)
                            vector.tensor_scalar(ytmp[:, (s % 2) * 128:(s % 2) * 128 + 128],
                                                 psO[:, s * 512:s * 512 + 128],
                                                 rs, None, op0=AluOpType.mult).then_inc(evc, 1)
                CNT_DV[0] = st
            CNT["ev"] += len(groups) * 4
            if which == 1:
                CNT["ysl"] += len(groups) * 4

        attention(0)
        ATT_EVC = CNT["ev"]
        attention(1)
        ALL_EVC = CNT["ev"]
        ALL_YSL = CNT["ysl"]

        wo_ld = C.sem("wo_ld")
        xld = [C.sem("xl0"), C.sem("xl1")]
        trs = C.sem("trs")
        tev = C.sem("tev")
        epm = C.sem("epm")
        ecb = C.sem("ecb")
        x1s = C.sem("x1s")

        @block.gpsimd
        def _(gpsimd):
            gpsimd.wait_ge(ysl, ALL_YSL)
            gpsimd.dma_start(out=wofv, in_=woT_h.rearrange("(c p) f -> p c f", p=128)).then_inc(wo_ld, 16)

        @block.sync
        def _(sync):
            for t in range(NTILE):
                if t >= 2:
                    sync.wait_ge(ecb, (t - 1) * 4)
                sync.dma_start(out=xt2[:, (t % 2) * D:(t % 2) * D + D],
                               in_=x_h.rearrange("(t p) f -> p t f", p=128)[:, t, :]).then_inc(xld[t % 2], 16)

        @block.tensor
        def _(tensor):
            tensor.wait_ge(evc, ALL_EVC)
            tensor.wait_ge(wo_ld, 16)
            for t in range(NTILE):
                for gtr in range(2):
                    G = t * 2 + gtr
                    if G >= 2:
                        tensor.wait_ge(tev, G - 1)
                    for jj in range(8):
                        e = gtr * 8 + jj
                        r = tensor.transpose(ptb[:, (G % 2) * 1024 + jj * 128:(G % 2) * 1024 + (jj + 1) * 128],
                                             attv[:, t, e * 128:(e + 1) * 128], ident[:])
                    r.then_inc(trs, 1)
                tensor.wait_ge(tev, 2 * t + 2)
                for fs in range(4):
                    it = t * 4 + fs
                    if it >= 2:
                        tensor.wait_ge(ecb, it - 1)
                    for e in range(NCH):
                        r = tensor.matmul(psA[:, (it % 2) * 512:(it % 2) * 512 + 512],
                                          attTt[:, e * 128:(e + 1) * 128],
                                          wofv[:, e, fs * 512:(fs + 1) * 512],
                                          start=(e == 0), stop=(e == NCH - 1))
                    r.then_inc(epm, 1)

        @block.vector
        def _(vector):
            for t in range(NTILE):
                for gtr in range(2):
                    G = t * 2 + gtr
                    vector.wait_ge(trs, G + 1)
                    if t >= 1 and gtr == 0:
                        vector.wait_ge(epm, t * 4)
                    vector.tensor_copy(
                        attTt[:].rearrange("p (c tt) -> p c tt", tt=128)[:, gtr * 8:(gtr + 1) * 8, :],
                        ptb[:, (G % 2) * 1024:(G % 2) * 1024 + 1024].rearrange("p (c tt) -> p c tt", tt=128))
                    vector.drain().then_inc(tev, 1)
                vector.wait_ge(xld[t % 2], (t // 2 + 1) * 16)
                vector.wait_ge(ysl, ALL_YSL)
                for fs in range(4):
                    it = t * 4 + fs
                    vector.wait_ge(epm, it + 1)
                    if t > 0 and fs == 0:
                        vector.wait_ge(x1s, t * 16)
                    vector.tensor_tensor(etmp[:], psA[:, (it % 2) * 512:(it % 2) * 512 + 512],
                                         y2gv[:, t, fs * 512:(fs + 1) * 512], op=AluOpType.mult)
                    vector.drain()
                    vector.tensor_tensor(x1st[:, fs * 512:(fs + 1) * 512], etmp[:],
                                         xt2[:, (t % 2) * D + fs * 512:(t % 2) * D + (fs + 1) * 512],
                                         op=AluOpType.add)
                    vector.drain().then_inc(ecb, 1)

        @block.gpsimd
        def _(gpsimd):
            for t in range(NTILE):
                gpsimd.wait_ge(ecb, (t + 1) * 4)
                gpsimd.dma_start(out=x1_o.rearrange("(t p) f -> p t f", p=128)[:, t, :],
                                 in_=x1st[:]).then_inc(x1s, 16)
            gpsimd.wait_ge(x1s, NTILE * 16)
            gpsimd.wait_ge(cvs, NCV[0] * 16)
            gpsimd.sem_inc(phase, 1)



EPSR = 1.1920929e-07


def emit_l1_a(nc, block, phase, pgate, ios, prefix, semstack):
    x1_h = ios["x1"]
    wqkvT_h = ios["wqkvT"]
    ident_h = ios["ident"]
    stats_o = ios["stats_o"]
    sig_o = ios["sig_o"]
    wcum_o = ios["wcum_o"]
    kvcum_o = ios["kvcum_o"]
    tot_o = ios["tot_o"]
    with contextlib.ExitStack() as ectx:
        C = Ctx(nc, ectx, prefix, semstack)
        xt2 = C.sb("xt2", [128, 2 * D], F32)
        xnf = C.sb("xnf", [128, D], F32)
        sq_s = C.sb("sq_s", [128, D], F32)
        sqbf = C.sb("sqbf", [128, TLOC], BF16)
        st8 = C.sb("st8", [128, 64], F32)
        sx8 = st8[:, 0:8]; sqs8 = st8[:, 8:16]; m8 = st8[:, 16:24]; e28 = st8[:, 24:32]
        msq8 = st8[:, 32:40]; var8 = st8[:, 40:48]; lnv8 = st8[:, 48:56]; rstd8 = st8[:, 56:64]
        stst = C.sb("stst", [128, 16], F32)
        rr8 = C.sb("rr8", [128, 24], F32)
        ssr8 = rr8[:, 0:8]; lnr8 = rr8[:, 8:16]; rsr8 = rr8[:, 16:24]
        eps1 = C.sb("eps1", [128, 1], F32)
        epsr = C.sb("epsr", [128, 1], F32)
        hbf = C.sb("hbf", [128, D], BF16)
        hT = C.sb("hT", [128, NCH * TLOC], BF16)
        ident = C.sb("ident_sb", [128, 128], BF16)
        ones_bf = C.sb("ones_bf", [128, 1], BF16)
        ones_f = C.sb("ones_f", [1, 128], F32)
        qsb = C.sb("qsb", [128, NCH * TLOC], BF16)
        ksb = C.sb("ksb", [128, NCH * TLOC], BF16)
        vsb = C.sb("vsb", [128, NCH * TLOC], BF16)
        wslab = C.sb("wslab", [128, 2 * NCH * 128], BF16)
        wsv = wslab[:].rearrange("p (b c f) -> p b c f", b=2, f=128)
        nqrow = C.sb("nqrow", [1, TLOC], F32)
        nkrow = C.sb("nkrow", [1, TLOC], F32)
        rqrow = C.sb("rqrow", [1, TLOC], F32)
        rkrow = C.sb("rkrow", [1, TLOC], F32)
        rqb = C.sb("rqb", [128, TLOC], F32)
        rkb = C.sb("rkb", [128, TLOC], F32)
        tot = C.sb("tot", [128, 64], F32)
        hbase = nc.lookup_mloc(hT).addr
        ktmp = nc.alloc_sbuf_tensor_at("ktmp", [128, TLOC], F32, offset=hbase)
        wtmp = nc.alloc_sbuf_tensor_at("wtmp", [128, TLOC], F32, offset=hbase + 4096)
        wvtmp = nc.alloc_sbuf_tensor_at("wvtmp", [128, TLOC], F32, offset=hbase + 8192)
        qtmp = nc.alloc_sbuf_tensor_at("qtmp", [128, TLOC], F32, offset=hbase + 12288)
        xbase = nc.lookup_mloc(xt2).addr
        cst = nc.alloc_sbuf_tensor_at("cst", [128, 4 * TLOC], F32, offset=xbase)
        sgst = nc.alloc_sbuf_tensor_at("sgst", [128, 2 * TLOC], BF16, offset=nc.lookup_mloc(sq_s).addr)

        pj = C.ps("pj", [128, 2048], F32)
        ptb_ctx = contextlib.ExitStack()
        ptb = ptb_ctx.enter_context(nc.psum_tensor("p3_ptb", [128, 2048], BF16))

        xldA = [C.sem("xldA0"), C.sem("xldA1")]
        xldB = [C.sem("xldB0"), C.sem("xldB1")]
        pav = C.sem("pav")
        pas = C.sem("pas")
        tld = C.sem("tld")
        ms = C.sem("ms")
        stm = C.sem("stm")
        n_st = [0]

        def St(r):
            r.then_inc(stm, 16)
            n_st[0] += 16


        @block.sync
        def _(sync):
            for t in range(NTILE):
                if t >= 2:
                    sync.wait_ge(pav, t - 1)
                    sync.wait_ge(pas, t - 1)
                sync.dma_start(out=xt2[:, (t % 2) * D:(t % 2) * D + D],
                               in_=x1_h.rearrange("(t p) f -> p t f", p=128)[:, t, :]).then_inc(xldA[t % 2], 16)
            sync.dma_start(out=ident[:], in_=ident_h).then_inc(tld, 16)

        @block.gpsimd
        def _(gpsimd):
            gpsimd.memset(eps1[:], 1e-5)
            gpsimd.memset(epsr[:], EPSR)
            gpsimd.memset(ones_bf[:], 1.0)
            gpsimd.memset(ones_f[:], 1.0)
            gpsimd.drain().then_inc(ms, 1)

        @block.vector
        def _(vector):
            for t in range(NTILE):
                vector.wait_ge(xldA[t % 2], (t // 2 + 1) * 16)
                vector.reduce_sum(sx8[:, t:t + 1], xt2[:, (t % 2) * D:(t % 2) * D + D], axis=AX)
                vector.drain().then_inc(pav, 1)

        @block.scalar
        def _(scalar):
            for t in range(NTILE):
                scalar.wait_ge(xldA[t % 2], (t // 2 + 1) * 16)
                scalar.activation(sq_s[:], xt2[:, (t % 2) * D:(t % 2) * D + D], AF.Square,
                                  accum_out=sqs8[:, t:t + 1])
                scalar.drain().then_inc(pas, 1)

        xn_s = C.sem("xn_s")
        pb = C.sem("pb")

        @block.sync
        def _(sync):
            sync.wait_ge(pav, NTILE)
            sync.wait_ge(pas, NTILE)
            for t in range(NTILE):
                if t >= 2:
                    sync.wait_ge(xn_s, 2 * (t - 1))
                sync.dma_start(out=xt2[:, (t % 2) * D:(t % 2) * D + D],
                               in_=x1_h.rearrange("(t p) f -> p t f", p=128)[:, t, :]).then_inc(xldB[t % 2], 16)

        @block.vector
        def _(vector):
            vector.wait_ge(pav, NTILE)
            vector.wait_ge(pas, NTILE)
            c = 1.0 / D
            vector.tensor_scalar_mul(m8, sx8, c)
            vector.tensor_scalar_mul(e28, sqs8, c)
            vector.drain()
            vector.tensor_tensor(msq8, m8, m8, op=AluOpType.mult)
            vector.drain()
            vector.tensor_tensor(var8, e28, msq8, op=AluOpType.subtract)
            vector.drain().then_inc(pb, 1)

        @block.scalar
        def _(scalar):
            scalar.wait_ge(pb, 1)
            scalar.wait_ge(ms, 1)
            scalar.activation(lnv8, var8, AF.Ln, bias=eps1[:])
            scalar.drain()
            scalar.activation(rstd8, lnv8, AF.Exp, scale=-0.5)
            scalar.drain().then_inc(pb, 1)

        @block.vector
        def _(vector):
            vector.wait_ge(pb, 2)
            vector.tensor_copy(stst[:, 0:8], m8)
            vector.tensor_copy(stst[:, 8:16], rstd8)
            vector.drain().then_inc(pb, 1)

        @block.gpsimd
        def _(gpsimd):
            gpsimd.wait_ge(pb, 3)
            St(gpsimd.dma_start(out=stats_o, in_=stst[:]))

        tr_s = C.sem("tr_s")
        xe_s = C.sem("xe_s")
        rq_s = C.sem("rq_s")

        @block.vector
        def _(vector):
            vector.wait_ge(pb, 2)
            for t in range(NTILE):
                vector.wait_ge(xldB[t % 2], (t // 2 + 1) * 16)
                vector.tensor_scalar(xnf[:], xt2[:, (t % 2) * D:(t % 2) * D + D],
                                     m8[:, t:t + 1], rstd8[:, t:t + 1],
                                     op0=AluOpType.subtract, op1=AluOpType.mult)
                vector.drain().then_inc(xn_s, 1)
                vector.wait_ge(rq_s, t + 1)
                vector.tensor_scalar(hbf[:], xnf[:], rsr8[:, t:t + 1], None, op0=AluOpType.mult)
                vector.drain().then_inc(xn_s, 1)
                for g in range(2):
                    G = t * 2 + g
                    vector.wait_ge(tr_s, G + 1)
                    vector.tensor_copy(
                        hT[:].rearrange("p (c tt) -> p c tt", tt=TLOC)[:, g * 8:(g + 1) * 8, t * 128:(t + 1) * 128],
                        ptb[:, (G % 2) * 1024:(G % 2) * 1024 + 1024].rearrange("p (c tt) -> p c tt", tt=128)).then_inc(xe_s, 1)

        @block.scalar
        def _(scalar):
            for t in range(NTILE):
                scalar.wait_ge(xn_s, 2 * t + 1)
                scalar.activation(sq_s[:], xnf[:], AF.Square, accum_out=ssr8[:, t:t + 1])
                scalar.drain()
                scalar.activation(lnr8[:, t:t + 1], ssr8[:, t:t + 1], AF.Ln, bias=epsr[:], scale=1.0 / D)
                scalar.drain()
                scalar.activation(rsr8[:, t:t + 1], lnr8[:, t:t + 1], AF.Exp, scale=-0.5)
                scalar.drain().then_inc(rq_s, 1)

        @block.tensor
        def _(tensor):
            tensor.wait_ge(tld, 16)
            for t in range(NTILE):
                tensor.wait_ge(xn_s, 2 * t + 2)
                for g in range(2):
                    G = t * 2 + g
                    if G >= 2:
                        tensor.wait_ge(xe_s, G - 1)
                    for j in range(8):
                        e = g * 8 + j
                        r = tensor.transpose(ptb[:, (G % 2) * 1024 + j * 128:(G % 2) * 1024 + (j + 1) * 128],
                                             hbf[:, e * 128:(e + 1) * 128], ident[:])
                    r.then_inc(tr_s, 1)
        XE_ALL = 16
        ptb_ctx.close()
        nrm = C.ps("nrm", [128, 1024], F32)
        nr2 = C.ps("nr2", [128, 1024], F32)

        lw = [C.sem("lw0"), C.sem("lw1")]
        pm = C.sem("pm")
        se = C.sem("se")
        qs2 = C.sem("qs2")
        nmq = C.sem("nmq")

        @block.gpsimd
        def _(gpsimd):
            gpsimd.wait_ge(ms, 1)
            for i in range(3 * NCH):
                if i >= 2:
                    gpsimd.wait_ge(pm, i - 1)
                gpsimd.dma_start(out=wsv[:, i % 2],
                                 in_=wqkvT_h[:, i * 128:(i + 1) * 128].rearrange("(c p) f -> p c f", p=128)
                                 ).then_inc(lw[i % 2], 16)

        @block.tensor
        def _(tensor):
            tensor.wait_ge(xe_s, XE_ALL)
            for i in range(3 * NCH):
                sect, hc = divmod(i, NCH)
                tensor.wait_ge(lw[i % 2], (i // 2 + 1) * 16)
                if i >= 2:
                    tensor.wait_ge(se, i - 1)
                for half in range(2):
                    bank = pj[:, (i % 2) * 1024 + half * 512:(i % 2) * 1024 + half * 512 + 512]
                    for e in range(NCH):
                        r = tensor.matmul(bank,
                                          wsv[:, i % 2, e],
                                          hT[:, e * TLOC + half * 512: e * TLOC + (half + 1) * 512],
                                          start=(e == 0), stop=(e == NCH - 1))
                r.then_inc(pm, 1)
                if sect < 2:
                    tensor.wait_ge(qs2, i + 1)
                    bank2 = nrm if sect == 0 else nr2
                    for half in range(2):
                        r = tensor.matmul(bank2[0:1, half * 512:half * 512 + 512],
                                          ones_bf[:], sqbf[:, half * 512:(half + 1) * 512],
                                          start=(hc == 0), stop=(hc == NCH - 1))
                    r.then_inc(nmq, 1)

        @block.vector
        def _(vector):
            for i in range(3 * NCH):
                sect, hc = divmod(i, NCH)
                dst = (qsb, ksb, vsb)[sect]
                vector.wait_ge(pm, i + 1)
                vector.tensor_copy(dst[:, hc * TLOC:(hc + 1) * TLOC],
                                   pj[:, (i % 2) * 1024:(i % 2) * 1024 + 1024]).then_inc(se, 1)

        @block.scalar
        def _(scalar):
            for i in range(3 * NCH):
                sect, hc = divmod(i, NCH)
                if sect < 2:
                    scalar.wait_ge(se, i + 1)
                    dst = (qsb, ksb)[sect]
                    scalar.activation(sqbf[:], dst[:, hc * TLOC:(hc + 1) * TLOC], AF.Square).then_inc(qs2, 1)
                else:
                    scalar.drain().then_inc(qs2, 1)

        rw = C.sem("rw")

        @block.vector
        def _(vector):
            vector.wait_ge(nmq, 2 * NCH)
            vector.tensor_copy(nqrow[:], nrm[0:1, :])
            vector.tensor_copy(nkrow[:], nr2[0:1, :])
            vector.drain().then_inc(rw, 1)

        @block.scalar
        def _(scalar):
            scalar.wait_ge(rw, 1)
            scalar.activation(nqrow[:], nqrow[:], AF.Ln, bias=epsr[0:1, :], scale=1.0 / D)
            scalar.activation(nkrow[:], nkrow[:], AF.Ln, bias=epsr[0:1, :], scale=1.0 / D)
            scalar.drain()
            scalar.activation(rqrow[:], nqrow[:], AF.Exp, scale=-0.5)
            scalar.activation(rkrow[:], nkrow[:], AF.Exp, scale=-0.5)
            scalar.drain().then_inc(rw, 1)

        @block.tensor
        def _(tensor):
            tensor.wait_ge(rw, 2)
            for half in range(2):
                tensor.matmul(nrm[:, half * 512:half * 512 + 512], ones_f[:],
                              rqrow[0:1, half * 512:(half + 1) * 512], start=True, stop=True)
                r = tensor.matmul(nr2[:, half * 512:half * 512 + 512], ones_f[:],
                                  rkrow[0:1, half * 512:(half + 1) * 512], start=True, stop=True)
            r.then_inc(rw, 1)

        @block.vector
        def _(vector):
            vector.wait_ge(rw, 3)
            vector.tensor_copy(rqb[:], nrm[:])
            vector.tensor_copy(rkb[:], nr2[:])
            vector.drain().then_inc(rw, 1)

        ck = C.sem("ck")
        cka = C.sem("cka")
        cst_s = [C.sem("cst_s0"), C.sem("cst_s1")]
        sg_s = [C.sem("sg_s0"), C.sem("sg_s1")]

        @block.vector
        def _(vector):
            vector.wait_ge(rw, 4)
            for e in range(NCH):
                if e >= 2:
                    vector.wait_ge(cst_s[e % 2], (e // 2) * 32)
                vector.tensor_tensor(ktmp[:], ksb[:, e * TLOC:(e + 1) * TLOC], rkb[:], op=AluOpType.mult)
                vector.drain().then_inc(ck, 1)
                vector.wait_ge(cka, 2 * e + 1)
                vector.tensor_tensor(wvtmp[:], wtmp[:], vsb[:, e * TLOC:(e + 1) * TLOC], op=AluOpType.mult)
                vector.drain()
                cslot = cst[:, (e % 2) * 2 * TLOC:(e % 2) * 2 * TLOC + 2 * TLOC]
                for b in range(2):
                    sl = slice(b * BLK, (b + 1) * BLK)
                    vector.tensor_tensor_scan(cslot[:, 0:TLOC][:, sl], wtmp[:, sl], wtmp[:, sl], 0.0,
                                              op0=AluOpType.add, op1=AluOpType.bypass)
                    vector.tensor_tensor_scan(cslot[:, TLOC:2 * TLOC][:, sl], wvtmp[:, sl], wvtmp[:, sl], 0.0,
                                              op0=AluOpType.add, op1=AluOpType.bypass)
                vector.drain()
                vector.tensor_copy(tot[:, e:e + 1], cslot[:, 511:512])
                vector.tensor_copy(tot[:, 16 + e:17 + e], cslot[:, 1023:1024])
                vector.tensor_copy(tot[:, 32 + e:33 + e], cslot[:, TLOC + 511:TLOC + 512])
                vector.tensor_copy(tot[:, 48 + e:49 + e], cslot[:, TLOC + 1023:TLOC + 1024])
                vector.drain().then_inc(ck, 1)
                vector.tensor_tensor(qtmp[:], qsb[:, e * TLOC:(e + 1) * TLOC], rqb[:], op=AluOpType.mult)
                vector.drain().then_inc(ck, 1)

        @block.scalar
        def _(scalar):
            for e in range(NCH):
                scalar.wait_ge(ck, 3 * e + 1)
                scalar.activation(wtmp[:], ktmp[:], AF.Exp).then_inc(cka, 1)
                scalar.wait_ge(ck, 3 * e + 3)
                if e >= 2:
                    scalar.wait_ge(sg_s[e % 2], (e // 2) * 16)
                scalar.activation(sgst[:, (e % 2) * TLOC:(e % 2) * TLOC + TLOC], qtmp[:], AF.Sigmoid).then_inc(cka, 1)

        @block.gpsimd
        def _(gpsimd):
            for e in range(NCH):
                gpsimd.wait_ge(ck, 3 * e + 2)
                cslot = cst[:, (e % 2) * 2 * TLOC:(e % 2) * 2 * TLOC + 2 * TLOC]
                gpsimd.dma_start(out=wcum_o[e * 128:(e + 1) * 128, :], in_=cslot[:, 0:TLOC]).then_inc(cst_s[e % 2], 16)
                gpsimd.dma_start(out=kvcum_o[e * 128:(e + 1) * 128, :], in_=cslot[:, TLOC:2 * TLOC]).then_inc(cst_s[e % 2], 16)
                gpsimd.wait_ge(cka, 2 * e + 2)
                gpsimd.dma_start(out=sig_o[e * 128:(e + 1) * 128, :],
                                 in_=sgst[:, (e % 2) * TLOC:(e % 2) * TLOC + TLOC]).then_inc(sg_s[e % 2], 16)
            gpsimd.wait_ge(ck, 3 * NCH)
            St(gpsimd.dma_start(out=tot_o, in_=tot[:]))
            for par in range(2):
                gpsimd.wait_ge(cst_s[par], NCH * 16)
                gpsimd.wait_ge(sg_s[par], NCH * 8)
            gpsimd.wait_ge(stm, n_st[0])
            gpsimd.sem_inc(phase, 1)



def emit_l1_b(nc, block, phase, pgate, ios, prefix, semstack):
    x1_h = ios["x1"]
    stats_h = ios["stats"]
    sig_h = ios["sig"]
    wcum_h = ios["wcum"]
    kvcum_h = ios["kvcum"]
    tots_h = ios["tots"]
    selm_h = ios["selm"]
    wswT_h = ios["wswT"]
    woutT_h = ios["woutT"]
    s_o = ios["s_o"]
    with contextlib.ExitStack() as ectx:
        C = Ctx(nc, ectx, prefix, semstack)
        tots = C.sb("tots_sb", [128, NCORE * 64], F32)
        selm = C.sb("selm_sb", [128, 32], F32)
        carry = C.sb("carry", [128, 64], F32)
        stt8 = C.sb("stt8", [128, 16], F32)
        cch = C.sb("cch", [128, 2 * 2 * TLOC], F32)
        sgch = C.sb("sgch", [128, 2 * TLOC], BF16)
        wg_t = C.sb("wg_t", [128, TLOC], F32)
        kvg = C.sb("kvg", [128, TLOC], F32)
        rec = C.sb("rec", [128, TLOC], F32)
        yT = C.sb("yT", [128, NCH * TLOC], BF16)
        zbuf = C.sb("zbuf", [128, 64 * 512], BF16)
        zbv = zbuf[:].rearrange("p (c f) -> p c f", f=512)
        swsl = C.sb("swsl", [128, 4 * NCH * 128], BF16)
        swv = swsl[:].rearrange("p (b c f) -> p b c f", b=4, f=128)
        wosl = C.sb("wosl", [128, 2 * 512], BF16)
        wov = wosl[:].rearrange("p (b f) -> p b f", b=2)
        silt = C.sb("silt", [128, 2 * 512], F32)
        x1t = C.sb("x1t", [128, 4 * D], F32)
        xnsl = C.sb("xnsl", [128, 2 * 512], F32)
        ssl = C.sb("ssl", [128, 2 * 512], F32)

        pj = C.ps("pj", [128, 1024], F32)
        po = C.ps("po", [128, 2048], F32)

        ld0 = C.sem("ld0")

        @block.sync
        def _(sync):
            sync.dma_start(out=tots[:].rearrange("p (r c) -> p r c", c=64),
                           in_=tots_h.rearrange("(r p) c -> p r c", p=128)).then_inc(ld0, 16)
            sync.wait_ge(ld0, 16)
            sync.dma_start(out=selm[:], in_=selm_h).then_inc(ld0, 16)
            sync.wait_ge(ld0, 32)
            sync.dma_start(out=stt8[:], in_=stats_h).then_inc(ld0, 16)

        cr = C.sem("cr")

        @block.vector
        def _(vector):
            vector.wait_ge(ld0, 48)
            vector.memset(carry[:], 0.0)
            vector.drain()
            for b in range(2):
                for qty in range(2):
                    for gblk in range(NBLK):
                        corec = gblk if gblk < 8 else 15 - gblk
                        blki = 0 if gblk < 8 else 1
                        src0 = tots[:, corec * 64 + qty * 32 + blki * 16: corec * 64 + qty * 32 + blki * 16 + 16]
                        vector.scalar_tensor_tensor(
                            carry[:, qty * 32 + b * 16: qty * 32 + b * 16 + 16],
                            src0, selm[:, b * 16 + gblk: b * 16 + gblk + 1],
                            carry[:, qty * 32 + b * 16: qty * 32 + b * 16 + 16],
                            op0=AluOpType.mult, op1=AluOpType.add)
                        vector.drain()
            vector.drain().then_inc(cr, 1)

        ych = [C.sem("ych0"), C.sem("ych1")]
        yc = C.sem("yc")

        @block.sync
        def _(sync):
            for e in range(NCH):
                if e >= 2:
                    sync.wait_ge(yc, e - 1)
                sync.dma_start(out=cch[:, (e % 2) * 2 * TLOC:(e % 2) * 2 * TLOC + TLOC],
                               in_=wcum_h[e * 128:(e + 1) * 128, :]).then_inc(ych[e % 2], 16)
                sync.dma_start(out=cch[:, (e % 2) * 2 * TLOC + TLOC:(e % 2) * 2 * TLOC + 2 * TLOC],
                               in_=kvcum_h[e * 128:(e + 1) * 128, :]).then_inc(ych[e % 2], 16)
                sync.dma_start(out=sgch[:, (e % 2) * TLOC:(e % 2) * TLOC + TLOC],
                               in_=sig_h[e * 128:(e + 1) * 128, :]).then_inc(ych[e % 2], 16)

        @block.vector
        def _(vector):
            vector.wait_ge(cr, 1)
            for e in range(NCH):
                vector.wait_ge(ych[e % 2], (e // 2 + 1) * 48)
                cslot = cch[:, (e % 2) * 2 * TLOC:(e % 2) * 2 * TLOC + 2 * TLOC]
                for b in range(2):
                    sl = slice(b * BLK, (b + 1) * BLK)
                    vector.tensor_scalar(wg_t[:, sl], cslot[:, 0:TLOC][:, sl],
                                         carry[:, 0 * 32 + b * 16 + e:0 * 32 + b * 16 + e + 1], 1e-6,
                                         op0=AluOpType.add, op1=AluOpType.add)
                    vector.tensor_scalar(kvg[:, sl], cslot[:, TLOC:2 * TLOC][:, sl],
                                         carry[:, 1 * 32 + b * 16 + e:1 * 32 + b * 16 + e + 1], None,
                                         op0=AluOpType.add)
                vector.drain()
                vector.reciprocal(rec[:], wg_t[:])
                vector.drain()
                vector.tensor_tensor(kvg[:], kvg[:], rec[:], op=AluOpType.mult)
                vector.drain()
                vector.tensor_tensor(yT[:, e * TLOC:(e + 1) * TLOC], kvg[:],
                                     sgch[:, (e % 2) * TLOC:(e % 2) * TLOC + TLOC], op=AluOpType.mult)
                vector.drain().then_inc(yc, 1)
        YC_ALL = NCH

        swl = [C.sem("swl0"), C.sem("swl1")]
        wol = [C.sem("wol0"), C.sem("wol1")]
        zm = C.sem("zm")
        zs = C.sem("zs")
        zv = C.sem("zv")
        om = C.sem("om")
        oe = C.sem("oe")
        xls = C.sem("xls")
        sso = [C.sem("sso0"), C.sem("sso1")]

        @block.gpsimd
        def _(gpsimd):
            for tp in range(2):
                for zc in range(64):
                    i = tp * 64 + zc
                    if i >= 2:
                        gpsimd.wait_ge(zm, i - 1)
                    gpsimd.dma_start(out=swv[:, (i % 2) * 2],
                                     in_=wswT_h[:, zc * 128:(zc + 1) * 128].rearrange("(c p) f -> p c f", p=128)
                                     ).then_inc(swl[i % 2], 16)
                    gpsimd.dma_start(out=swv[:, (i % 2) * 2 + 1],
                                     in_=wswT_h[:, 8192 + zc * 128:8192 + (zc + 1) * 128].rearrange("(c p) f -> p c f", p=128)
                                     ).then_inc(swl[i % 2], 16)
                for fs in range(4):
                    for zc in range(64):
                        i2 = tp * 256 + fs * 64 + zc
                        if i2 >= 2:
                            gpsimd.wait_ge(om, i2 - 1)
                        gpsimd.dma_start(out=wov[:, i2 % 2],
                                         in_=woutT_h[zc * 128:(zc + 1) * 128, fs * 512:(fs + 1) * 512]
                                         .rearrange("(c p) f -> p c f", p=128)[:, 0, :] if False else
                                         woutT_h[zc * 128:(zc + 1) * 128, fs * 512:(fs + 1) * 512]
                                         ).then_inc(wol[i2 % 2], 16)

        @block.tensor
        def _(tensor):
            tensor.wait_ge(yc, YC_ALL)
            for tp in range(2):
                for zc in range(64):
                    i = tp * 64 + zc
                    tensor.wait_ge(swl[i % 2], (i // 2 + 1) * 32)
                    if i >= 1:
                        tensor.wait_ge(zv, i)
                        tensor.wait_ge(zs, i)
                    if tp == 1 and zc == 0:
                        tensor.wait_ge(oe, 16)
                    for e in range(NCH):
                        tensor.matmul(pj[:, 0:512], swv[:, (i % 2) * 2, e],
                                      yT[:, e * TLOC + tp * 512: e * TLOC + tp * 512 + 512],
                                      start=(e == 0), stop=(e == NCH - 1))
                    for e in range(NCH):
                        r = tensor.matmul(pj[:, 512:1024], swv[:, (i % 2) * 2 + 1, e],
                                          yT[:, e * TLOC + tp * 512: e * TLOC + tp * 512 + 512],
                                          start=(e == 0), stop=(e == NCH - 1))
                    r.then_inc(zm, 1)
                for fs in range(4):
                    for zc in range(64):
                        i2 = tp * 256 + fs * 64 + zc
                        tensor.wait_ge(wol[i2 % 2], (i2 // 2 + 1) * 16)
                        if zc == 0:
                            tensor.wait_ge(zv, tp * 64 + 64)
                            if fs > 0 or tp > 0:
                                tensor.wait_ge(oe, (tp * 4 + fs) * 4)
                        for t4 in range(4):
                            r = tensor.matmul(po[:, t4 * 512:(t4 + 1) * 512],
                                              zbv[:, zc, t4 * 128:(t4 + 1) * 128],
                                              wov[:, i2 % 2],
                                              start=(zc == 0), stop=(zc == 63))
                        r.then_inc(om, 1)

        @block.scalar
        def _(scalar):
            for tp in range(2):
                for zc in range(64):
                    i = tp * 64 + zc
                    scalar.wait_ge(zm, i + 1)
                    if i >= 2:
                        scalar.wait_ge(zv, i - 1)
                    scalar.activation(silt[:, (i % 2) * 512:(i % 2) * 512 + 512], pj[:, 512:1024], AF.Silu).then_inc(zs, 1)

        @block.sync
        def _(sync):
            for tp in range(2):
                for t4 in range(4):
                    tg = tp * 4 + t4
                    if tg >= 4:
                        sync.wait_ge(oe, tg + 9)
                    sync.dma_start(out=x1t[:, t4 * D:(t4 + 1) * D],
                                   in_=x1_h.rearrange("(t p) f -> p t f", p=128)[:, tg, :]).then_inc(xls, 16)
                for fs in range(4):
                    for t4 in range(4):
                        tg = tp * 4 + t4
                        idx = tp * 16 + fs * 4 + t4
                        j = (fs * 4 + t4) % 2
                        sync.wait_ge(oe, idx + 1)
                        sync.dma_start(
                            out=s_o.rearrange("(t p) f -> p t f", p=128)[:, tg, fs * 512:(fs + 1) * 512],
                            in_=ssl[:, j * 512:(j + 1) * 512]).then_inc(sso[idx % 2], 16)
            sync.wait_ge(sso[0], 16 * 16)
            sync.wait_ge(sso[1], 16 * 16)
            sync.sem_inc(phase, 1)

        @block.vector
        def _(vector):
            for tp in range(2):
                for zc in range(64):
                    i = tp * 64 + zc
                    vector.wait_ge(zs, i + 1)
                    if tp == 1 and zc == 0:
                        vector.wait_ge(om, 256)
                    vector.tensor_tensor(zbv[:, zc, :], pj[:, 0:512],
                                         silt[:, (i % 2) * 512:(i % 2) * 512 + 512], op=AluOpType.mult).then_inc(zv, 1)
                for fs in range(4):
                    for t4 in range(4):
                        tg = tp * 4 + t4
                        vector.wait_ge(om, tp * 256 + fs * 64 + 64)
                        vector.wait_ge(xls, (tp + 1) * 64)
                        j = (fs * 4 + t4) % 2
                        idx2 = tp * 16 + fs * 4 + t4
                        if idx2 >= 2:
                            vector.wait_ge(sso[idx2 % 2], (idx2 // 2) * 16)
                        xsl_ = xnsl[:, j * 512:(j + 1) * 512]
                        vector.tensor_scalar(xsl_, x1t[:, t4 * D + fs * 512: t4 * D + fs * 512 + 512],
                                             stt8[:, tg:tg + 1], stt8[:, 8 + tg:9 + tg],
                                             op0=AluOpType.subtract, op1=AluOpType.mult)
                        vector.drain()
                        vector.tensor_tensor(ssl[:, j * 512:(j + 1) * 512],
                                             po[:, t4 * 512:(t4 + 1) * 512], xsl_, op=AluOpType.add).then_inc(oe, 1)




def emit_l1_c(nc, block, phase, pgate, ios, prefix, semstack):
    x1_h = ios["x1"]
    stats_h = ios["stats"]
    s_h = ios["s"]
    WT_h = ios["WT"]
    b_h = ios["b"]
    lAT_h = ios["lAT"]
    lBT_h = ios["lBT"]
    gwT_h = ios["gwT"]
    gb_h = ios["gb"]
    ident_h = ios["ident"]
    x2_o = ios["x2_o"]
    with contextlib.ExitStack() as ectx:
        C = Ctx(nc, ectx, prefix, semstack)
        stt8 = C.sb("stt8", [128, 16], F32)
        xt2 = C.sb("xt2", [128, 2 * D], F32)
        xnbf = C.sb("xnbf", [128, D], BF16)
        xnT = C.sb("xnT", [128, NCH * TLOC], BF16)
        ident = C.sb("ident_sb", [128, 128], BF16)
        ones_bf = C.sb("ones_bf", [128, 1], BF16)
        ones_rbf = C.sb("ones_rbf", [1, 128], BF16)
        lATs = C.sb("lATs", [128, NCH * 128], BF16)
        lATv = lATs[:].rearrange("p (c f) -> p c f", f=128)
        lBTs = C.sb("lBTs", [128, D], BF16)
        gws = C.sb("gws", [128, NCH * 8], BF16)
        gwv = gws[:].rearrange("p (c f) -> p c f", f=8)
        gbr = C.sb("gbr", [1, 8], BF16)
        br = C.sb("br", [1, D], BF16)
        WTsl = C.sb("WTsl", [128, 2 * NCH * 512], BF16)
        WTv = WTsl[:].rearrange("p (b c f) -> p b c f", b=2, f=512)
        lg8 = C.sb("lg8", [128, 8 * 8], F32)
        t2_8 = C.sb("t2_8", [128, 8 * 8], F32)
        gx = C.sb("gx", [128, 6 * 8], F32)
        m1 = gx[:, 0:8]; m2 = gx[:, 8:16]; dd = gx[:, 16:24]
        g1 = gx[:, 24:32]; g2 = gx[:, 32:40]; mneg = gx[:, 40:48]
        wgt8 = C.sb("wgt8", [128, 8 * 8], F32)
        w2t = C.sb("w2t", [128, 8], F32)
        xa = C.sb("xa", [128, 128], F32)
        xawbf = C.sb("xawbf", [128, 128], BF16)
        xawT = C.sb("xawT", [128, NTILE * 128], BF16)
        gsl = C.sb("gsl", [128, 2 * 512], BF16)
        x1sl = C.sb("x1sl", [128, 2 * 512], F32)
        s_sl = C.sb("s_sl", [128, 2 * 512], F32)
        x2sl = C.sb("x2sl", [128, 2 * 512], F32)

        pj = C.ps("pj", [128, 1024], F32)
        plg = C.ps("plg", [128, 512], F32)
        ptb = C.ps("ptb", [128, 2048], BF16)

        ld0 = C.sem("ld0")
        xld = [C.sem("xld0"), C.sem("xld1")]
        ms = C.sem("ms")

        gld = C.sem("gld")
        xn_s = C.sem("xn_s")
        tr_s = C.sem("tr_s")
        xe_s = C.sem("xe_s")
        lgm = C.sem("lgm")
        pp = C.sem("pp")
        ppa = C.sem("ppa")
        xtr = C.sem("xtr")

        @block.sync
        def _(sync):
            sync.dma_start(out=stt8[:], in_=stats_h).then_inc(ld0, 16)
            sync.wait_ge(ld0, 16)
            sync.dma_start(out=ident[:], in_=ident_h).then_inc(ld0, 16)
            for t in range(NTILE):
                if t >= 2:
                    sync.wait_ge(xn_s, 2 * t - 3)
                sync.dma_start(out=xt2[:, (t % 2) * D:(t % 2) * D + D],
                               in_=x1_h.rearrange("(t p) f -> p t f", p=128)[:, t, :]).then_inc(xld[t % 2], 16)
        LD0 = 32

        @block.gpsimd
        def _(gpsimd):
            gpsimd.memset(ones_bf[:], 1.0)
            gpsimd.memset(ones_rbf[:], 1.0)
            gpsimd.drain().then_inc(ms, 1)
            for i, (dst, srcap) in enumerate([
                (lATv, lAT_h.rearrange("(c p) f -> p c f", p=128)),
                (lBTs[:], lBT_h),
                (gwv, gwT_h.rearrange("(c p) f -> p c f", p=128)),
                (gbr[:], gb_h),
                (br[:], b_h),
            ]):
                gpsimd.wait_ge(gld, i * 16)
                gpsimd.dma_start(out=dst, in_=srcap).then_inc(gld, 16)

        @block.vector
        def _(vector):
            vector.wait_ge(ld0, LD0)
            vector.wait_ge(gld, 80)
            for t in range(NTILE):
                vector.wait_ge(xld[t % 2], (t // 2 + 1) * 16)
                vector.tensor_scalar(xnbf[:], xt2[:, (t % 2) * D:(t % 2) * D + D],
                                     stt8[:, t:t + 1], stt8[:, 8 + t:9 + t],
                                     op0=AluOpType.subtract, op1=AluOpType.mult).then_inc(xn_s, 1)
                for g in range(2):
                    G = t * 2 + g
                    vector.wait_ge(tr_s, G + 1)
                    vector.tensor_copy(
                        xnT[:].rearrange("p (c tt) -> p c tt", tt=TLOC)[:, g * 8:(g + 1) * 8, t * 128:(t + 1) * 128],
                        ptb[:, (G % 2) * 1024:(G % 2) * 1024 + 1024].rearrange("p (c tt) -> p c tt", tt=128)).then_inc(xe_s, 1)
                vector.wait_ge(lgm, t + 1)
                vector.tensor_copy(lg8[:, t * 8:(t + 1) * 8], plg[:, t * 16:t * 16 + 8])
                vector.tensor_copy(xa[:], pj[:, 0:128])
                vector.drain()
                vector.reduce_max(m1[:, t:t + 1], lg8[:, t * 8:(t + 1) * 8], axis=AX)
                vector.drain()
                vector.tensor_scalar(t2_8[:, t * 8:(t + 1) * 8], lg8[:, t * 8:(t + 1) * 8],
                                     m1[:, t:t + 1], -1e30, op0=AluOpType.is_equal, op1=AluOpType.mult)
                vector.drain()
                vector.tensor_tensor(t2_8[:, t * 8:(t + 1) * 8], t2_8[:, t * 8:(t + 1) * 8],
                                     lg8[:, t * 8:(t + 1) * 8], op=AluOpType.add)
                vector.drain()
                vector.reduce_max(m2[:, t:t + 1], t2_8[:, t * 8:(t + 1) * 8], axis=AX)
                vector.drain()
                vector.tensor_tensor(dd[:, t:t + 1], m1[:, t:t + 1], m2[:, t:t + 1], op=AluOpType.subtract)
                vector.drain().then_inc(pp, 1)
                vector.wait_ge(ppa, t + 1)
                vector.tensor_scalar(g2[:, t:t + 1], g1[:, t:t + 1], -1.0, 1.0,
                                     op0=AluOpType.mult, op1=AluOpType.add)
                vector.drain()
                vector.tensor_scalar(wgt8[:, t * 8:(t + 1) * 8], lg8[:, t * 8:(t + 1) * 8],
                                     m1[:, t:t + 1], g1[:, t:t + 1], op0=AluOpType.is_equal, op1=AluOpType.mult)
                vector.tensor_scalar(w2t[:], t2_8[:, t * 8:(t + 1) * 8],
                                     m2[:, t:t + 1], g2[:, t:t + 1], op0=AluOpType.is_equal, op1=AluOpType.mult)
                vector.drain()
                vector.tensor_tensor(wgt8[:, t * 8:(t + 1) * 8], wgt8[:, t * 8:(t + 1) * 8],
                                     w2t[:], op=AluOpType.add)
                vector.drain()
                vector.tensor_tensor(xawbf[:].rearrange("p (e r) -> p e r", r=16),
                                     xa[:].rearrange("p (e r) -> p e r", r=16),
                                     wgt8[:, t * 8:(t + 1) * 8].rearrange("p (e o) -> p e o", o=1)
                                     .broadcast_to((128, 8, 16)),
                                     op=AluOpType.mult)
                vector.drain().then_inc(xn_s, 1)
                vector.wait_ge(xtr, t + 1)
                vector.tensor_copy(xawT[:, t * 128:(t + 1) * 128], ptb[:, 0:128])
                vector.drain().then_inc(xe_s, 1)

        @block.scalar
        def _(scalar):
            for t in range(NTILE):
                scalar.wait_ge(pp, t + 1)
                scalar.activation(g1[:, t:t + 1], dd[:, t:t + 1], AF.Sigmoid)
                scalar.drain().then_inc(ppa, 1)

        @block.tensor
        def _(tensor):
            tensor.wait_ge(ld0, LD0)
            tensor.wait_ge(gld, 80)
            tensor.wait_ge(ms, 1)
            for t in range(NTILE):
                tensor.wait_ge(xn_s, 2 * t + 1)
                for g in range(2):
                    G = t * 2 + g
                    if G >= 2:
                        tensor.wait_ge(xe_s, 3 * (G // 2) if G % 2 == 0 else 3 * ((G - 2) // 2) + 2)
                    for j in range(8):
                        e = g * 8 + j
                        r = tensor.transpose(ptb[:, (G % 2) * 1024 + j * 128:(G % 2) * 1024 + (j + 1) * 128],
                                             xnbf[:, e * 128:(e + 1) * 128], ident[:])
                    r.then_inc(tr_s, 1)
                tensor.wait_ge(xe_s, 3 * t + 2)
                for e in range(NCH):
                    tensor.matmul(plg[:, t * 16:t * 16 + 8],
                                  xnT[:, e * TLOC + t * 128:e * TLOC + (t + 1) * 128],
                                  gwv[:, e], start=(e == 0), stop=False)
                tensor.matmul(plg[:, t * 16:t * 16 + 8], ones_rbf[:], gbr[:], start=False, stop=True)
                for e in range(NCH):
                    r = tensor.matmul(pj[:, 0:128],
                                      xnT[:, e * TLOC + t * 128:e * TLOC + (t + 1) * 128],
                                      lATv[:, e], start=(e == 0), stop=(e == NCH - 1))
                r.then_inc(lgm, 1)
                tensor.wait_ge(xn_s, 2 * t + 2)
                tensor.transpose(ptb[:, 0:128], xawbf[:], ident[:]).then_inc(xtr, 1)
        XE_ALL = 3 * NTILE

        wl = [C.sem("wl0"), C.sem("wl1")]
        xsl2 = [C.sem("xsl20"), C.sem("xsl21")]
        fm = C.sem("fm")
        fsl = C.sem("fsl")
        cmb = C.sem("cmb")
        sto = [C.sem("sto0"), C.sem("sto1")]

        @block.gpsimd
        def _(gpsimd):
            gpsimd.wait_ge(ms, 1)
            for fs in range(4):
                if fs >= 2:
                    gpsimd.wait_ge(fm, (fs - 1) * 8)
                gpsimd.dma_start(out=WTv[:, fs % 2],
                                 in_=WT_h[:, fs * 512:(fs + 1) * 512].rearrange("(c p) f -> p c f", p=128)
                                 ).then_inc(wl[fs % 2], 16)

        @block.sync
        def _(sync):
            for idx in range(34):
                if idx < 32:
                    fs, t = divmod(idx, 8)
                    if idx >= 2:
                        sync.wait_ge(cmb, idx - 1)
                    sync.dma_start(out=x1sl[:, (idx % 2) * 512:(idx % 2) * 512 + 512],
                                   in_=x1_h.rearrange("(t p) f -> p t f", p=128)[:, t, fs * 512:(fs + 1) * 512]
                                   ).then_inc(xsl2[idx % 2], 16)
                    sync.dma_start(out=s_sl[:, (idx % 2) * 512:(idx % 2) * 512 + 512],
                                   in_=s_h.rearrange("(t p) f -> p t f", p=128)[:, t, fs * 512:(fs + 1) * 512]
                                   ).then_inc(xsl2[idx % 2], 16)
                if idx >= 2:
                    j2 = idx - 2
                    fs2, t2 = divmod(j2, 8)
                    sync.wait_ge(cmb, j2 + 1)
                    sync.dma_start(
                        out=x2_o.rearrange("(t p) f -> p t f", p=128)[:, t2, fs2 * 512:(fs2 + 1) * 512],
                        in_=x2sl[:, (j2 % 2) * 512:(j2 % 2) * 512 + 512]).then_inc(sto[j2 % 2], 16)
            sync.wait_ge(sto[0], 16 * 16)
            sync.wait_ge(sto[1], 16 * 16)
            sync.sem_inc(phase, 1)

        XLD_BASE = [0]

        @block.tensor
        def _(tensor):
            tensor.wait_ge(xe_s, XE_ALL)
            tensor.wait_ge(xtr, NTILE)
            for fs in range(4):
                tensor.wait_ge(wl[fs % 2], (fs // 2 + 1) * 16)
                for t in range(NTILE):
                    idx = fs * 8 + t
                    if idx >= 2:
                        tensor.wait_ge(fsl, idx - 1)
                    bank = pj[:, (idx % 2) * 512:(idx % 2) * 512 + 512]
                    for e in range(NCH):
                        tensor.matmul(bank, xnT[:, e * TLOC + t * 128:e * TLOC + (t + 1) * 128],
                                      WTv[:, fs % 2, e], start=(e == 0), stop=False)
                    tensor.matmul(bank, ones_rbf[:], br[0:1, fs * 512:(fs + 1) * 512], start=False, stop=False)
                    r = tensor.matmul(bank, xawT[:, t * 128:(t + 1) * 128],
                                      lBTs[:, fs * 512:(fs + 1) * 512], start=False, stop=True)
                    r.then_inc(fm, 1)

        @block.scalar
        def _(scalar):
            for fs in range(4):
                for t in range(NTILE):
                    idx = fs * 8 + t
                    scalar.wait_ge(fm, idx + 1)
                    if idx >= 2:
                        scalar.wait_ge(cmb, idx - 1)
                    bank = pj[:, (idx % 2) * 512:(idx % 2) * 512 + 512]
                    scalar.activation(gsl[:, (idx % 2) * 512:(idx % 2) * 512 + 512], bank, AF.Silu).then_inc(fsl, 1)

        @block.vector
        def _(vector):
            for fs in range(4):
                for t in range(NTILE):
                    idx = fs * 8 + t
                    vector.wait_ge(fsl, idx + 1)
                    vector.wait_ge(xsl2[idx % 2], (idx // 2 + 1) * 32)
                    if idx >= 2:
                        vector.wait_ge(sto[idx % 2], (idx // 2) * 16)
                    j = idx % 2
                    vector.tensor_tensor(x2sl[:, j * 512:(j + 1) * 512],
                                         s_sl[:, j * 512:(j + 1) * 512],
                                         gsl[:, j * 512:(j + 1) * 512], op=AluOpType.mult)
                    vector.drain()
                    vector.tensor_tensor(x2sl[:, j * 512:(j + 1) * 512],
                                         x2sl[:, j * 512:(j + 1) * 512],
                                         x1sl[:, j * 512:(j + 1) * 512], op=AluOpType.add).then_inc(cmb, 1)




def _prep_core_inputs_l1(inputs, i):
    rows = _global_rows(i)
    x = np.ascontiguousarray(np.asarray(inputs["L0_attn_x"], dtype=np.float32).reshape(T, D)[rows])
    cosq, sinq = _rope_tables(rows, RS)
    cosk, sink = _rope_tables(rows, 1.0)
    return {
        "x": x,
        "wqT": np.ascontiguousarray(np.asarray(inputs["L0_attn_wq"], np.float32).T),
        "wkT": np.ascontiguousarray(np.asarray(inputs["L0_attn_wk"], np.float32).T),
        "wvT": np.ascontiguousarray(np.asarray(inputs["L0_attn_wv"], np.float32).T),
        "wqkvT": np.ascontiguousarray(np.asarray(inputs["L0_ffn_W_qkv"], np.float32).T),
        "wgT": np.ascontiguousarray(np.asarray(inputs["L0_ffn_W_gate"], np.float32).T),
        "cosq": cosq, "sinq": sinq, "cosk": cosk, "sink": sink,
        "ident": np.eye(128, dtype=BF),
    }


def run_l0_pre(inputs):
    nc1 = build_l0_pre()
    in_maps = [_prep_core_inputs_l1(inputs, i) for i in range(NCORE)]
    return run_bass_kernel_spmd(nc1, in_maps, core_ids=list(range(NCORE)))


def _prep_l2(inputs, r1):
    f32 = np.float32
    kTf = np.empty((D, T), BF)
    vf = np.empty((T, NH * 129), BF)
    kshTf = np.empty((128, T), BF)
    vshf = np.empty((T, 132), BF)
    k2ma = np.empty((NH, NCORE), f32)
    k2shma = np.empty((1, NCORE), f32)
    for c in range(NCORE):
        rows = _global_rows(c)
        kTf[:, rows] = np.asarray(r1[c]["kT_o"])
        vf[rows, :] = np.asarray(r1[c]["v_o"])
        kshTf[:, rows] = np.asarray(r1[c]["kshT_o"])
        vshf[rows, :] = np.asarray(r1[c]["vsh_o"])
        k2ma[:, c] = np.asarray(r1[c]["k2m_o"])[:, 0]
        k2shma[0, c] = np.asarray(r1[c]["k2shm_o"])[0, 0]
    woT = np.ascontiguousarray(np.asarray(inputs["L0_attn_wo"], f32).T)
    maskT = np.triu(np.ones((128, 128), f32)).astype(BF)
    ident = np.eye(128, dtype=BF)
    x = np.asarray(inputs["L0_attn_x"], f32).reshape(T, D)
    maps = []
    for c in range(NCORE):
        rows = _global_rows(c)
        bg = _core_blocks(c)
        pm = np.zeros((2, NH, 2, KSLOT), f32)
        pm[:, :, :, 0] = 1.0
        for g in range(2):
            for kb in range(16):
                if kb < bg[g]:
                    pm[:, :, g, 1 + kb] = 1.0
        maps.append({
            "x": np.ascontiguousarray(x[rows]),
            "qT": np.asarray(r1[c]["qT_o"]),
            "q2T": np.asarray(r1[c]["q2T_o"]),
            "kTl": np.asarray(r1[c]["kT_o"]),
            "vl": np.asarray(r1[c]["v_o"]),
            "kshTl": np.asarray(r1[c]["kshT_o"]),
            "vshl": np.asarray(r1[c]["vsh_o"]),
            "kTf": kTf, "vf": vf, "kshTf": kshTf, "vshf": vshf,
            "qn2": np.asarray(r1[c]["qn2_o"]),
            "q2n2": np.asarray(r1[c]["q2n2_o"]),
            "k2ma": k2ma, "k2shma": k2shma,
            "pm": np.ascontiguousarray(pm.reshape(1, -1)),
            "woT": woT, "maskT": maskT, "ident": ident,
        })
    return maps


def _prep_l4(inputs, r3):
    f32 = np.float32
    tots = np.concatenate([np.asarray(r3[c]["tot_o"]) for c in range(NCORE)], axis=1)
    tots = np.ascontiguousarray(tots)
    wswT = np.ascontiguousarray(np.asarray(inputs["L1_attn_w_swiglu"], f32).T)
    woutT = np.ascontiguousarray(np.asarray(inputs["L1_attn_w_out"], f32).T)
    maps = []
    for c in range(NCORE):
        bg = _core_blocks(c)
        selm = np.zeros((2, NBLK), f32)
        for b in range(2):
            selm[b, :bg[b]] = 1.0
        selm = np.broadcast_to(selm.reshape(1, 32), (128, 32)).copy()
        maps.append(dict(tots=tots, selm=selm, wswT=wswT, woutT=woutT))
    return maps


def _prep_l5(inputs):
    f32 = np.float32
    lA = np.asarray(inputs["L1_ffn_lora_A"], f32).reshape(128, D)
    lB = np.asarray(inputs["L1_ffn_lora_B"], f32)
    return {
        "WT": np.ascontiguousarray(np.asarray(inputs["L1_ffn_W"], f32).T),
        "b": np.asarray(inputs["L1_ffn_b"], f32).reshape(1, D),
        "lAT": np.ascontiguousarray(lA.T),
        "lBT": np.ascontiguousarray(np.transpose(lB, (0, 2, 1)).reshape(128, D)),
        "gwT": np.ascontiguousarray(np.asarray(inputs["L1_ffn_gate_w"], f32).T),
        "gb": np.asarray(inputs["L1_ffn_gate_b"], f32).reshape(1, 8),
        "ident": np.eye(128, dtype=BF),
    }


_old_kernel_removed = None



SIM_LOCAL_SHARED = False


def build_fused():
    nc = bass.Bass("TRN2", target_bir_lowering=False, debug=False)
    E = lambda n, s, d: nc.dram_tensor(n, s, d, kind="ExternalInput").ap()
    I = lambda n, s, d: nc.dram_tensor(n, s, d, kind="Internal").ap()
    aspace = "Local" if SIM_LOCAL_SHARED else "Shared"
    S = lambda n, s, d: nc.dram_tensor(n, s, d, kind="Internal", addr_space=aspace).ap()

    x = E("x", [TLOC, D], F32)
    wqT = E("wqT", [D, D], F32); wkT = E("wkT", [D, D], F32); wvT = E("wvT", [D, D], F32)
    wqkvT0 = E("wqkvT0", [D, 3 * D], F32); wgT = E("wgT", [D, 16], F32)
    cosq = E("cosq", [128, TLOC], F32); sinq = E("sinq", [128, TLOC], F32)
    cosk = E("cosk", [128, TLOC], F32); sink = E("sink", [128, TLOC], F32)
    ident = E("ident", [128, 128], BF16)
    pm = E("pm", [1, 2 * NH * 2 * KSLOT], F32)
    woT = E("woT", [D, D], F32)
    maskT = E("maskT", [128, 128], BF16)
    wqkvT1 = E("wqkvT1", [D, 3 * D], F32)
    selm = E("selm", [128, 32], F32)
    wswT = E("wswT", [D, 16384], F32)
    woutT = E("woutT", [8192, D], F32)
    WT = E("WT", [D, D], F32); bb = E("b", [1, D], F32)
    lAT = E("lAT", [D, 128], F32); lBT = E("lBT", [128, D], F32)
    gwT = E("gwT", [D, 8], F32); gb = E("gb", [1, 8], F32)
    x2_o = nc.dram_tensor("x2_o", [TLOC, D], F32, kind="ExternalOutput").ap()

    qT_d = I("qT_d", [D, TLOC], BF16); kT_d = I("kT_d", [D, TLOC], BF16)
    v_d = I("v_d", [TLOC, NH * 129], BF16); q2T_d = I("q2T_d", [D, TLOC], BF16)
    kshT_d = I("kshT_d", [128, TLOC], BF16); vsh_d = I("vsh_d", [TLOC, 132], BF16)
    qn2_d = I("qn2_d", [NH, TLOC], F32); q2n2_d = I("q2n2_d", [NH, TLOC], F32)
    kn_d = I("kn_d", [17, 1], F32)
    knorm_d = I("knorm_d", [NH, TLOC], F32)
    nb_d = I("nb_d", [NH, 4], F32)
    x1_d = I("x1_d", [TLOC, D], F32)
    stats_d = I("stats_d", [128, 16], F32)
    sig_d = I("sig_d", [D, TLOC], BF16)
    wcum_d = I("wcum_d", [D, TLOC], F32); kvcum_d = I("kvcum_d", [D, TLOC], F32)
    tot_d = I("tot_d", [128, 64], F32)
    s_d = I("s_d", [TLOC, D], F32)
    wqkvT1b = I("wqkvT1b", [D, 3 * D], BF16)
    wswb = I("wswb", [D, 16384], BF16)
    woutb = I("woutb", [8192, D], BF16)
    WTb = I("WTb", [D, D], BF16)

    kTg = S("kTg", [NCORE * D, TLOC], BF16)
    vg = S("vg", [NCORE * TLOC, NH * 129], BF16)
    kshg = S("kshg", [NCORE * 128, TLOC], BF16)
    vshg = S("vshg", [NCORE * TLOC, 132], BF16)
    kng = S("kng", [NCORE * 17, 1], F32)
    totg = S("totg", [NCORE * 128, 64], F32)

    with contextlib.ExitStack() as outer:
        phase = outer.enter_context(nc.semaphore("phase"))
        ccA = outer.enter_context(nc.semaphore("ccA"))
        ccB = outer.enter_context(nc.semaphore("ccB"))
        bar = outer.enter_context(nc.semaphore("bar"))
        pf2 = outer.enter_context(nc.semaphore("pf2"))
        nbar = [0]
        block = outer.enter_context(nc.Block())
        pool = SemPool(outer)

        def barrier_in(pk):
            nbar[0] += 4

            @block.sync
            def _(sync):
                sync.wait_ge(phase, pk)
                sync.sem_inc(bar, 1)

            @block.tensor
            def _(tensor):
                tensor.wait_ge(phase, pk)
                tensor.sem_inc(bar, 1)

            @block.vector
            def _(vector):
                vector.wait_ge(phase, pk)
                vector.sem_inc(bar, 1)

            @block.scalar
            def _(scalar):
                scalar.wait_ge(phase, pk)
                scalar.sem_inc(bar, 1)

        def gate_all(val):
            @block.sync
            def _(sync):
                sync.wait_ge(phase, val)

            @block.gpsimd
            def _(gpsimd):
                gpsimd.wait_ge(phase, val)

            @block.tensor
            def _(tensor):
                tensor.wait_ge(phase, val)

            @block.vector
            def _(vector):
                vector.wait_ge(phase, val)

            @block.scalar
            def _(scalar):
                scalar.wait_ge(phase, val)

        ios1 = dict(x=x, wqT=wqT, wkT=wkT, wvT=wvT, wqkvT=wqkvT0, wgT=wgT,
                    cosq=cosq, sinq=sinq, cosk=cosk, sink=sink, ident=ident,
                    qT_o=qT_d, kT_o=kT_d, v_o=v_d, q2T_o=q2T_d, kshT_o=kshT_d,
                    vsh_o=vsh_d, qn2_o=qn2_d, q2n2_o=q2n2_d,
                    k2m_o=kn_d[0:16, :], k2shm_o=kn_d[16:17, :], knorm_d=knorm_d)
        pool.cur = []
        emit_l0_pre(nc, block, phase, 0, ios1, "p1_", pool)
        sems1 = pool.cur

        barrier_in(1)

        @block.gpsimd
        def _(gpsimd):
            gpsimd.wait_ge(phase, 1)
            gpsimd.wait_ge(bar, nbar[0])
            nc.clear_and_free_semaphores(sems1)
            for i, (s_ap, d_ap) in enumerate([(kT_d, kTg), (v_d, vg), (kshT_d, kshg),
                                              (vsh_d, vshg), (kn_d, kng)]):
                gpsimd.collective_compute("AllGather", AluOpType.bypass,
                                          replica_groups=[list(range(NCORE))],
                                          ins=[s_ap.opt()], outs=[d_ap.opt()]).then_inc(ccA, 1)
            gpsimd.wait_ge(ccA, 5)
            gpsimd.sem_inc(phase, 1)

        for _eng in ("gpsimd", "tensor", "vector", "scalar"):
            getattr(block, _eng)(lambda e: e.wait_ge(phase, 2))
        ios2 = dict(x=x, qT=qT_d, q2T=q2T_d, kTl=kT_d, vl=v_d, kshTl=kshT_d,
                    vshl=vsh_d, kTg=kTg, vg=vg, kshg=kshg, vshg=vshg,
                    qn2=qn2_d, q2n2=q2n2_d, kng=kng, pm=pm, woT=woT,
                    maskT=maskT, ident=ident, x1_o=x1_d, nb_d=nb_d, pfsem=pf2,
                    conv=[(wqkvT1, wqkvT1b), (wswT, wswb), (woutT, woutb), (WT, WTb)])
        pool.cur = []
        emit_l0_attn(nc, block, phase, 2, ios2, "p2_", pool)
        sems2 = pool.cur

        barrier_in(3)

        @block.gpsimd
        def _(gpsimd):
            gpsimd.wait_ge(phase, 3)
            gpsimd.wait_ge(bar, nbar[0])
            nc.clear_and_free_semaphores(sems2)
            gpsimd.sem_inc(phase, 1)

        gate_all(4)
        ios3 = dict(x1=x1_d, wqkvT=wqkvT1b, ident=ident, stats_o=stats_d,
                    sig_o=sig_d, wcum_o=wcum_d, kvcum_o=kvcum_d, tot_o=tot_d)
        pool.cur = []
        emit_l1_a(nc, block, phase, 4, ios3, "p3_", pool)
        sems3 = pool.cur

        barrier_in(5)

        @block.gpsimd
        def _(gpsimd):
            gpsimd.wait_ge(phase, 5)
            gpsimd.wait_ge(bar, nbar[0])
            nc.clear_and_free_semaphores(sems3)
            gpsimd.collective_compute("AllGather", AluOpType.bypass,
                                      replica_groups=[list(range(NCORE))],
                                      ins=[tot_d.opt()], outs=[totg.opt()]).then_inc(ccB, 1)
            gpsimd.wait_ge(ccB, 1)
            gpsimd.sem_inc(phase, 1)

        gate_all(6)
        ios4 = dict(x1=x1_d, stats=stats_d, sig=sig_d, wcum=wcum_d, kvcum=kvcum_d,
                    tots=totg, selm=selm, wswT=wswb, woutT=woutb, s_o=s_d)
        pool.cur = []
        emit_l1_b(nc, block, phase, 6, ios4, "p4_", pool)
        sems4 = pool.cur

        barrier_in(7)

        @block.gpsimd
        def _(gpsimd):
            gpsimd.wait_ge(phase, 7)
            gpsimd.wait_ge(bar, nbar[0])
            nc.clear_and_free_semaphores(sems4)
            gpsimd.sem_inc(phase, 1)

        gate_all(8)
        ios5 = dict(x1=x1_d, stats=stats_d, s=s_d, WT=WTb, b=bb, lAT=lAT, lBT=lBT,
                    gwT=gwT, gb=gb, ident=ident, x2_o=x2_o)
        pool.cur = []
        emit_l1_c(nc, block, phase, 8, ios5, "p5_", pool)

    return nc


def _prep_fused_maps(inputs):
    base = {
        "wqT": np.ascontiguousarray(np.asarray(inputs["L0_attn_wq"], np.float32).T),
        "wkT": np.ascontiguousarray(np.asarray(inputs["L0_attn_wk"], np.float32).T),
        "wvT": np.ascontiguousarray(np.asarray(inputs["L0_attn_wv"], np.float32).T),
        "wqkvT0": np.ascontiguousarray(np.asarray(inputs["L0_ffn_W_qkv"], np.float32).T),
        "wgT": np.ascontiguousarray(np.asarray(inputs["L0_ffn_W_gate"], np.float32).T),
        "woT": np.ascontiguousarray(np.asarray(inputs["L0_attn_wo"], np.float32).T),
        "wqkvT1": np.ascontiguousarray(np.asarray(inputs["L1_attn_w_qkv"], np.float32).T),
        "wswT": np.ascontiguousarray(np.asarray(inputs["L1_attn_w_swiglu"], np.float32).T),
        "woutT": np.ascontiguousarray(np.asarray(inputs["L1_attn_w_out"], np.float32).T),
        "maskT": np.triu(np.ones((128, 128), np.float32)).astype(BF),
        "ident": np.eye(128, dtype=BF),
    }
    base.update(_prep_l5(inputs))
    base.pop("stats", None)
    x = np.asarray(inputs["L0_attn_x"], np.float32).reshape(T, D)
    maps = []
    for c in range(NCORE):
        rows = _global_rows(c)
        bg = _core_blocks(c)
        cosq, sinq = _rope_tables(rows, RS)
        cosk, sink = _rope_tables(rows, 1.0)
        pmv = np.zeros((2, NH, 2, KSLOT), np.float32)
        pmv[:, :, :, 0] = 1.0
        for g in range(2):
            for kb in range(16):
                if kb < bg[g]:
                    pmv[:, :, g, 1 + kb] = 1.0
        selmv = np.zeros((2, NBLK), np.float32)
        for b2 in range(2):
            selmv[b2, :bg[b2]] = 1.0
        selmv = np.broadcast_to(selmv.reshape(1, 32), (128, 32)).copy()
        m = dict(base)
        m.update(x=np.ascontiguousarray(x[rows]), cosq=cosq, sinq=sinq, cosk=cosk,
                 sink=sink, pm=np.ascontiguousarray(pmv.reshape(1, -1)), selm=selmv)
        maps.append(m)
    return maps


def kernel(**inputs):
    nc = build_fused()
    maps = _prep_fused_maps(inputs)
    res = run_bass_kernel_spmd(nc, maps, core_ids=list(range(NCORE)))
    out = np.empty((T, D), np.float32)
    for c in range(NCORE):
        out[_global_rows(c)] = np.asarray(res.results[c]["x2_o"])
    return out.reshape(1, T, D)
